# version 27
# baseline (speedup 1.0000x reference)
import math
from contextlib import ExitStack

import numpy as np
import concourse.bass as bass
import concourse.mybir as mybir
from concourse.alu_op_type import AluOpType
from concourse.bass_utils import run_bass_kernel_spmd

F32 = mybir.dt.float32
BF16 = mybir.dt.bfloat16
FP16 = mybir.dt.float16
FP8 = mybir.dt.float8e4
AF = mybir.ActivationFunctionType
AX = mybir.AxisListType

ENGINES = ("tensor", "vector", "scalar", "gpsimd", "sync")
SEMS = (("pe", "dv", "ac", "gp", "out")
        + tuple(f"qs{i}" for i in range(8)) + tuple(f"qg{i}" for i in range(8)))


def wall(tuples):
    d = {}
    for s_, v in tuples:
        if v:
            d[s_] = max(d.get(s_, 0), v)
    return list(d.items())


class Cfg:
    def __init__(self, T=4096, D=1024, H=16, F=4096):
        self.T, self.D, self.H, self.F = T, D, H, F
        self.DH = 64
        assert H * self.DH == D
        self.DC = D // 128
        self.FC = F // 128
        self.KT = T // 128
        self.G = 2
        self.HG = H // 2
        self.CH = self.HG * 64 // 128
        self.TBLK = 512
        self.NTB = T // self.TBLK
        self.TB3 = 256
        self.NT3 = T // self.TB3
        self.VW = self.HG * 65
        self.scale = 1.0 / math.sqrt(D)
        self.kgroups = []
        k = 0
        while k < self.KT:
            n = min(3, self.KT - k)
            self.kgroups.append((k, n))
            k += n


def apx(view, offset, dims):
    p = view.ap[0]
    return bass.AP(view.tensor, offset, [[p[0], p[1]]] + [list(d) for d in dims])


class Prog:
    def __init__(self):
        self.ops = {e: [] for e in ENGINES}
        self.cnt = {s: 0 for s in SEMS}

    def do(self, eng, fn, waits=(), inc=None, n=1):
        self.ops[eng].append((fn, tuple(waits), inc, n))
        if inc is not None:
            self.cnt[inc] += n
            return self.cnt[inc]
        return None

    def emit(self, block, sems, first_waits=()):
        for ename in ENGINES:
            def body(eng, ename=ename):
                waited = {}
                if ename != "gpsimd":
                    for s, v in first_waits:
                        if v:
                            eng.wait_ge(sems[s], v)
                            waited[s] = v
                for fn, waits, inc, n in self.ops[ename]:
                    for s, v in waits:
                        if v and waited.get(s, 0) < v:
                            eng.wait_ge(sems[s], v)
                            waited[s] = v
                    instr = fn(eng)
                    if inc is not None:
                        instr.then_inc(sems[inc], n)
            getattr(block, ename)(body)


def build_nc(cfg: Cfg):
    c = cfg
    nc = bass.Bass("TRN2", target_bir_lowering=False)

    X = nc.dram_tensor("x", [c.T, c.D], F32, kind="ExternalInput").ap()
    Wqkv = nc.dram_tensor("w_qkv", [c.D, 3 * c.D], F32, kind="ExternalInput").ap()
    Wout = nc.dram_tensor("w_out", [c.D, c.D], F32, kind="ExternalInput").ap()
    W1 = nc.dram_tensor("w1", [c.F, c.D], F32, kind="ExternalInput").ap()
    B1 = nc.dram_tensor("b1", [c.F], F32, kind="ExternalInput").ap()
    W2 = nc.dram_tensor("w2", [c.D, c.F], F32, kind="ExternalInput").ap()
    B2 = nc.dram_tensor("b2", [c.D], F32, kind="ExternalInput").ap()
    OUT = nc.dram_tensor("out", [c.T, c.D], F32, kind="ExternalOutput").ap()

    P = Prog()
    with ExitStack() as st:
        def sb(name, shape, dtype):
            h = st.enter_context(nc.sbuf_tensor(name, shape, dtype))
            return h[tuple(slice(None) for _ in shape)]

        id32 = sb("id32", [128, 128], F32)
        ones16 = sb("ones16", [1, 128], FP16)
        ones_bf = sb("ones_bf", [128, 1], BF16)
        eps_t = sb("eps_t", [128, 1], F32)
        mhalf = sb("mhalf", [128, 1], F32)
        alpha1 = sb("alpha1", [128, c.FC], F32)
        alpha2 = sb("alpha2", [128, c.DC], F32)
        b1s = sb("b1s", [128, c.FC], F32)
        b2s = sb("b2s", [128, c.DC], F32)
        y_acc = sb("y_acc", [128, (c.T // 128) * c.D], FP16)
        kmaxs = sb("kmaxs", [1, 2], F32)

        stats = sb("stats", [128, 80], F32)
        qmax4 = sb("qmax4", [128, 8], F32)
        sc1 = sb("sc1", [1, 8], F32)
        negBv = sb("negBv", [128, 1], F32)

        st2 = ExitStack()

        def sb2(name, shape, dtype):
            h = st2.enter_context(nc.sbuf_tensor(name, shape, dtype))
            return h[tuple(slice(None) for _ in shape)]

        kT = sb2("kT", [128, c.CH * c.T], BF16)
        vP = sb2("vP", [128, c.KT * c.VW], BF16)
        wsh = sb2("wsh", [128, c.DC * c.HG * 64 + c.HG * c.D], BF16)
        wkv = wsh[:, 0:c.DC * c.HG * 128]
        wq = wsh[:, 0:c.DC * c.HG * 64]
        wo = wsh[:, c.DC * c.HG * 64:]
        xs = sb2("xs", [128, 4 * c.D], F32)
        lnxT = sb2("lnxT", [128, c.DC * c.TBLK], BF16)
        qT = sb2("qT", [128, 2 * c.CH * c.TBLK], BF16)
        oT = sb2("oT", [128, c.HG * c.TBLK], BF16)
        pT = sb2("pT", [128, 4 * 1024], BF16)
        escr = sb2("escr", [128, 1024], F32)
        otmp = escr[0:64, :]
        recl = escr[64:65, :]
        vtmp = escr[:, 0:512]
        recl16 = sb2("recl16", [1, 520], FP16)

        psh = st.enter_context(nc.psum_tensor("ps", [128, 4096], F32))
        ps = psh[:, :]

        def bank(i, lo=0, width=512):
            return ps[:, 512 * i + lo: 512 * i + lo + width]

        semh = {s: st.enter_context(nc.semaphore(s)) for s in SEMS}
        block = st.enter_context(nc.Block())

        SC = c.scale
        BW = {i: [] for i in range(8)}

        rr = {"sync": 0, "gpsimd": 0}

        def dma(q, out, in_, waits=()):
            pre = "qs" if q == "sync" else "qg"
            sem = f"{pre}{rr[q] % 8}"
            rr[q] += 1
            chain = [(sem, P.cnt[sem])]
            v = P.do(q, lambda e, o=out, i=in_: e.dma_start(out=o, in_=i),
                     waits=list(waits) + chain, inc=sem, n=16)
            return (sem, v)

        m0 = P.do("gpsimd", lambda g: g.memset(id32, 0.0), inc="gp")

        P.do("gpsimd", lambda g: g.affine_select(
            out=id32, in_=id32, compare_op=AluOpType.not_equal, fill=1.0,
            base=0, pattern=[[-1, 128]], channel_multiplier=1),
            waits=[("gp", m0)], inc="gp")
        P.do("gpsimd", lambda g: g.memset(ones16, 1.0), inc="gp")
        P.do("gpsimd", lambda g: g.memset(ones_bf, 1.0), inc="gp")
        P.do("gpsimd", lambda g: g.memset(eps_t, 1e-5), inc="gp")
        P.do("gpsimd", lambda g: g.memset(mhalf, -0.5), inc="gp")
        P.do("gpsimd", lambda g: g.memset(
            apx(vP, 64, [[c.VW, c.KT], [65, c.HG]]), 1.0), inc="gp")
        P.do("gpsimd", lambda g: g.memset(wo, 0.0), inc="gp")
        m_init = P.do("gpsimd", lambda g: g.memset(oT, 0.0), inc="gp")

        m_bload = []
        for fc in range(c.FC):
            m_bload.append(dma("sync", b1s[:, fc:fc + 1],
                               bass.AP(B1.tensor, fc * 128, [[1, 128], [1, 1]])))
        for dc in range(c.DC):
            m_bload.append(dma("sync", b2s[:, dc:dc + 1],
                               bass.AP(B2.tensor, dc * 128, [[1, 128], [1, 1]])))
        m_bload = wall(m_bload)

        slot_pow = {}
        slot_norm = {}

        def ln_stats(src, slot, wait_src):
            o = slot * 20
            ts_ = min(512, c.D)
            nh = c.D // ts_
            for i in range(nh):
                P.do("vector",
                     lambda v, i=i, o=o, s=src, ts_=ts_: v.bn_stats(
                         stats[:, o + 6 * i:o + 6 * i + 6],
                         s[:, ts_ * i:ts_ * (i + 1)]),
                     waits=list(wait_src) + [("gp", slot_pow.get(slot, 0)),
                                             ("dv", slot_norm.get(slot, 0))]
                     if i == 0 else (),
                     inc="dv" if i == nh - 1 else None)
            m1 = P.cnt["dv"]
            m2 = P.do("vector", lambda v, o=o, n=nh: v.bn_aggr(
                stats[:, o + 12:o + 14], stats[:, o:o + 6 * n]),
                waits=[("dv", m1)], inc="dv")
            m3 = P.do("vector", lambda v, o=o: v.tensor_scalar(
                stats[:, o + 14:o + 15], stats[:, o + 13:o + 14], eps_t[:, 0:1],
                None, AluOpType.add, AluOpType.bypass),
                waits=[("dv", m2), ("gp", slot_pow.get(slot, 0))], inc="dv")
            m4 = P.do("gpsimd", lambda g, o=o: g.tensor_tensor(
                stats[:, o + 15:o + 16], stats[:, o + 14:o + 15], mhalf,
                AluOpType.pow),
                waits=[("dv", m3), ("gp", m_init),
                       ("dv", slot_norm.get(slot, 0))], inc="gp")
            slot_pow[slot] = m4
            return m4

        def ln_norm(src, dst, slot, m4, extra_waits=()):
            o = slot * 20
            m5 = P.do("vector", lambda v, o=o: v.scalar_tensor_tensor(
                stats[:, o + 16:o + 17], stats[:, o + 12:o + 13], -1.0,
                stats[:, o + 15:o + 16], AluOpType.mult, AluOpType.mult),
                waits=[("gp", m4)], inc="dv")
            m6 = P.do("vector", lambda v, o=o, s=src, d=dst: v.tensor_scalar(
                d, s, stats[:, o + 15:o + 16], stats[:, o + 16:o + 17],
                AluOpType.mult, AluOpType.add),
                waits=[("dv", m5)] + list(extra_waits), inc="dv")
            slot_norm[slot] = m6
            return m6

        def transp(src, n, b0, dst_of, wait_first, evac_eng="vector", nb=2):
            sem = "dv" if evac_eng == "vector" else "ac"
            step = 4
            last = None
            g0 = 0
            while g0 < n:
                g1 = min(g0 + step, n)
                bi = b0 + ((g0 // 4) % nb if step == 4 else 0)
                bw = BW[bi] if step == 4 else BW[b0] + BW[b0 + 1]
                for j in range(g0, g1):
                    w = []
                    if j == 0:
                        w += list(wait_first)
                    if j == g0:
                        w += bw
                    P.do("tensor", lambda t, j=j, bi=bi, g0=g0, s=src: t.transpose(
                        bank(bi + (j - g0) // 4, 128 * ((j - g0) % 4), 128),
                        s[:, 128 * j:128 * (j + 1)], id32),
                        waits=w, inc="pe" if j == g1 - 1 else None)
                mpe = P.cnt["pe"]
                cn = g1 - g0
                if evac_eng == "vector":
                    last = P.do("vector", lambda v, bi=bi, g0=g0, cn=cn: v.tensor_copy(
                        dst_of(g0, cn), bank(bi, 0, 128 * cn)),
                        waits=[("pe", mpe)], inc="dv")
                else:
                    last = P.do("scalar", lambda e, bi=bi, g0=g0, cn=cn: e.activation(
                        dst_of(g0, cn), bank(bi, 0, 128 * cn), AF.Copy),
                        waits=[("pe", mpe)], inc="ac")
                if step == 4:
                    BW[bi] = [(sem, last)]
                else:
                    BW[b0] = [(sem, last)]
                    BW[b0 + 1] = [(sem, last)]
                g0 = g1
            return (sem, last), P.cnt["pe"]

        st_blk = {"xw": [[] for _ in range(4)],
                  "lnxT_users": [],
                  "m_lnxT_sem": "dv", "evac_eng": "vector"}

        def ln1_part1(t0_base):
            nsub = c.TBLK // 128
            pows, loads = [], []
            for s in range(nsub):
                mx = dma("sync", xs[:, s * c.D:(s + 1) * c.D],
                         X[t0_base + s * 128: t0_base + s * 128 + 128, :],
                         waits=st_blk["xw"][s])
                loads.append(mx)
            for s in range(nsub):
                pows.append(ln_stats(xs[:, s * c.D:(s + 1) * c.D], s, [loads[s]]))
            norms = []
            for s in range(nsub):
                xsl = xs[:, s * c.D:(s + 1) * c.D]
                norms.append(ln_norm(xsl, xsl, s, pows[s]))
            return norms

        def ln1_part2(norms, b0=0):
            nsub = c.TBLK // 128
            mlast = None
            for s in range(nsub):
                xsl = xs[:, s * c.D:(s + 1) * c.D]
                def dst(j0, cn, s=s):
                    return apx(lnxT, j0 * c.TBLK + s * 128, [[c.TBLK, cn], [1, 128]])
                (semn, mev), mtp = transp(
                    xsl, c.DC, b0, dst,
                    wait_first=[("dv", norms[s])]
                    + [("pe", u) for u in st_blk["lnxT_users"][-1:]],
                    evac_eng=st_blk["evac_eng"])
                st_blk["xw"][s] = [("pe", mtp)]
                st_blk["m_lnxT_sem"] = semn
                mlast = mev
            return mlast

        war_wkv = {"v": []}
        war_kT = {"v": []}
        war_vP = {"v": []}
        m_kv_done = {}

        def phase_A(g):
            W2G = c.HG * 128
            m_w = []
            for dc in range(c.DC):
                m_w.append(dma("gpsimd", wkv[:, dc * W2G: dc * W2G + c.HG * 64],
                    Wqkv[dc * 128:(dc + 1) * 128,
                         c.D + g * c.HG * 64: c.D + (g + 1) * c.HG * 64],
                    waits=(war_wkv["v"] + [("gp", m_init)]) if dc == 0 else ()))
                m_w.append(dma("gpsimd", wkv[:, dc * W2G + c.HG * 64: (dc + 1) * W2G],
                    Wqkv[dc * 128:(dc + 1) * 128,
                         2 * c.D + g * c.HG * 64: 2 * c.D + (g + 1) * c.HG * 64]))
            m_w = wall(m_w)
            mk0 = P.do("vector", lambda v: v.memset(qmax4[:, 7:8], 0.0),
                       waits=war_kT["v"], inc="dv")
            kmax_run = mk0

            nmk = c.HG * 128 // 128
            nK = nmk // 2
            BW_vtmp[:] = [("dv", P.cnt["dv"]), ("pe", P.cnt["pe"])]
            st_blk["evac_eng"] = "scalar"
            norms = ln1_part1(0)
            for tb in range(c.NTB):
                m_lnxT = ln1_part2(norms)
                if tb + 1 < c.NTB:
                    norms = ln1_part1((tb + 1) * c.TBLK)
                for mc in range(nmk):
                    pb = 2 + (mc % 2)
                    is_v = mc >= nK
                    for dc in range(c.DC):
                        w = (BW[pb] + m_w
                             + [(st_blk["m_lnxT_sem"], m_lnxT)]) if dc == 0 else ()
                        P.do("tensor", lambda t, dc=dc, mc=mc, pb=pb: t.matmul(
                            bank(pb),
                            wkv[:, dc * W2G + mc * 128: dc * W2G + (mc + 1) * 128],
                            lnxT[:, dc * c.TBLK:(dc + 1) * c.TBLK],
                            start=(dc == 0), stop=(dc == c.DC - 1)),
                            waits=w, inc="pe" if dc == c.DC - 1 else None)
                    m_mm = P.cnt["pe"]
                    st_blk["lnxT_users"] = [m_mm]
                    if not is_v:
                        mea = P.do("scalar", lambda e, mc=mc, tb=tb, pb=pb: e.activation(
                            kT[:, mc * c.T + tb * c.TBLK: mc * c.T + (tb + 1) * c.TBLK],
                            bank(pb), AF.Copy),
                            waits=[("pe", m_mm)] + war_kT["v"], inc="ac")
                        mr = P.do("vector", lambda v, mc=mc, tb=tb: v.tensor_reduce(
                            qmax4[:, 6:7],
                            kT[:, mc * c.T + tb * c.TBLK: mc * c.T + (tb + 1) * c.TBLK],
                            axis=AX.X, op=AluOpType.max,
                            apply_absolute_value=True),
                            waits=[("ac", mea), ("dv", kmax_run)], inc="dv")
                        kmax_run = P.do("vector", lambda v: v.tensor_tensor(
                            qmax4[:, 7:8], qmax4[:, 7:8], qmax4[:, 6:7],
                            AluOpType.max), waits=[("dv", mr)], inc="dv")
                        BW[pb] = [("ac", mea)]
                    else:
                        mev = P.do("scalar", lambda e, pb=pb: e.activation(
                            vtmp, bank(pb), AF.Copy),
                            waits=[("pe", m_mm), BW_vtmp[0]] if BW_vtmp else [("pe", m_mm)],
                            inc="ac")
                        BW[pb] = [("ac", mev)]
                        mcv = mc - nK
                        vb = 4 + (mcv % 2)
                        nsub = c.TBLK // 128
                        for j in range(nsub):
                            w = ([("ac", mev)] + BW[vb]) if j == 0 else ()
                            P.do("tensor", lambda t, j=j, vb=vb: t.transpose(
                                bank(vb, 128 * j, 128),
                                vtmp[:, 128 * j:128 * (j + 1)], id32),
                                waits=w, inc="pe" if j == nsub - 1 else None)
                        m_vt = P.cnt["pe"]
                        BW_vtmp[:] = [("pe", m_vt)]
                        mlv = None
                        for h2 in range(2):
                            hl = 2 * mcv + h2
                            mlv = P.do("vector", lambda v, h2=h2, hl=hl, tb=tb,
                                       nsub=nsub, vb=vb:
                                       v.tensor_copy(
                                           apx(vP, tb * nsub * c.VW + hl * 65,
                                               [[c.VW, nsub], [1, 64]]),
                                           apx(ps, 512 * vb + h2 * 64,
                                               [[128, nsub], [1, 64]])),
                                       waits=([("pe", m_vt)] + war_vP["v"]) if h2 == 0 else (),
                                       inc="dv")
                        BW[vb] = [("dv", mlv)]
            war_wq["v"] = [("pe", P.cnt["pe"])]
            w5 = BW[5]
            P.do("tensor", lambda t: t.transpose(bank(5, 0, 128)[0:1, :],
                                                 qmax4[:, 7:8], id32),
                 waits=[("dv", kmax_run)] + w5, inc="pe")
            mt = P.cnt["pe"]
            mk = P.do("vector", lambda v, g=g: v.tensor_reduce(
                kmaxs[:, g:g + 1], bank(5)[0:1, 0:128], axis=AX.X,
                op=AluOpType.max), waits=[("pe", mt)], inc="dv")
            BW[5] = [("dv", mk)]
            m_kv_done[g] = mk

        BW_vtmp = []

        war_qT = {"v": []}
        war_oT = {"v": []}
        war_wq = {"v": []}
        war_otmp = {0: [], 1: []}
        war_recl = {0: [], 1: []}
        war_recl16 = {"v": []}
        tailstate = {"fn": None}
        prolstate = {"fn": None, "m_qT": 0}
        bgq = []
        bgq_meta = {"ob0": -1}
        pvstate = {"m_pv_0": 0, "m_pv_1": 0}
        ringstate = {"exp": {}, "pv": {}, "i": 0, "pgc": 0}

        def phase_B(g):
            m_w = []
            for dc in range(c.DC):
                m_w.append(dma("gpsimd", wq[:, dc * c.HG * 64:(dc + 1) * c.HG * 64],
                    Wqkv[dc * 128:(dc + 1) * 128,
                         g * c.HG * 64:(g + 1) * c.HG * 64],
                    waits=war_wq["v"] if dc == 0 else ()))
            for hl in range(c.HG):
                m_w.append(dma("gpsimd", wo[0:64, hl * c.D:(hl + 1) * c.D],
                    Wout[(g * c.HG + hl) * 64:(g * c.HG + hl) * 64 + 64, :],
                    waits=([("gp", m_init)] + war_wq["v"]) if hl == 0 else ()))
            m_w = wall(m_w)
            m_pw = [("pe", P.cnt["pe"])]
            m_sq = P.do("scalar", lambda e: e.activation(
                lnxT[:, 0:c.DC * c.HG * 64], wq, AF.Square),
                waits=m_w + m_pw + [("ac", P.cnt["ac"])], inc="ac")
            NQ = c.HG * 64
            for dc in range(c.DC):
                w = ([("ac", m_sq)] + BW[0]) if dc == 0 else ()
                P.do("tensor", lambda t, dc=dc: t.matmul(
                    bank(0, 0, NQ)[0:1, :], ones_bf,
                    lnxT[:, dc * NQ:(dc + 1) * NQ],
                    start=(dc == 0), stop=(dc == c.DC - 1)),
                    waits=w, inc="pe" if dc == c.DC - 1 else None)
            m_cn = P.cnt["pe"]
            st_blk["lnxT_users"] = [m_cn]
            pvstate["pT_free"] = 0
            mr2 = P.do("vector", lambda v: v.tensor_reduce(
                sc1[:, 2:3], bank(0, 0, NQ)[0:1, :], axis=AX.X,
                op=AluOpType.max), waits=[("pe", m_cn)], inc="dv")
            BW[0] = [("dv", mr2)]
            ml1 = P.do("scalar", lambda e: e.activation(
                sc1[:, 3:4], sc1[:, 2:3], AF.Ln), waits=[("dv", mr2)], inc="ac")
            ml2 = P.do("scalar", lambda e: e.activation(
                sc1[:, 3:4], sc1[:, 3:4], AF.Exp, scale=0.5),
                waits=[("ac", ml1)], inc="ac")
            mr3 = P.do("vector", lambda v, g=g: v.scalar_tensor_tensor(
                sc1[:, 1:2], sc1[:, 3:4],
                -float(c.DH) * SC * math.sqrt(c.D), kmaxs[:, g:g + 1],
                AluOpType.mult, AluOpType.mult),
                waits=[("ac", ml2), ("dv", m_kv_done[g])], inc="dv")
            mr4 = P.do("vector", lambda v: v.tensor_copy(
                recl16[0:1, 512:513], sc1[:, 1:2]),
                waits=[("dv", mr3)] + war_recl16["v"], inc="dv")
            w1b = BW[1]
            P.do("tensor", lambda t: t.matmul(
                bank(1, 0, 1), ones16, recl16[0:1, 512:513],
                start=True, stop=True),
                waits=[("dv", mr4)] + w1b, inc="pe")
            mbc = P.cnt["pe"]
            m_negB = P.do("vector", lambda v: v.tensor_copy(
                negBv, bank(1, 0, 1)),
                waits=[("pe", mbc), ("ac", P.cnt["ac"])], inc="dv")
            BW[1] = BW[1] + [("dv", m_negB)]

            def emit_prologueB(qb, norms, g=None, pbanks=(2, 3)):
                m_lnxT = ln1_part2(norms, b0=pbanks[0])
                mq = 0
                qsl = (qb % 2) * c.CH * c.TBLK
                for mc in range(c.CH):
                    pb = pbanks[mc % 2]
                    for dc in range(c.DC):
                        w = (BW[pb] + m_w + [("dv", m_lnxT)]) if dc == 0 else ()
                        P.do("tensor", lambda t, dc=dc, mc=mc, pb=pb: t.matmul(
                            bank(pb),
                            wq[:, dc * c.HG * 64 + mc * 128: dc * c.HG * 64 + (mc + 1) * 128],
                            lnxT[:, dc * c.TBLK:(dc + 1) * c.TBLK],
                            start=(dc == 0), stop=(dc == c.DC - 1)),
                            waits=w, inc="pe" if dc == c.DC - 1 else None)
                    m_mm = P.cnt["pe"]
                    st_blk["lnxT_users"] = [m_mm]
                    if mc % 2 == 1 or mc == c.CH - 1:
                        nmc = 2 if mc % 2 == 1 else 1
                        mc0 = mc - nmc + 1
                        me = P.do("vector", lambda v, mc0=mc0, nmc=nmc, qsl=qsl:
                                  v.tensor_copy(
                                      qT[:, qsl + mc0 * c.TBLK:
                                         qsl + (mc0 + nmc) * c.TBLK],
                                      bank(pbanks[0] if mc0 % 2 == 0 else pbanks[1],
                                           0, nmc * 512)),
                                  waits=[("pe", m_mm)] + war_qT["v"], inc="dv")
                        mq = me
                        BW[pbanks[0]] = [("dv", me)]
                        BW[pbanks[1]] = [("dv", me)]
                return mq

            def emit_outproj(qb, g, ob0, spread=True):
                m_oT = P.cnt["dv"]
                nsub = c.TBLK // 128
                OW = min(512, c.D)
                ops = []
                for tc in range(nsub):
                    for oh in range(c.D // OW):
                        pb = ob0 + (tc * (c.D // OW) + oh) % 2
                        grp = []
                        for hl in range(c.HG):
                            first = hl == 0
                            last = hl == c.HG - 1

                            def mm(tc=tc, oh=oh, hl=hl, pb=pb, first=first,
                                   last=last, qb=qb, g=g):
                                w = ([("dv", m_oT)] + m_w + BW[pb]) if first else ()
                                P.do("tensor", lambda t: t.matmul(
                                    bank(pb, 0, OW),
                                    oT[:, hl * c.TBLK + tc * 128:
                                       hl * c.TBLK + (tc + 1) * 128],
                                    wo[:, hl * c.D + oh * OW:
                                       hl * c.D + (oh + 1) * OW],
                                    start=first, stop=last),
                                    waits=w, inc="pe" if last else None)
                                if last:
                                    m_mm = P.cnt["pe"]
                                    war_oT["v"] = [("pe", m_mm)]
                                    dsty = y_acc[:, (qb * nsub + tc) * c.D + oh * OW:
                                                 (qb * nsub + tc) * c.D + (oh + 1) * OW]
                                    if g == 0:
                                        mev = P.do("vector",
                                                   lambda v, d=dsty, pb=pb: v.tensor_copy(
                                                       d, bank(pb, 0, OW)),
                                                   waits=[("pe", m_mm)], inc="dv")
                                    else:
                                        mev = P.do("vector",
                                                   lambda v, d=dsty, pb=pb: v.tensor_tensor(
                                                       d, d, bank(pb, 0, OW),
                                                       AluOpType.add),
                                                   waits=[("pe", m_mm),
                                                          ("ac", m_yg0[0])], inc="dv")
                                    BW[pb] = [("dv", mev)]
                            grp.append(mm)
                        ops.append(lambda grp=grp: [f() for f in grp])
                if spread:
                    bgq_meta["ob0"] = ob0
                    bgq.extend(ops)
                else:
                    for op in ops:
                        op()

            st_blk["evac_eng"] = "vector"
            for _h in range(2):            st_blk["evac_eng"] = "vector"
            for _h in range(2):            st_blk["evac_eng"] = "vector"
            for _h in range(2):            st_blk["evac_eng"] = "vector"
            for _h in range(2):
                war_otmp[_h] = war_otmp[_h] + [("ac", P.cnt["ac"])]
                war_recl[_h] = war_recl[_h] + [("ac", P.cnt["ac"])]
            normsB = ln1_part1(0)
            prol = emit_prologueB(0, normsB, g=g)
            for qb in range(c.NTB):
                m_qT = prol
                if qb + 1 < c.NTB:
                    normsB = ln1_part1((qb + 1) * c.TBLK)
                    if c.CH >= 3:
                        def mk_prol(qb=qb, g=g, nb_=normsB):
                            def fn():
                                prolstate["m_qT"] = emit_prologueB(
                                    qb + 1, nb_, g=g, pbanks=(4, 5))
                            return fn
                        prolstate["fn"] = mk_prol()
                def emit_epilogue(p, obase, m_pv):
                    hls = (2 * p, 2 * p + 1)
                    mults = []
                    for half in range(2):
                        ob = obase + half
                        hl = hls[half]
                        mo = P.do("vector", lambda v, ob=ob, half=half:
                                  v.tensor_copy(
                                      otmp[:, half * 512:(half + 1) * 512],
                                      bank(ob)[0:64, :]),
                                  waits=[("pe", m_pv)]
                                  + war_otmp[half], inc="dv")
                        mrc = P.do("vector", lambda v, ob=ob, half=half:
                                   v.reciprocal(
                                       recl[0:1, half * 512:(half + 1) * 512],
                                       bank(ob)[64:65, :]),
                                   waits=[("pe", m_pv),
                                          ("dv", mo)] + war_recl[half], inc="dv")
                        mrc2 = P.do("vector", lambda v, half=half: v.tensor_copy(
                            recl16[0:1, 0:512],
                            recl[0:1, half * 512:(half + 1) * 512]),
                            waits=[("dv", mrc)] + war_recl16["v"], inc="dv")
                        P.do("tensor", lambda t, ob=ob: t.matmul(
                            bank(ob), ones16,
                            recl16[0:1, 0:512],
                            start=True, stop=True),
                            waits=[("dv", mrc2)], inc="pe")
                        mb = P.cnt["pe"]
                        war_recl16["v"] = [("pe", mb)]
                        mm_ = P.do("vector", lambda v, ob=ob, hl=hl, half=half:
                                   v.tensor_tensor(
                                       oT[0:64, hl * c.TBLK:(hl + 1) * c.TBLK],
                                       otmp[:, half * 512:(half + 1) * 512],
                                       bank(ob)[0:64, :], AluOpType.mult),
                                   waits=[("pe", mb)] + war_oT["v"], inc="dv")
                        BW[ob] = [("dv", mm_)]
                        war_otmp[half] = [("dv", mm_)]
                        war_recl[half] = [("pe", mb)]
                        mults.append(mm_)
                    return mults

                pending_epi = None
                EH, PH = ringstate["exp"], ringstate["pv"]
                qsl = (qb % 2) * c.CH * c.TBLK
                for p in range(c.CH):
                    hlA, hlB = 2 * p, 2 * p + 1
                    obase = 6 if (ringstate["pgc"] % 2 == 0) else 4
                    ringstate["pgc"] += 1
                    base = ringstate["i"]
                    if obase == bgq_meta["ob0"]:
                        while bgq:
                            bgq.pop(0)()

                    def emit_qk(t_, p=p, base=base, qsl=qsl):
                        gi = base + t_
                        sp = gi % 2
                        for half in range(2):
                            rb = 2 * sp + half
                            r0 = 64 * half
                            w = []
                            if half == 0:
                                w = [("ac", EH.get(gi - 2, 0))]
                                if t_ == 0:
                                    w += [("dv", m_qT)] + BW[0] + BW[1]
                                elif t_ == 1:
                                    w += BW[2] + BW[3]
                            P.do("tensor",
                                 lambda t, t_=t_, p=p, r0=r0, rb=rb, qsl=qsl: t.matmul(
                                     bank(rb),
                                     kT[r0:r0 + 64,
                                        p * c.T + t_ * 128: p * c.T + (t_ + 1) * 128],
                                     qT[r0:r0 + 64,
                                        qsl + p * c.TBLK: qsl + (p + 1) * c.TBLK],
                                     start=True, stop=True),
                                 waits=w, inc="pe" if half == 1 else None)
                        EH[("qk", gi)] = P.cnt["pe"]

                    def emit_exp2(t_, base=base):
                        gi = base + t_
                        sl4 = gi % 4
                        m = P.do(
                            "scalar", lambda e, sl4=sl4: e.activation(
                                pT[:, sl4 * 1024: (sl4 + 2) * 1024],
                                bank(0, 0, 2048),
                                AF.Exp, scale=SC, bias=negBv[:, 0:1]),
                            waits=[("pe", EH[("qk", gi + 1)]),
                                   ("pe", PH.get(gi - 4, 0)),
                                   ("pe", PH.get(gi - 3, 0)),
                                   ("pe", pvstate.get("pT_free", 0)),
                                   ("dv", m_negB)], inc="ac")
                        EH[gi] = m
                        EH[gi + 1] = m

                    def emit_pv(t_, p=p, obase=obase, base=base):
                        gi = base + t_
                        sl4 = gi % 4
                        hls = (2 * p, 2 * p + 1)
                        for half in range(2):
                            ob = obase + half
                            hl = hls[half]
                            w = []
                            if half == 0:
                                w = [("ac", EH[gi])]
                                if t_ == 0:
                                    w += BW[obase] + BW[obase + 1]
                            P.do("tensor",
                                 lambda t, t_=t_, hl=hl, sl4=sl4, ob=ob, half=half:
                                 t.matmul(
                                     bank(ob, 0, 512)[0:65, :],
                                     vP[:, t_ * c.VW + hl * 65:
                                        t_ * c.VW + hl * 65 + 65],
                                     pT[:, sl4 * 1024 + half * 512:
                                        sl4 * 1024 + (half + 1) * 512],
                                     start=(t_ == 0), stop=(t_ == c.KT - 1)),
                                 waits=w, inc="pe" if half == 1 else None)
                        PH[gi] = P.cnt["pe"]
                        if (bgq and obase != bgq_meta["ob0"]
                                and t_ % 4 == 0):
                            bgq.pop(0)()

                    for u in range(c.KT // 2):
                        emit_qk(2 * u)
                        emit_qk(2 * u + 1)
                        emit_exp2(2 * u)
                        if u > 0:
                            emit_pv(2 * u - 2)
                            emit_pv(2 * u - 1)
                        if (u == min(2, c.KT // 2 - 1)
                                and pending_epi is not None):
                            pending_epi()
                            pending_epi = None
                        if (u == min(4, c.KT // 2 - 1) and p == 0
                                and tailstate["fn"] is not None):
                            tailstate["fn"]()
                            tailstate["fn"] = None
                        if (u == min(3, c.KT // 2 - 1) and p == 2
                                and prolstate["fn"] is not None):
                            prolstate["fn"]()
                            prolstate["fn"] = None
                    emit_pv(c.KT - 2)
                    emit_pv(c.KT - 1)
                    ringstate["i"] += c.KT
                    pending_epi = (lambda p=p, obase=obase, m=P.cnt["pe"]:
                                   emit_epilogue(p, obase, m))
                while bgq:
                    bgq.pop(0)()
                war_qT["v"] = [("pe", P.cnt["pe"])]
                mac_all = P.cnt["ac"]
                for b_ in range(4):
                    BW[b_] = [("ac", mac_all)]
                if qb + 1 < c.NTB:
                    if c.CH >= 3:
                        assert prolstate["fn"] is None
                        prol = prolstate["m_qT"]
                    else:
                        prol = emit_prologueB(qb + 1, normsB, g=g)
                    ob0_now = 6 if ((ringstate["pgc"] - 1) % 2 == 0) else 4
                    def mk_tail(qb=qb, g=g, ep=pending_epi, ob0=ob0_now):
                        def fn():
                            if ep is not None:
                                ep()
                            emit_outproj(qb, g, ob0)
                        return fn
                    tailstate["fn"] = mk_tail()
                    pending_epi = None
                else:
                    assert tailstate["fn"] is None
                    if pending_epi is not None:
                        pending_epi()
                        pending_epi = None
                    emit_outproj(qb, g,
                                 6 if ((ringstate["pgc"] - 1) % 2 == 0) else 4,
                                 spread=False)
            war_wkv["v"] = [("pe", P.cnt["pe"])]
            war_kT["v"] = [("pe", P.cnt["pe"])]
            war_vP["v"] = [("pe", P.cnt["pe"])]
            war_qT["v"] = [("pe", P.cnt["pe"])]

        def phase3_prologue(M_END, tz):
            s1T, s2T, wch, wch2, sgn = (tz["s1T"], tz["s2T"], tz["wch"],
                                        tz["wch2"], tz["sgn"])
            war_wch = {0: list(BW[0]) + list(BW[1]) + M_END, 1: list(M_END)}
            war_sgn = {0: list(M_END), 1: list(M_END)}
            sgn_chain = {"v": list(M_END)}
            war_red = []
            ms = {}
            for fc in range(c.FC):
                sl = fc % 2
                WC = wch[:, sl * c.D:(sl + 1) * c.D]
                SG = sgn[:, 0:c.D]
                mw = dma("sync", WC, W1[fc * 128:(fc + 1) * 128, :],
                         waits=war_wch[sl])
                mrd = P.do("vector", lambda v, WC=WC: v.tensor_reduce(
                    stats[:, 18:19], WC, axis=AX.X, op=AluOpType.add,
                    apply_absolute_value=True), waits=[mw] + war_red, inc="dv")
                mal = P.do("vector", lambda v, fc=fc: v.tensor_scalar(
                    alpha1[:, fc:fc + 1], stats[:, 18:19], 1.0 / c.D, None,
                    AluOpType.mult, AluOpType.bypass), waits=[("dv", mrd)], inc="dv")
                war_red = [("dv", mal)]
                msg = P.do("scalar", lambda e, WC=WC, SG=SG: e.activation(
                    SG, WC, AF.Sign),
                    waits=[mw] + sgn_chain["v"], inc="ac")
                war_wch[sl] = [("dv", mrd), ("ac", msg)]

                def dst(j0, cn, fc=fc):
                    return apx(s1T, j0 * c.F + fc * 128, [[c.F, cn], [1, 128]])
                (semn, mev), mtp = transp(SG, c.DC, 0 if sl == 0 else 2, dst,
                                          wait_first=[("ac", msg)])
                sgn_chain["v"] = [("pe", mtp)]
            ms["s1T"] = P.cnt["dv"]
            war_wch = {0: war_wch[0] + war_wch[1], 1: war_wch[0] + war_wch[1]}
            for dc in range(c.DC):
                sl = dc % 2
                WC = wch2[:, sl * c.F:(sl + 1) * c.F]
                SG = sgn[:, 0:c.F]
                mw = dma("sync", WC, W2[dc * 128:(dc + 1) * 128, :],
                         waits=war_wch[sl])
                mrd = P.do("vector", lambda v, WC=WC: v.tensor_reduce(
                    stats[:, 18:19], WC, axis=AX.X, op=AluOpType.add,
                    apply_absolute_value=True), waits=[mw] + war_red, inc="dv")
                mal = P.do("vector", lambda v, dc=dc: v.tensor_scalar(
                    alpha2[:, dc:dc + 1], stats[:, 18:19], 1.0 / c.F, None,
                    AluOpType.mult, AluOpType.bypass), waits=[("dv", mrd)], inc="dv")
                war_red = [("dv", mal)]
                msg = P.do("scalar", lambda e, WC=WC, SG=SG: e.activation(
                    SG, WC, AF.Sign),
                    waits=[mw] + sgn_chain["v"], inc="ac")
                war_wch[sl] = [("dv", mrd), ("ac", msg)]

                def dst(j0, cn, dc=dc):
                    return apx(s2T, j0 * c.D + dc * 128, [[c.D, cn], [1, 128]])
                (semn, mev), mtp = transp(SG, c.FC, 0 if sl == 0 else 2, dst,
                                          wait_first=[("ac", msg)])
                sgn_chain["v"] = [("pe", mtp)]
            ms["s2T"] = P.cnt["dv"]
            ms["alpha"] = P.cnt["dv"]
            return ms

        def phase3(M_END, ms, tz):
            s1T, s2T, hT = tz["s1T"], tz["s2T"], tz["hT"]
            x3, x1, ln3T = tz["x3"], tz["x1"], tz["ln3T"]
            ln3f = tz["ln3f"]
            o2tm, oscr = tz["o2tm"], tz["oscr"]
            out_dmas = phase3.out_dmas = []
            m_y_done = tz["m_y_done"]
            MP = tz["m_prol"]
            war_x3 = {0: list(M_END) + MP, 1: list(M_END) + MP}
            war_x1 = {0: MP, 1: MP}
            war_ln3f = {k: MP for k in range(2 * (c.TB3 // 128))}
            war_ln3T = {0: MP, 1: MP}
            war_hT = {"v": []}
            war_o2 = []
            nsub = c.TB3 // 128
            DCT = c.DC * c.TB3

            def front(tb):
                t0 = tb * c.TB3
                sl = tb % 2
                X3 = lambda s: x3[:, (sl * nsub + s) * c.D:(sl * nsub + s + 1) * c.D]
                X1 = lambda s: x1[:, (sl * nsub + s) * c.D:(sl * nsub + s + 1) * c.D]
                L3s = lambda s: ln3f[:, (sl * nsub + s) * c.D:(sl * nsub + s + 1) * c.D]
                mx = []
                for s in range(nsub):
                    mx.append(dma("sync", X3(s),
                                  X[t0 + s * 128: t0 + 128 * (s + 1), :],
                                  waits=war_x3[sl] if s == 0 else ()))
                pows = []
                for s in range(nsub):
                    ysl = y_acc[:, (t0 // 128 + s) * c.D:(t0 // 128 + s + 1) * c.D]
                    pows.append(ln_stats(ysl, sl * 2 + s,
                                         m_y_done if tb == 0 else []))
                m_x1 = []
                for s in range(nsub):
                    ysl = y_acc[:, (t0 // 128 + s) * c.D:(t0 // 128 + s + 1) * c.D]
                    mn = ln_norm(ysl, L3s(s), sl * 2 + s, pows[s],
                                 extra_waits=war_ln3f[sl * nsub + s])
                    ma = P.do("vector", lambda v, s=s, X3=X3, X1=X1, L3s=L3s:
                              v.tensor_tensor(X1(s), X3(s), L3s(s), AluOpType.add),
                              waits=[("dv", mn), mx[s]]
                              + (war_x1[sl] if s == 0 else []), inc="dv")
                    war_ln3f[sl * nsub + s] = [("dv", ma)]
                    m_x1.append(ma)
                pows = []
                for s in range(nsub):
                    pows.append(ln_stats(X1(s), sl * 2 + s, [("dv", m_x1[s])]))
                norms = []
                for s in range(nsub):
                    L3 = ln3f[:, (sl * nsub + s) * c.D:(sl * nsub + s + 1) * c.D]
                    mn = ln_norm(X1(s), L3, sl * 2 + s, pows[s],
                                 extra_waits=war_ln3f[sl * nsub + s])
                    norms.append(mn)
                return norms

            def emit_ln3T(tb, norms):
                sl = tb % 2
                m_ln3 = None
                for s_ in range(nsub):
                    L3 = ln3f[:, (sl * nsub + s_) * c.D:(sl * nsub + s_ + 1) * c.D]

                    def dst(j0, cn, s_=s_, sl=sl):
                        return apx(ln3T, sl * DCT + j0 * c.TB3 + s_ * 128,
                                   [[c.TB3, cn], [1, 128]])
                    (semn, mev), mtp = transp(L3, c.DC, 0, dst,
                                              wait_first=[("dv", norms[s_])]
                                              + war_ln3T[sl])
                    war_ln3f[sl * nsub + s_] = [("pe", mtp)]
                    m_ln3 = mev
                return m_ln3

            f_norms = front(0)
            pend_t = {"norms": f_norms, "tb": 0}
            for tb in range(c.NT3):
                t0 = tb * c.TB3
                sl = tb % 2
                X3 = lambda s, sl=sl: x3[:, (sl * nsub + s) * c.D:
                                         (sl * nsub + s + 1) * c.D]
                X1 = lambda s, sl=sl: x1[:, (sl * nsub + s) * c.D:
                                         (sl * nsub + s + 1) * c.D]
                L3s = lambda s, sl=sl: ln3f[:, (sl * nsub + s) * c.D:
                                            (sl * nsub + s + 1) * c.D]
                if pend_t["tb"] == tb:
                    m_ln3 = emit_ln3T(tb, pend_t["norms"])
                else:
                    m_ln3 = pend_t["m_ln3"]
                for fc in range(c.FC):
                    pb = 2 + (fc % 2)
                    for dc in range(c.DC):
                        w = (BW[pb] + [("dv", m_ln3),
                                       ("dv", ms["s1T"])]) if dc == 0 else ()
                        P.do("tensor", lambda t, fc=fc, dc=dc, pb=pb, sl=sl:
                             t.matmul(
                                 bank(pb, 0, c.TB3),
                                 s1T[:, dc * c.F + fc * 128: dc * c.F + (fc + 1) * 128],
                                 ln3T[:, sl * DCT + dc * c.TB3:
                                      sl * DCT + (dc + 1) * c.TB3],
                                 start=(dc == 0), stop=(dc == c.DC - 1)),
                             waits=w, inc="pe" if dc == c.DC - 1 else None)
                    m_mm = P.cnt["pe"]
                    war_ln3T[sl] = [("pe", m_mm)]
                    mg = P.do("scalar", lambda e, fc=fc, pb=pb: e.activation(
                        hT[:, fc * c.TB3:(fc + 1) * c.TB3], bank(pb, 0, c.TB3),
                        AF.Gelu, scale=alpha1[:, fc:fc + 1], bias=b1s[:, fc:fc + 1]),
                        waits=[("pe", m_mm), ("dv", ms["alpha"])] + m_bload
                        + war_hT["v"], inc="ac")
                    BW[pb] = [("ac", mg)]
                m_h = P.cnt["ac"]
                if tb + 1 < c.NT3:
                    nxt = front(tb + 1)
                    pend_t = {"norms": nxt, "tb": tb + 1,
                              "m_ln3": emit_ln3T(tb + 1, nxt)}
                for dc in range(c.DC):
                    pb = 4 + (dc % 2)
                    for fc in range(c.FC):
                        w = (BW[pb] + [("ac", m_h), ("dv", ms["s2T"])]) if fc == 0 else ()
                        P.do("tensor", lambda t, fc=fc, dc=dc, pb=pb: t.matmul(
                            bank(pb, 0, c.TB3),
                            s2T[:, fc * c.D + dc * 128: fc * c.D + (dc + 1) * 128],
                            hT[:, fc * c.TB3:(fc + 1) * c.TB3],
                            start=(fc == 0), stop=(fc == c.FC - 1)),
                            waits=w, inc="pe" if fc == c.FC - 1 else None)
                    m_mm = P.cnt["pe"]
                    war_hT["v"] = [("pe", m_mm)]
                    mo = P.do("scalar", lambda e, dc=dc, pb=pb: e.activation(
                        oscr, bank(pb, 0, c.TB3), AF.Identity,
                        scale=alpha2[:, dc:dc + 1], bias=b2s[:, dc:dc + 1]),
                        waits=[("pe", m_mm), ("dv", ms["alpha"])] + m_bload
                        + war_o2, inc="ac")
                    BW[pb] = [("ac", mo)]

                    def dst(j0, cn, dc=dc):
                        return apx(o2tm, j0 * c.D + dc * 128, [[c.D, cn], [1, 128]])
                    (semn, mev), mtp = transp(oscr, nsub, 6, dst,
                                              wait_first=[("ac", mo)])
                    war_o2 = [("pe", mtp)]
                    m_o2 = mev
                pows = []
                for s in range(nsub):
                    pows.append(ln_stats(o2tm[:, s * c.D:(s + 1) * c.D],
                                         sl * 2 + s, [("dv", m_o2)]))
                for s in range(nsub):
                    mn = ln_norm(o2tm[:, s * c.D:(s + 1) * c.D], L3s(s),
                                 sl * 2 + s, pows[s],
                                 extra_waits=war_ln3f[sl * nsub + s])
                    ma = P.do("vector", lambda v, s=s, X3=X3, X1=X1, L3s=L3s:
                              v.tensor_tensor(X3(s), X1(s), L3s(s), AluOpType.add),
                              waits=[("dv", mn)], inc="dv")
                    war_ln3f[sl * nsub + s] = [("dv", ma)]
                    mo_ = dma("sync", OUT[t0 + s * 128: t0 + (s + 1) * 128, :],
                              X3(s), waits=[("dv", ma)])
                    out_dmas.append(mo_)
                war_x3[sl] = wall(out_dmas[-nsub:])
                war_x1[sl] = [("dv", P.cnt["dv"])]

        m_yg0 = [0]
        phase_A(0)
        phase_B(0)
        m_yg0[0] = P.cnt["ac"]
        phase_A(1)
        phase_B(1)
        M_END = [("pe", P.cnt["pe"]), ("dv", P.cnt["dv"]), ("ac", P.cnt["ac"]),
                 ("gp", P.cnt["gp"])]
        m_y_done = [("dv", P.cnt["dv"]), ("ac", P.cnt["ac"])]
        st2.close()
        s1T = sb("s1T", [128, c.DC * c.F], FP8)
        s2T = sb("s2T", [128, c.FC * c.D], FP8)
        hT = sb("hT", [128, c.FC * c.TB3], BF16)
        o2tm = sb("o2tm", [128, (c.TB3 // 128) * c.D], FP16)
        oscr = sb("oscr", [128, c.TB3], F32)
        st3 = ExitStack()
        wch = st3.enter_context(nc.sbuf_tensor("wch", [128, 2 * c.D], F32))[:, :]
        wch2 = st3.enter_context(nc.sbuf_tensor("wch2", [128, 2 * c.F], F32))[:, :]
        sgn = st3.enter_context(
            nc.sbuf_tensor("sgn", [128, max(c.D, c.F)], F32))[:, :]
        p3tensors = dict(s1T=s1T, s2T=s2T, hT=hT, o2tm=o2tm, oscr=oscr,
                         wch=wch, wch2=wch2, sgn=sgn)
        ms = phase3_prologue(M_END, p3tensors)
        m_prol = [("pe", P.cnt["pe"]), ("dv", P.cnt["dv"]), ("ac", P.cnt["ac"])]
        st3.close()
        nsub3 = c.TB3 // 128
        p3tensors["x3"] = sb("x3", [128, 2 * nsub3 * c.D], F32)
        p3tensors["x1"] = sb("x1", [128, 2 * nsub3 * c.D], F32)
        p3tensors["ln3f"] = sb("ln3f", [128, 2 * (c.TB3 // 128) * c.D], F32)
        p3tensors["lnscr"] = None
        p3tensors["ln3T"] = sb("ln3T", [128, 2 * c.DC * c.TB3], BF16)
        p3tensors["m_y_done"] = m_y_done
        p3tensors["m_prol"] = m_prol
        phase3(M_END, ms, p3tensors)

        P.do("sync", lambda e: e.nop(), waits=wall(phase3.out_dmas))

        P.emit(block, semh, first_waits=[("gp", m_init)])
    return nc


def kernel(L0_attn_x, L0_attn_w_qkv, L0_attn_w_out, L0_ffn_w1, L0_ffn_b1,
           L0_ffn_w2, L0_ffn_b2):
    x = np.ascontiguousarray(np.asarray(L0_attn_x, dtype=np.float32))
    wqkv = np.ascontiguousarray(np.asarray(L0_attn_w_qkv, dtype=np.float32))
    wout = np.ascontiguousarray(np.asarray(L0_attn_w_out, dtype=np.float32))
    w1 = np.ascontiguousarray(np.asarray(L0_ffn_w1, dtype=np.float32))
    b1 = np.ascontiguousarray(np.asarray(L0_ffn_b1, dtype=np.float32))
    w2 = np.ascontiguousarray(np.asarray(L0_ffn_w2, dtype=np.float32))
    b2 = np.ascontiguousarray(np.asarray(L0_ffn_b2, dtype=np.float32))

    cfg = Cfg(T=4096, D=1024, H=16, F=4096)
    nc = build_nc(cfg)
    in_maps = []
    for b in range(8):
        in_maps.append({"x": x[b], "w_qkv": wqkv, "w_out": wout,
                        "w1": w1, "b1": b1, "w2": w2, "b2": b2})
    res = run_bass_kernel_spmd(nc, in_maps, core_ids=list(range(8)))
    out = np.stack([res.results[b]["out"] for b in range(8)], axis=0)
    return out.astype(np.float32)


# version 29
# speedup vs baseline: 1.1421x; 1.1421x over previous
import math
from contextlib import ExitStack

import numpy as np
import concourse.bass as bass
import concourse.mybir as mybir
from concourse.alu_op_type import AluOpType
from concourse.bass_utils import run_bass_kernel_spmd

F32 = mybir.dt.float32
BF16 = mybir.dt.bfloat16
FP16 = mybir.dt.float16
FP8 = mybir.dt.float8e4
AF = mybir.ActivationFunctionType
AX = mybir.AxisListType

ENGINES = ("tensor", "vector", "scalar", "gpsimd", "sync")
SEMS = (("pe", "dv", "ac", "gp", "out")
        + tuple(f"qs{i}" for i in range(8)) + tuple(f"qg{i}" for i in range(8)))


def wall(tuples):
    d = {}
    for s_, v in tuples:
        if v:
            d[s_] = max(d.get(s_, 0), v)
    return list(d.items())


class Cfg:
    def __init__(self, T=4096, D=1024, H=16, F=4096):
        self.T, self.D, self.H, self.F = T, D, H, F
        self.DH = 64
        assert H * self.DH == D
        self.DC = D // 128
        self.FC = F // 128
        self.KT = T // 128
        self.G = 2
        self.HG = H // 2
        self.CH = self.HG * 64 // 128
        self.TBLK = 512
        self.NTB = T // self.TBLK
        self.TB3 = 256
        self.NT3 = T // self.TB3
        self.VW = self.HG * 65
        self.scale = 1.0 / math.sqrt(D)
        self.kgroups = []
        k = 0
        while k < self.KT:
            n = min(3, self.KT - k)
            self.kgroups.append((k, n))
            k += n


def apx(view, offset, dims):
    p = view.ap[0]
    return bass.AP(view.tensor, offset, [[p[0], p[1]]] + [list(d) for d in dims])


class Prog:
    def __init__(self):
        self.ops = {e: [] for e in ENGINES}
        self.cnt = {s: 0 for s in SEMS}

    def do(self, eng, fn, waits=(), inc=None, n=1):
        self.ops[eng].append((fn, tuple(waits), inc, n))
        if inc is not None:
            self.cnt[inc] += n
            return self.cnt[inc]
        return None

    def emit(self, block, sems, first_waits=()):
        for ename in ENGINES:
            def body(eng, ename=ename):
                waited = {}
                if ename != "gpsimd":
                    for s, v in first_waits:
                        if v:
                            eng.wait_ge(sems[s], v)
                            waited[s] = v
                for fn, waits, inc, n in self.ops[ename]:
                    for s, v in waits:
                        if v and waited.get(s, 0) < v:
                            eng.wait_ge(sems[s], v)
                            waited[s] = v
                    instr = fn(eng)
                    if inc is not None:
                        instr.then_inc(sems[inc], n)
            getattr(block, ename)(body)


def build_nc(cfg: Cfg):
    c = cfg
    nc = bass.Bass("TRN2", target_bir_lowering=False)

    X = nc.dram_tensor("x", [c.T, c.D], F32, kind="ExternalInput").ap()
    Wqkv = nc.dram_tensor("w_qkv", [c.D, 3 * c.D], F32, kind="ExternalInput").ap()
    Wout = nc.dram_tensor("w_out", [c.D, c.D], F32, kind="ExternalInput").ap()
    W1 = nc.dram_tensor("w1", [c.F, c.D], F32, kind="ExternalInput").ap()
    B1 = nc.dram_tensor("b1", [c.F], F32, kind="ExternalInput").ap()
    W2 = nc.dram_tensor("w2", [c.D, c.F], F32, kind="ExternalInput").ap()
    B2 = nc.dram_tensor("b2", [c.D], F32, kind="ExternalInput").ap()
    OUT = nc.dram_tensor("out", [c.T, c.D], F32, kind="ExternalOutput").ap()

    P = Prog()
    with ExitStack() as st:
        def sb(name, shape, dtype):
            h = st.enter_context(nc.sbuf_tensor(name, shape, dtype))
            return h[tuple(slice(None) for _ in shape)]

        id32 = sb("id32", [128, 128], F32)
        id_bf = sb("id_bf", [128, 128], BF16)
        id16 = sb("id16", [128, 128], FP16)
        ones16 = sb("ones16", [1, 128], FP16)
        ones_bf = sb("ones_bf", [128, 1], BF16)
        eps_t = sb("eps_t", [128, 1], F32)
        mhalf = sb("mhalf", [128, 1], F32)
        alpha1 = sb("alpha1", [128, c.FC], F32)
        alpha2 = sb("alpha2", [128, c.DC], F32)
        b1s = sb("b1s", [128, c.FC], F32)
        b2s = sb("b2s", [128, c.DC], F32)
        y_acc = sb("y_acc", [128, (c.T // 128) * c.D], FP16)
        kmaxs = sb("kmaxs", [1, 2], F32)

        stats = sb("stats", [128, 80], F32)
        qmax4 = sb("qmax4", [128, 8], F32)
        sc1 = sb("sc1", [1, 8], F32)
        negBv = sb("negBv", [128, 1], F32)

        st2 = ExitStack()

        def sb2(name, shape, dtype):
            h = st2.enter_context(nc.sbuf_tensor(name, shape, dtype))
            return h[tuple(slice(None) for _ in shape)]

        kT = sb2("kT", [128, c.CH * c.T], BF16)
        vP = sb2("vP", [128, c.KT * c.VW], BF16)
        wsh = sb2("wsh", [128, c.DC * c.HG * 64 + (c.HG // 2) * c.D], BF16)
        wkv = wsh[:, 0:c.DC * c.HG * 128]
        wq = wsh[:, 0:c.DC * c.HG * 64]
        wo = wsh[:, c.DC * c.HG * 64:]
        xs = sb2("xs", [128, 4 * c.D], F32)
        lnx16 = sb2("lnx16", [128, 4 * c.D], BF16)
        lnxT = sb2("lnxT", [128, c.DC * c.TBLK], BF16)
        qT = sb2("qT", [128, 2 * c.CH * c.TBLK], BF16)
        oT = sb2("oT", [128, (c.HG // 2) * c.TBLK], BF16)
        pT = sb2("pT", [128, 4 * 1024], BF16)
        escr = sb2("escr", [128, 1024], F32)
        otmp = escr[0:64, :]
        recl = escr[64:65, :]
        vtmp16 = pT[:, 0:512]
        recl16 = sb2("recl16", [1, 520], FP16)

        psh = st.enter_context(nc.psum_tensor("ps", [128, 4096], F32))
        ps = psh[:, :]

        def bank(i, lo=0, width=512):
            return ps[:, 512 * i + lo: 512 * i + lo + width]

        semh = {s: st.enter_context(nc.semaphore(s)) for s in SEMS}
        block = st.enter_context(nc.Block())

        SC = c.scale
        BW = {i: [] for i in range(8)}

        rr = {"sync": 0, "gpsimd": 0}

        def dma(q, out, in_, waits=()):
            pre = "qs" if q == "sync" else "qg"
            sem = f"{pre}{rr[q] % 8}"
            rr[q] += 1
            chain = [(sem, P.cnt[sem])]
            v = P.do(q, lambda e, o=out, i=in_: e.dma_start(out=o, in_=i),
                     waits=list(waits) + chain, inc=sem, n=16)
            return (sem, v)

        m0 = P.do("gpsimd", lambda g: g.memset(id32, 0.0), inc="gp")

        P.do("gpsimd", lambda g: g.affine_select(
            out=id32, in_=id32, compare_op=AluOpType.not_equal, fill=1.0,
            base=0, pattern=[[-1, 128]], channel_multiplier=1),
            waits=[("gp", m0)], inc="gp")
        mib = P.do("gpsimd", lambda g: g.memset(id_bf, 0.0), inc="gp")
        P.do("gpsimd", lambda g: g.affine_select(
            out=id_bf, in_=id_bf, compare_op=AluOpType.not_equal, fill=1.0,
            base=0, pattern=[[-1, 128]], channel_multiplier=1),
            waits=[("gp", mib)], inc="gp")
        mih = P.do("gpsimd", lambda g: g.memset(id16, 0.0), inc="gp")
        P.do("gpsimd", lambda g: g.affine_select(
            out=id16, in_=id16, compare_op=AluOpType.not_equal, fill=1.0,
            base=0, pattern=[[-1, 128]], channel_multiplier=1),
            waits=[("gp", mih)], inc="gp")
        P.do("gpsimd", lambda g: g.memset(ones16, 1.0), inc="gp")
        P.do("gpsimd", lambda g: g.memset(ones_bf, 1.0), inc="gp")
        P.do("gpsimd", lambda g: g.memset(eps_t, 1e-5), inc="gp")
        P.do("gpsimd", lambda g: g.memset(mhalf, -0.5), inc="gp")
        P.do("gpsimd", lambda g: g.memset(
            apx(vP, 64, [[c.VW, c.KT], [65, c.HG]]), 1.0), inc="gp")
        m_init = P.do("gpsimd", lambda g: g.memset(oT, 0.0), inc="gp")

        m_bload = []
        for fc in range(c.FC):
            m_bload.append(dma("sync", b1s[:, fc:fc + 1],
                               bass.AP(B1.tensor, fc * 128, [[1, 128], [1, 1]])))
        for dc in range(c.DC):
            m_bload.append(dma("sync", b2s[:, dc:dc + 1],
                               bass.AP(B2.tensor, dc * 128, [[1, 128], [1, 1]])))
        m_bload = wall(m_bload)

        slot_pow = {}
        slot_norm = {}

        def ln_stats(src, slot, wait_src):
            o = slot * 20
            ts_ = min(512, c.D)
            nh = c.D // ts_
            for i in range(nh):
                P.do("vector",
                     lambda v, i=i, o=o, s=src, ts_=ts_: v.bn_stats(
                         stats[:, o + 6 * i:o + 6 * i + 6],
                         s[:, ts_ * i:ts_ * (i + 1)]),
                     waits=list(wait_src) + [("gp", slot_pow.get(slot, 0)),
                                             ("dv", slot_norm.get(slot, 0))]
                     if i == 0 else (),
                     inc="dv" if i == nh - 1 else None)
            m1 = P.cnt["dv"]
            m2 = P.do("vector", lambda v, o=o, n=nh: v.bn_aggr(
                stats[:, o + 12:o + 14], stats[:, o:o + 6 * n]),
                waits=[("dv", m1)], inc="dv")
            m3 = P.do("vector", lambda v, o=o: v.tensor_scalar(
                stats[:, o + 14:o + 15], stats[:, o + 13:o + 14], eps_t[:, 0:1],
                None, AluOpType.add, AluOpType.bypass),
                waits=[("dv", m2), ("gp", slot_pow.get(slot, 0))], inc="dv")
            m4 = P.do("gpsimd", lambda g, o=o: g.tensor_tensor(
                stats[:, o + 15:o + 16], stats[:, o + 14:o + 15], mhalf,
                AluOpType.pow),
                waits=[("dv", m3), ("gp", m_init),
                       ("dv", slot_norm.get(slot, 0))], inc="gp")
            slot_pow[slot] = m4
            return m4

        def ln_norm(src, dst, slot, m4, extra_waits=()):
            o = slot * 20
            m5 = P.do("vector", lambda v, o=o: v.scalar_tensor_tensor(
                stats[:, o + 16:o + 17], stats[:, o + 12:o + 13], -1.0,
                stats[:, o + 15:o + 16], AluOpType.mult, AluOpType.mult),
                waits=[("gp", m4)], inc="dv")
            m6 = P.do("vector", lambda v, o=o, s=src, d=dst: v.tensor_scalar(
                d, s, stats[:, o + 15:o + 16], stats[:, o + 16:o + 17],
                AluOpType.mult, AluOpType.add),
                waits=[("dv", m5)] + list(extra_waits), inc="dv")
            slot_norm[slot] = m6
            return m6

        def transp(src, n, b0, dst_of, wait_first, evac_eng="vector", nb=2,
                   ident=None):
            sem = "dv" if evac_eng == "vector" else "ac"
            idt = id32 if ident is None else ident
            step = 4
            last = None
            g0 = 0
            while g0 < n:
                g1 = min(g0 + step, n)
                bi = b0 + ((g0 // 4) % nb if step == 4 else 0)
                bw = BW[bi] if step == 4 else BW[b0] + BW[b0 + 1]
                for j in range(g0, g1):
                    w = []
                    if j == 0:
                        w += list(wait_first)
                    if j == g0:
                        w += bw
                    P.do("tensor", lambda t, j=j, bi=bi, g0=g0, s=src, idt=idt:
                         (t.transpose(
                             bank(bi + (j - g0) // 4, 128 * ((j - g0) % 4), 128),
                             s[:, 128 * j:128 * (j + 1)], idt)
                          if idt is id32 else
                          t.matmul(
                             bank(bi + (j - g0) // 4, 128 * ((j - g0) % 4), 128),
                             s[:, 128 * j:128 * (j + 1)], idt,
                             start=True, stop=True)),
                         waits=w, inc="pe" if j == g1 - 1 else None)
                mpe = P.cnt["pe"]
                cn = g1 - g0
                if evac_eng == "vector":
                    last = P.do("vector", lambda v, bi=bi, g0=g0, cn=cn: v.tensor_copy(
                        dst_of(g0, cn), bank(bi, 0, 128 * cn)),
                        waits=[("pe", mpe)], inc="dv")
                else:
                    last = P.do("scalar", lambda e, bi=bi, g0=g0, cn=cn: e.activation(
                        dst_of(g0, cn), bank(bi, 0, 128 * cn), AF.Copy),
                        waits=[("pe", mpe)], inc="ac")
                if step == 4:
                    BW[bi] = [(sem, last)]
                else:
                    BW[b0] = [(sem, last)]
                    BW[b0 + 1] = [(sem, last)]
                g0 = g1
            return (sem, last), P.cnt["pe"]

        st_blk = {"xw": [[] for _ in range(4)],
                  "l16w": [[] for _ in range(4)],
                  "lnxT_users": [],
                  "m_lnxT_sem": "dv", "evac_eng": "vector"}

        def ln1_part1(t0_base):
            nsub = c.TBLK // 128
            pows, loads = [], []
            for s in range(nsub):
                mx = dma("sync", xs[:, s * c.D:(s + 1) * c.D],
                         X[t0_base + s * 128: t0_base + s * 128 + 128, :],
                         waits=st_blk["xw"][s])
                loads.append(mx)
            for s in range(nsub):
                pows.append(ln_stats(xs[:, s * c.D:(s + 1) * c.D], s, [loads[s]]))
            norms = []
            for s in range(nsub):
                xsl = xs[:, s * c.D:(s + 1) * c.D]
                l16 = lnx16[:, s * c.D:(s + 1) * c.D]
                mn = ln_norm(xsl, l16, s, pows[s],
                             extra_waits=st_blk["l16w"][s])
                st_blk["xw"][s] = [("dv", mn)]
                norms.append(mn)
            return norms

        def ln1_part2(norms, b0=0):
            nsub = c.TBLK // 128
            mlast = None
            for s in range(nsub):
                l16 = lnx16[:, s * c.D:(s + 1) * c.D]
                def dst(j0, cn, s=s):
                    return apx(lnxT, j0 * c.TBLK + s * 128, [[c.TBLK, cn], [1, 128]])
                (semn, mev), mtp = transp(
                    l16, c.DC, b0, dst,
                    wait_first=[("dv", norms[s])]
                    + [("pe", u) for u in st_blk["lnxT_users"][-1:]],
                    evac_eng=st_blk["evac_eng"], ident=id_bf)
                st_blk["l16w"][s] = [("pe", mtp)]
                st_blk["m_lnxT_sem"] = semn
                mlast = mev
            return mlast

        war_wkv = {"v": []}
        war_kT = {"v": []}
        war_vP = {"v": []}
        m_kv_done = {}

        def phase_A(g):
            W2G = c.HG * 128
            m_w = []
            for dc in range(c.DC):
                m_w.append(dma("gpsimd", wkv[:, dc * W2G: dc * W2G + c.HG * 64],
                    Wqkv[dc * 128:(dc + 1) * 128,
                         c.D + g * c.HG * 64: c.D + (g + 1) * c.HG * 64],
                    waits=(war_wkv["v"] + [("gp", m_init)]) if dc == 0 else ()))
                m_w.append(dma("gpsimd", wkv[:, dc * W2G + c.HG * 64: (dc + 1) * W2G],
                    Wqkv[dc * 128:(dc + 1) * 128,
                         2 * c.D + g * c.HG * 64: 2 * c.D + (g + 1) * c.HG * 64]))
            m_w = wall(m_w)
            mk0 = P.do("vector", lambda v: v.memset(qmax4[:, 7:8], 0.0),
                       waits=war_kT["v"], inc="dv")
            kmax_run = mk0

            nmk = c.HG * 128 // 128
            nK = nmk // 2
            BW_vtmp[:] = [("dv", P.cnt["dv"]), ("pe", P.cnt["pe"])]
            st_blk["evac_eng"] = "scalar"
            norms = ln1_part1(0)
            for tb in range(c.NTB):
                m_lnxT = ln1_part2(norms)
                if tb + 1 < c.NTB:
                    norms = ln1_part1((tb + 1) * c.TBLK)
                for mc in range(nmk):
                    pb = 2 + (mc % 2)
                    is_v = mc >= nK
                    for dc in range(c.DC):
                        w = (BW[pb] + m_w
                             + [(st_blk["m_lnxT_sem"], m_lnxT)]) if dc == 0 else ()
                        P.do("tensor", lambda t, dc=dc, mc=mc, pb=pb: t.matmul(
                            bank(pb),
                            wkv[:, dc * W2G + mc * 128: dc * W2G + (mc + 1) * 128],
                            lnxT[:, dc * c.TBLK:(dc + 1) * c.TBLK],
                            start=(dc == 0), stop=(dc == c.DC - 1)),
                            waits=w, inc="pe" if dc == c.DC - 1 else None)
                    m_mm = P.cnt["pe"]
                    st_blk["lnxT_users"] = [m_mm]
                    if not is_v:
                        mea = P.do("scalar", lambda e, mc=mc, tb=tb, pb=pb: e.activation(
                            kT[:, mc * c.T + tb * c.TBLK: mc * c.T + (tb + 1) * c.TBLK],
                            bank(pb), AF.Copy),
                            waits=[("pe", m_mm)] + war_kT["v"], inc="ac")
                        mr = P.do("vector", lambda v, mc=mc, tb=tb: v.tensor_reduce(
                            qmax4[:, 6:7],
                            kT[:, mc * c.T + tb * c.TBLK: mc * c.T + (tb + 1) * c.TBLK],
                            axis=AX.X, op=AluOpType.max,
                            apply_absolute_value=True),
                            waits=[("ac", mea), ("dv", kmax_run)], inc="dv")
                        kmax_run = P.do("vector", lambda v: v.tensor_tensor(
                            qmax4[:, 7:8], qmax4[:, 7:8], qmax4[:, 6:7],
                            AluOpType.max), waits=[("dv", mr)], inc="dv")
                        BW[pb] = [("ac", mea)]
                    else:
                        mev = P.do("scalar", lambda e, pb=pb: e.activation(
                            vtmp16, bank(pb), AF.Copy),
                            waits=[("pe", m_mm), BW_vtmp[0]] if BW_vtmp else [("pe", m_mm)],
                            inc="ac")
                        BW[pb] = [("ac", mev)]
                        mcv = mc - nK
                        vb = 4 + (mcv % 2)
                        nsub = c.TBLK // 128
                        for j in range(nsub):
                            w = ([("ac", mev)] + BW[vb]) if j == 0 else ()
                            P.do("tensor", lambda t, j=j, vb=vb: t.matmul(
                                bank(vb, 128 * j, 128),
                                vtmp16[:, 128 * j:128 * (j + 1)], id_bf,
                                start=True, stop=True),
                                waits=w, inc="pe" if j == nsub - 1 else None)
                        m_vt = P.cnt["pe"]
                        BW_vtmp[:] = [("pe", m_vt)]
                        mlv = None
                        for h2 in range(2):
                            hl = 2 * mcv + h2
                            mlv = P.do("vector", lambda v, h2=h2, hl=hl, tb=tb,
                                       nsub=nsub, vb=vb:
                                       v.tensor_copy(
                                           apx(vP, tb * nsub * c.VW + hl * 65,
                                               [[c.VW, nsub], [1, 64]]),
                                           apx(ps, 512 * vb + h2 * 64,
                                               [[128, nsub], [1, 64]])),
                                       waits=([("pe", m_vt)] + war_vP["v"]) if h2 == 0 else (),
                                       inc="dv")
                        BW[vb] = [("dv", mlv)]
            war_wq["v"] = [("pe", P.cnt["pe"])]
            w5 = BW[5]
            P.do("tensor", lambda t: t.transpose(bank(5, 0, 128)[0:1, :],
                                                 qmax4[:, 7:8], id32),
                 waits=[("dv", kmax_run)] + w5, inc="pe")
            mt = P.cnt["pe"]
            mk = P.do("vector", lambda v, g=g: v.tensor_reduce(
                kmaxs[:, g:g + 1], bank(5)[0:1, 0:128], axis=AX.X,
                op=AluOpType.max), waits=[("pe", mt)], inc="dv")
            BW[5] = [("dv", mk)]
            m_kv_done[g] = mk

        BW_vtmp = []

        war_qT = {"v": []}
        war_oT = {"v": []}
        war_wq = {"v": []}
        war_otmp = {0: [], 1: []}
        war_recl = {0: [], 1: []}
        war_recl16 = {"v": []}
        tailstate = {"fn": None}
        prolstate = {"fn": None, "m_qT": 0}
        bgq = []
        bgq_meta = {"ob0": -1}
        pvstate = {"m_pv_0": 0, "m_pv_1": 0}
        ringstate = {"exp": {}, "pv": {}, "i": 0, "pgc": 0}

        def phase_B(g):
            m_w = []
            for dc in range(c.DC):
                m_w.append(dma("gpsimd", wq[:, dc * c.HG * 64:(dc + 1) * c.HG * 64],
                    Wqkv[dc * 128:(dc + 1) * 128,
                         g * c.HG * 64:(g + 1) * c.HG * 64],
                    waits=war_wq["v"] if dc == 0 else ()))
            for hp in range(c.HG // 2):
                m_w.append(dma("gpsimd", wo[:, hp * c.D:(hp + 1) * c.D],
                    Wout[(g * c.HG + 2 * hp) * 64:
                         (g * c.HG + 2 * hp) * 64 + 128, :],
                    waits=([("gp", m_init)] + war_wq["v"]) if hp == 0 else ()))
            m_w = wall(m_w)
            m_pw = [("pe", P.cnt["pe"])]
            m_sq = P.do("scalar", lambda e: e.activation(
                lnxT[:, 0:c.DC * c.HG * 64], wq, AF.Square),
                waits=m_w + m_pw + [("ac", P.cnt["ac"])], inc="ac")
            NQ = c.HG * 64
            for dc in range(c.DC):
                w = ([("ac", m_sq)] + BW[0]) if dc == 0 else ()
                P.do("tensor", lambda t, dc=dc: t.matmul(
                    bank(0, 0, NQ)[0:1, :], ones_bf,
                    lnxT[:, dc * NQ:(dc + 1) * NQ],
                    start=(dc == 0), stop=(dc == c.DC - 1)),
                    waits=w, inc="pe" if dc == c.DC - 1 else None)
            m_cn = P.cnt["pe"]
            st_blk["lnxT_users"] = [m_cn]
            pvstate["pT_free"] = 0
            mr2 = P.do("vector", lambda v: v.tensor_reduce(
                sc1[:, 2:3], bank(0, 0, NQ)[0:1, :], axis=AX.X,
                op=AluOpType.max), waits=[("pe", m_cn)], inc="dv")
            BW[0] = [("dv", mr2)]
            ml1 = P.do("scalar", lambda e: e.activation(
                sc1[:, 3:4], sc1[:, 2:3], AF.Ln), waits=[("dv", mr2)], inc="ac")
            ml2 = P.do("scalar", lambda e: e.activation(
                sc1[:, 3:4], sc1[:, 3:4], AF.Exp, scale=0.5),
                waits=[("ac", ml1)], inc="ac")
            mr3 = P.do("vector", lambda v, g=g: v.scalar_tensor_tensor(
                sc1[:, 1:2], sc1[:, 3:4],
                -float(c.DH) * SC * math.sqrt(c.D), kmaxs[:, g:g + 1],
                AluOpType.mult, AluOpType.mult),
                waits=[("ac", ml2), ("dv", m_kv_done[g])], inc="dv")
            mr4 = P.do("vector", lambda v: v.tensor_copy(
                recl16[0:1, 512:513], sc1[:, 1:2]),
                waits=[("dv", mr3)] + war_recl16["v"], inc="dv")
            w1b = BW[1]
            P.do("tensor", lambda t: t.matmul(
                bank(1, 0, 1), ones16, recl16[0:1, 512:513],
                start=True, stop=True),
                waits=[("dv", mr4)] + w1b, inc="pe")
            mbc = P.cnt["pe"]
            m_negB = P.do("vector", lambda v: v.tensor_copy(
                negBv, bank(1, 0, 1)),
                waits=[("pe", mbc), ("ac", P.cnt["ac"])], inc="dv")
            BW[1] = BW[1] + [("dv", m_negB)]

            def emit_prologueB(qb, norms, g=None, pbanks=(2, 3)):
                m_lnxT = ln1_part2(norms, b0=pbanks[0])
                mq = 0
                qsl = (qb % 2) * c.CH * c.TBLK
                for mc in range(c.CH):
                    pb = pbanks[mc % 2]
                    for dc in range(c.DC):
                        w = (BW[pb] + m_w + [("dv", m_lnxT)]) if dc == 0 else ()
                        P.do("tensor", lambda t, dc=dc, mc=mc, pb=pb: t.matmul(
                            bank(pb),
                            wq[:, dc * c.HG * 64 + mc * 128: dc * c.HG * 64 + (mc + 1) * 128],
                            lnxT[:, dc * c.TBLK:(dc + 1) * c.TBLK],
                            start=(dc == 0), stop=(dc == c.DC - 1)),
                            waits=w, inc="pe" if dc == c.DC - 1 else None)
                    m_mm = P.cnt["pe"]
                    st_blk["lnxT_users"] = [m_mm]
                    if mc % 2 == 1 or mc == c.CH - 1:
                        nmc = 2 if mc % 2 == 1 else 1
                        mc0 = mc - nmc + 1
                        me = P.do("vector", lambda v, mc0=mc0, nmc=nmc, qsl=qsl:
                                  v.tensor_copy(
                                      qT[:, qsl + mc0 * c.TBLK:
                                         qsl + (mc0 + nmc) * c.TBLK],
                                      bank(pbanks[0] if mc0 % 2 == 0 else pbanks[1],
                                           0, nmc * 512)),
                                  waits=[("pe", m_mm)] + war_qT["v"], inc="dv")
                        mq = me
                        BW[pbanks[0]] = [("dv", me)]
                        BW[pbanks[1]] = [("dv", me)]
                return mq

            def emit_outproj(qb, g, ob0, spread=True):
                m_oT = P.cnt["dv"]
                nsub = c.TBLK // 128
                OW = min(512, c.D)
                ops = []
                for tc in range(nsub):
                    for oh in range(c.D // OW):
                        pb = ob0 + (tc * (c.D // OW) + oh) % 2
                        grp = []
                        for hl in range(c.HG // 2):
                            first = hl == 0
                            last = hl == c.HG // 2 - 1

                            def mm(tc=tc, oh=oh, hl=hl, pb=pb, first=first,
                                   last=last, qb=qb, g=g):
                                w = ([("dv", m_oT)] + m_w + BW[pb]) if first else ()
                                P.do("tensor", lambda t: t.matmul(
                                    bank(pb, 0, OW),
                                    oT[:, hl * c.TBLK + tc * 128:
                                       hl * c.TBLK + (tc + 1) * 128],
                                    wo[:, hl * c.D + oh * OW:
                                       hl * c.D + (oh + 1) * OW],
                                    start=first, stop=last),
                                    waits=w, inc="pe" if last else None)
                                if last:
                                    m_mm = P.cnt["pe"]
                                    war_oT["v"] = [("pe", m_mm)]
                                    dsty = y_acc[:, (qb * nsub + tc) * c.D + oh * OW:
                                                 (qb * nsub + tc) * c.D + (oh + 1) * OW]
                                    if g == 0:
                                        mev = P.do("vector",
                                                   lambda v, d=dsty, pb=pb: v.tensor_copy(
                                                       d, bank(pb, 0, OW)),
                                                   waits=[("pe", m_mm)], inc="dv")
                                    else:
                                        mev = P.do("vector",
                                                   lambda v, d=dsty, pb=pb: v.tensor_tensor(
                                                       d, d, bank(pb, 0, OW),
                                                       AluOpType.add),
                                                   waits=[("pe", m_mm),
                                                          ("ac", m_yg0[0])], inc="dv")
                                    BW[pb] = [("dv", mev)]
                            grp.append(mm)
                        ops.append(lambda grp=grp: [f() for f in grp])
                if spread:
                    bgq_meta["ob0"] = ob0
                    bgq.extend(ops)
                else:
                    for op in ops:
                        op()

            st_blk["evac_eng"] = "vector"
            for _h in range(2):            st_blk["evac_eng"] = "vector"
            for _h in range(2):            st_blk["evac_eng"] = "vector"
            for _h in range(2):            st_blk["evac_eng"] = "vector"
            for _h in range(2):
                war_otmp[_h] = war_otmp[_h] + [("ac", P.cnt["ac"])]
                war_recl[_h] = war_recl[_h] + [("ac", P.cnt["ac"])]
            normsB = ln1_part1(0)
            prol = emit_prologueB(0, normsB, g=g)
            for qb in range(c.NTB):
                m_qT = prol
                if qb + 1 < c.NTB:
                    normsB = ln1_part1((qb + 1) * c.TBLK)
                    if c.CH >= 3:
                        def mk_prol(qb=qb, g=g, nb_=normsB):
                            def fn():
                                prolstate["m_qT"] = emit_prologueB(
                                    qb + 1, nb_, g=g, pbanks=(4, 5))
                            return fn
                        prolstate["fn"] = mk_prol()
                def emit_epilogue(p, obase, m_pv):
                    mults = []
                    for half in range(2):
                        ob = obase + half
                        mo = P.do("vector", lambda v, ob=ob, half=half:
                                  v.tensor_copy(
                                      otmp[:, half * 512:(half + 1) * 512],
                                      bank(ob)[0:64, :]),
                                  waits=[("pe", m_pv)]
                                  + war_otmp[half], inc="dv")
                        mrc = P.do("vector", lambda v, ob=ob, half=half:
                                   v.reciprocal(
                                       recl[0:1, half * 512:(half + 1) * 512],
                                       bank(ob)[64:65, :]),
                                   waits=[("pe", m_pv),
                                          ("dv", mo)] + war_recl[half], inc="dv")
                        mrc2 = P.do("vector", lambda v, half=half: v.tensor_copy(
                            recl16[0:1, 0:512],
                            recl[0:1, half * 512:(half + 1) * 512]),
                            waits=[("dv", mrc)] + war_recl16["v"], inc="dv")
                        P.do("tensor", lambda t, ob=ob: t.matmul(
                            bank(ob), ones16,
                            recl16[0:1, 0:512],
                            start=True, stop=True),
                            waits=[("dv", mrc2)], inc="pe")
                        mb = P.cnt["pe"]
                        war_recl16["v"] = [("pe", mb)]
                        mm_ = P.do("vector", lambda v, ob=ob, half=half, p=p:
                                   v.tensor_tensor(
                                       oT[64 * half:64 * half + 64,
                                          p * c.TBLK:(p + 1) * c.TBLK],
                                       otmp[:, half * 512:(half + 1) * 512],
                                       bank(ob)[0:64, :], AluOpType.mult),
                                   waits=[("pe", mb)] + war_oT["v"], inc="dv")
                        BW[ob] = [("dv", mm_)]
                        war_otmp[half] = [("dv", mm_)]
                        war_recl[half] = [("pe", mb)]
                        mults.append(mm_)
                    return mults

                pending_epi = None
                EH, PH = ringstate["exp"], ringstate["pv"]
                qsl = (qb % 2) * c.CH * c.TBLK
                for p in range(c.CH):
                    hlA, hlB = 2 * p, 2 * p + 1
                    obase = 6 if (ringstate["pgc"] % 2 == 0) else 4
                    ringstate["pgc"] += 1
                    base = ringstate["i"]
                    if obase == bgq_meta["ob0"]:
                        while bgq:
                            bgq.pop(0)()

                    def emit_qk(t_, p=p, base=base, qsl=qsl):
                        gi = base + t_
                        sp = gi % 2
                        for half in range(2):
                            rb = 2 * sp + half
                            r0 = 64 * half
                            w = []
                            if half == 0:
                                w = [("ac", EH.get(gi - 2, 0))]
                                if t_ == 0:
                                    w += [("dv", m_qT)] + BW[0] + BW[1]
                                elif t_ == 1:
                                    w += BW[2] + BW[3]
                            P.do("tensor",
                                 lambda t, t_=t_, p=p, r0=r0, rb=rb, qsl=qsl: t.matmul(
                                     bank(rb),
                                     kT[r0:r0 + 64,
                                        p * c.T + t_ * 128: p * c.T + (t_ + 1) * 128],
                                     qT[r0:r0 + 64,
                                        qsl + p * c.TBLK: qsl + (p + 1) * c.TBLK],
                                     start=True, stop=True),
                                 waits=w, inc="pe" if half == 1 else None)
                        EH[("qk", gi)] = P.cnt["pe"]

                    def emit_exp(t_, base=base):
                        gi = base + t_
                        sp = gi % 2
                        sl4 = gi % 4
                        EH[gi] = P.do(
                            "scalar", lambda e, sp=sp, sl4=sl4: e.activation(
                                pT[:, sl4 * 1024: (sl4 + 1) * 1024],
                                bank(2 * sp, 0, 1024),
                                AF.Exp, scale=SC, bias=negBv[:, 0:1]),
                            waits=[("pe", EH[("qk", gi)]),
                                   ("pe", PH.get(gi - 4, 0)),
                                   ("pe", pvstate.get("pT_free", 0)),
                                   ("dv", m_negB)], inc="ac")

                    def emit_pv(t_, p=p, obase=obase, base=base):
                        gi = base + t_
                        sl4 = gi % 4
                        hls = (2 * p, 2 * p + 1)
                        for half in range(2):
                            ob = obase + half
                            hl = hls[half]
                            w = []
                            if half == 0:
                                w = [("ac", EH[gi])]
                                if t_ == 0:
                                    w += BW[obase] + BW[obase + 1]
                            P.do("tensor",
                                 lambda t, t_=t_, hl=hl, sl4=sl4, ob=ob, half=half:
                                 t.matmul(
                                     bank(ob, 0, 512)[0:65, :],
                                     vP[:, t_ * c.VW + hl * 65:
                                        t_ * c.VW + hl * 65 + 65],
                                     pT[:, sl4 * 1024 + half * 512:
                                        sl4 * 1024 + (half + 1) * 512],
                                     start=(t_ == 0), stop=(t_ == c.KT - 1)),
                                 waits=w, inc="pe" if half == 1 else None)
                        PH[gi] = P.cnt["pe"]
                        if (bgq and obase != bgq_meta["ob0"]
                                and t_ % 4 == 0):
                            bgq.pop(0)()

                    for u in range(c.KT // 2):
                        emit_qk(2 * u)
                        emit_exp(2 * u)
                        emit_qk(2 * u + 1)
                        if u > 0:
                            emit_pv(2 * u - 2)
                        emit_exp(2 * u + 1)
                        if u > 0:
                            emit_pv(2 * u - 1)
                        if (u == min(2, c.KT // 2 - 1)
                                and pending_epi is not None):
                            pending_epi()
                            pending_epi = None
                        if (u == min(4, c.KT // 2 - 1) and p == 0
                                and tailstate["fn"] is not None):
                            tailstate["fn"]()
                            tailstate["fn"] = None
                        if (u == min(3, c.KT // 2 - 1) and p == 2
                                and prolstate["fn"] is not None):
                            prolstate["fn"]()
                            prolstate["fn"] = None
                    emit_pv(c.KT - 2)
                    emit_pv(c.KT - 1)
                    ringstate["i"] += c.KT
                    pending_epi = (lambda p=p, obase=obase, m=P.cnt["pe"]:
                                   emit_epilogue(p, obase, m))
                while bgq:
                    bgq.pop(0)()
                war_qT["v"] = [("pe", P.cnt["pe"])]
                mac_all = P.cnt["ac"]
                for b_ in range(4):
                    BW[b_] = [("ac", mac_all)]
                if qb + 1 < c.NTB:
                    if c.CH >= 3:
                        assert prolstate["fn"] is None
                        prol = prolstate["m_qT"]
                    else:
                        prol = emit_prologueB(qb + 1, normsB, g=g)
                    ob0_now = 6 if ((ringstate["pgc"] - 1) % 2 == 0) else 4
                    def mk_tail(qb=qb, g=g, ep=pending_epi, ob0=ob0_now):
                        def fn():
                            if ep is not None:
                                ep()
                            emit_outproj(qb, g, ob0)
                        return fn
                    tailstate["fn"] = mk_tail()
                    pending_epi = None
                else:
                    assert tailstate["fn"] is None
                    if pending_epi is not None:
                        pending_epi()
                        pending_epi = None
                    emit_outproj(qb, g,
                                 6 if ((ringstate["pgc"] - 1) % 2 == 0) else 4,
                                 spread=False)
            war_wkv["v"] = [("pe", P.cnt["pe"])]
            war_kT["v"] = [("pe", P.cnt["pe"])]
            war_vP["v"] = [("pe", P.cnt["pe"])]
            war_qT["v"] = [("pe", P.cnt["pe"])]

        def phase3_prologue(M_END, tz):
            s1T, s2T, wch, wch2, sgn = (tz["s1T"], tz["s2T"], tz["wch"],
                                        tz["wch2"], tz["sgn"])
            war_wch = {0: list(BW[0]) + list(BW[1]) + M_END, 1: list(M_END)}
            war_sgn = {0: list(M_END), 1: list(M_END)}
            sgn_chain = {"v": list(M_END)}
            war_red = []
            ms = {}
            for fc in range(c.FC):
                sl = fc % 2
                WC = wch[:, sl * c.D:(sl + 1) * c.D]
                SG = sgn[:, 0:c.D]
                mw = dma("sync", WC, W1[fc * 128:(fc + 1) * 128, :],
                         waits=war_wch[sl])
                mrd = P.do("vector", lambda v, WC=WC: v.tensor_reduce(
                    stats[:, 18:19], WC, axis=AX.X, op=AluOpType.add,
                    apply_absolute_value=True), waits=[mw] + war_red, inc="dv")
                mal = P.do("vector", lambda v, fc=fc: v.tensor_scalar(
                    alpha1[:, fc:fc + 1], stats[:, 18:19], 1.0 / c.D, None,
                    AluOpType.mult, AluOpType.bypass), waits=[("dv", mrd)], inc="dv")
                war_red = [("dv", mal)]
                msg = P.do("scalar", lambda e, WC=WC, SG=SG: e.activation(
                    SG, WC, AF.Sign),
                    waits=[mw] + sgn_chain["v"], inc="ac")
                war_wch[sl] = [("dv", mrd), ("ac", msg)]

                def dst(j0, cn, fc=fc):
                    return apx(s1T, j0 * c.F + fc * 128, [[c.F, cn], [1, 128]])
                (semn, mev), mtp = transp(SG, c.DC, 0 if sl == 0 else 2, dst,
                                          wait_first=[("ac", msg)], ident=id_bf)
                sgn_chain["v"] = [("pe", mtp)]
            ms["s1T"] = P.cnt["dv"]
            war_wch = {0: war_wch[0] + war_wch[1], 1: war_wch[0] + war_wch[1]}
            for dc in range(c.DC):
                sl = dc % 2
                WC = wch2[:, sl * c.F:(sl + 1) * c.F]
                SG = sgn[:, 0:c.F]
                mw = dma("sync", WC, W2[dc * 128:(dc + 1) * 128, :],
                         waits=war_wch[sl])
                mrd = P.do("vector", lambda v, WC=WC: v.tensor_reduce(
                    stats[:, 18:19], WC, axis=AX.X, op=AluOpType.add,
                    apply_absolute_value=True), waits=[mw] + war_red, inc="dv")
                mal = P.do("vector", lambda v, dc=dc: v.tensor_scalar(
                    alpha2[:, dc:dc + 1], stats[:, 18:19], 1.0 / c.F, None,
                    AluOpType.mult, AluOpType.bypass), waits=[("dv", mrd)], inc="dv")
                war_red = [("dv", mal)]
                msg = P.do("scalar", lambda e, WC=WC, SG=SG: e.activation(
                    SG, WC, AF.Sign),
                    waits=[mw] + sgn_chain["v"], inc="ac")
                war_wch[sl] = [("dv", mrd), ("ac", msg)]

                def dst(j0, cn, dc=dc):
                    return apx(s2T, j0 * c.D + dc * 128, [[c.D, cn], [1, 128]])
                (semn, mev), mtp = transp(SG, c.FC, 0 if sl == 0 else 2, dst,
                                          wait_first=[("ac", msg)], ident=id_bf)
                sgn_chain["v"] = [("pe", mtp)]
            ms["s2T"] = P.cnt["dv"]
            ms["alpha"] = P.cnt["dv"]
            return ms

        def phase3(M_END, ms, tz):
            s1T, s2T, hT = tz["s1T"], tz["s2T"], tz["hT"]
            x3, x1, ln3T = tz["x3"], tz["x1"], tz["ln3T"]
            ln3f, ln3b = tz["ln3f"], tz["ln3b"]
            o2tm, oscr = tz["o2tm"], tz["oscr"]
            out_dmas = phase3.out_dmas = []
            m_y_done = tz["m_y_done"]
            MP = tz["m_prol"]
            war_x3 = {0: list(M_END) + MP, 1: list(M_END) + MP}
            war_x1 = {0: MP, 1: MP}
            war_ln3f = {k: MP for k in range(2 * (c.TB3 // 128))}
            war_ln3b = {k: MP for k in range(2 * (c.TB3 // 128))}
            war_ln3T = {0: MP, 1: MP}
            war_hT = {"v": []}
            war_o2 = []
            nsub = c.TB3 // 128
            DCT = c.DC * c.TB3

            def front(tb):
                t0 = tb * c.TB3
                sl = tb % 2
                X3 = lambda s: x3[:, (sl * nsub + s) * c.D:(sl * nsub + s + 1) * c.D]
                X1 = lambda s: x1[:, (sl * nsub + s) * c.D:(sl * nsub + s + 1) * c.D]
                L3s = lambda s: ln3f[:, (sl * nsub + s) * c.D:(sl * nsub + s + 1) * c.D]
                mx = []
                for s in range(nsub):
                    mx.append(dma("sync", X3(s),
                                  X[t0 + s * 128: t0 + 128 * (s + 1), :],
                                  waits=war_x3[sl] if s == 0 else ()))
                pows = []
                for s in range(nsub):
                    ysl = y_acc[:, (t0 // 128 + s) * c.D:(t0 // 128 + s + 1) * c.D]
                    pows.append(ln_stats(ysl, sl * 2 + s,
                                         m_y_done if tb == 0 else []))
                m_x1 = []
                for s in range(nsub):
                    ysl = y_acc[:, (t0 // 128 + s) * c.D:(t0 // 128 + s + 1) * c.D]
                    mn = ln_norm(ysl, L3s(s), sl * 2 + s, pows[s],
                                 extra_waits=war_ln3f[sl * nsub + s])
                    ma = P.do("vector", lambda v, s=s, X3=X3, X1=X1, L3s=L3s:
                              v.tensor_tensor(X1(s), X3(s), L3s(s), AluOpType.add),
                              waits=[("dv", mn), mx[s]]
                              + (war_x1[sl] if s == 0 else []), inc="dv")
                    war_ln3f[sl * nsub + s] = [("dv", ma)]
                    m_x1.append(ma)
                pows = []
                for s in range(nsub):
                    pows.append(ln_stats(X1(s), sl * 2 + s, [("dv", m_x1[s])]))
                norms = []
                for s in range(nsub):
                    L3 = ln3b[:, (sl * nsub + s) * c.D:(sl * nsub + s + 1) * c.D]
                    mn = ln_norm(X1(s), L3, sl * 2 + s, pows[s],
                                 extra_waits=war_ln3b[sl * nsub + s])
                    norms.append(mn)
                return norms

            def emit_ln3T(tb, norms):
                sl = tb % 2
                m_ln3 = None
                for s_ in range(nsub):
                    L3 = ln3b[:, (sl * nsub + s_) * c.D:(sl * nsub + s_ + 1) * c.D]

                    def dst(j0, cn, s_=s_, sl=sl):
                        return apx(ln3T, sl * DCT + j0 * c.TB3 + s_ * 128,
                                   [[c.TB3, cn], [1, 128]])
                    (semn, mev), mtp = transp(L3, c.DC, 0, dst,
                                              wait_first=[("dv", norms[s_])]
                                              + war_ln3T[sl], ident=id_bf)
                    war_ln3b[sl * nsub + s_] = [("pe", mtp)]
                    m_ln3 = mev
                return m_ln3

            f_norms = front(0)
            pend_t = {"norms": f_norms, "tb": 0}
            for tb in range(c.NT3):
                t0 = tb * c.TB3
                sl = tb % 2
                X3 = lambda s, sl=sl: x3[:, (sl * nsub + s) * c.D:
                                         (sl * nsub + s + 1) * c.D]
                X1 = lambda s, sl=sl: x1[:, (sl * nsub + s) * c.D:
                                         (sl * nsub + s + 1) * c.D]
                L3s = lambda s, sl=sl: ln3f[:, (sl * nsub + s) * c.D:
                                            (sl * nsub + s + 1) * c.D]
                if pend_t["tb"] == tb:
                    m_ln3 = emit_ln3T(tb, pend_t["norms"])
                else:
                    m_ln3 = pend_t["m_ln3"]
                for fc in range(c.FC):
                    pb = 2 + (fc % 2)
                    for dc in range(c.DC):
                        w = (BW[pb] + [("dv", m_ln3),
                                       ("dv", ms["s1T"])]) if dc == 0 else ()
                        P.do("tensor", lambda t, fc=fc, dc=dc, pb=pb, sl=sl:
                             t.matmul(
                                 bank(pb, 0, c.TB3),
                                 s1T[:, dc * c.F + fc * 128: dc * c.F + (fc + 1) * 128],
                                 ln3T[:, sl * DCT + dc * c.TB3:
                                      sl * DCT + (dc + 1) * c.TB3],
                                 start=(dc == 0), stop=(dc == c.DC - 1)),
                             waits=w, inc="pe" if dc == c.DC - 1 else None)
                    m_mm = P.cnt["pe"]
                    war_ln3T[sl] = [("pe", m_mm)]
                    mg = P.do("scalar", lambda e, fc=fc, pb=pb: e.activation(
                        hT[:, fc * c.TB3:(fc + 1) * c.TB3], bank(pb, 0, c.TB3),
                        AF.Gelu, scale=alpha1[:, fc:fc + 1], bias=b1s[:, fc:fc + 1]),
                        waits=[("pe", m_mm), ("dv", ms["alpha"])] + m_bload
                        + war_hT["v"], inc="ac")
                    BW[pb] = [("ac", mg)]
                m_h = P.cnt["ac"]
                if tb + 1 < c.NT3:
                    nxt = front(tb + 1)
                    pend_t = {"norms": nxt, "tb": tb + 1,
                              "m_ln3": emit_ln3T(tb + 1, nxt)}
                for dc in range(c.DC):
                    pb = 4 + (dc % 2)
                    for fc in range(c.FC):
                        w = (BW[pb] + [("ac", m_h), ("dv", ms["s2T"])]) if fc == 0 else ()
                        P.do("tensor", lambda t, fc=fc, dc=dc, pb=pb: t.matmul(
                            bank(pb, 0, c.TB3),
                            s2T[:, fc * c.D + dc * 128: fc * c.D + (dc + 1) * 128],
                            hT[:, fc * c.TB3:(fc + 1) * c.TB3],
                            start=(fc == 0), stop=(fc == c.FC - 1)),
                            waits=w, inc="pe" if fc == c.FC - 1 else None)
                    m_mm = P.cnt["pe"]
                    war_hT["v"] = [("pe", m_mm)]
                    mo = P.do("scalar", lambda e, dc=dc, pb=pb: e.activation(
                        oscr, bank(pb, 0, c.TB3), AF.Identity,
                        scale=alpha2[:, dc:dc + 1], bias=b2s[:, dc:dc + 1]),
                        waits=[("pe", m_mm), ("dv", ms["alpha"])] + m_bload
                        + war_o2, inc="ac")
                    BW[pb] = [("ac", mo)]

                    def dst(j0, cn, dc=dc):
                        return apx(o2tm, j0 * c.D + dc * 128, [[c.D, cn], [1, 128]])
                    (semn, mev), mtp = transp(oscr, nsub, 6, dst,
                                              wait_first=[("ac", mo)], ident=id16)
                    war_o2 = [("pe", mtp)]
                    m_o2 = mev
                pows = []
                for s in range(nsub):
                    pows.append(ln_stats(o2tm[:, s * c.D:(s + 1) * c.D],
                                         sl * 2 + s, [("dv", m_o2)]))
                for s in range(nsub):
                    mn = ln_norm(o2tm[:, s * c.D:(s + 1) * c.D], L3s(s),
                                 sl * 2 + s, pows[s],
                                 extra_waits=war_ln3f[sl * nsub + s])
                    ma = P.do("vector", lambda v, s=s, X3=X3, X1=X1, L3s=L3s:
                              v.tensor_tensor(X3(s), X1(s), L3s(s), AluOpType.add),
                              waits=[("dv", mn)], inc="dv")
                    war_ln3f[sl * nsub + s] = [("dv", ma)]
                    mo_ = dma("sync", OUT[t0 + s * 128: t0 + (s + 1) * 128, :],
                              X3(s), waits=[("dv", ma)])
                    out_dmas.append(mo_)
                war_x3[sl] = wall(out_dmas[-nsub:])
                war_x1[sl] = [("dv", P.cnt["dv"])]

        m_yg0 = [0]
        phase_A(0)
        phase_B(0)
        m_yg0[0] = P.cnt["ac"]
        phase_A(1)
        phase_B(1)
        M_END = [("pe", P.cnt["pe"]), ("dv", P.cnt["dv"]), ("ac", P.cnt["ac"]),
                 ("gp", P.cnt["gp"])]
        m_y_done = [("dv", P.cnt["dv"]), ("ac", P.cnt["ac"])]
        st2.close()
        s1T = sb("s1T", [128, c.DC * c.F], FP8)
        s2T = sb("s2T", [128, c.FC * c.D], FP8)
        hT = sb("hT", [128, c.FC * c.TB3], BF16)
        o2tm = sb("o2tm", [128, (c.TB3 // 128) * c.D], FP16)
        oscr = sb("oscr", [128, c.TB3], FP16)
        st3 = ExitStack()
        wch = st3.enter_context(nc.sbuf_tensor("wch", [128, 2 * c.D], F32))[:, :]
        wch2 = st3.enter_context(nc.sbuf_tensor("wch2", [128, 2 * c.F], F32))[:, :]
        sgn = st3.enter_context(
            nc.sbuf_tensor("sgn", [128, max(c.D, c.F)], BF16))[:, :]
        p3tensors = dict(s1T=s1T, s2T=s2T, hT=hT, o2tm=o2tm, oscr=oscr,
                         wch=wch, wch2=wch2, sgn=sgn)
        ms = phase3_prologue(M_END, p3tensors)
        m_prol = [("pe", P.cnt["pe"]), ("dv", P.cnt["dv"]), ("ac", P.cnt["ac"])]
        st3.close()
        nsub3 = c.TB3 // 128
        p3tensors["x3"] = sb("x3", [128, 2 * nsub3 * c.D], F32)
        p3tensors["x1"] = p3tensors["x3"]
        p3tensors["ln3f"] = sb("ln3f", [128, 2 * (c.TB3 // 128) * c.D], F32)
        p3tensors["ln3b"] = sb("ln3b", [128, 2 * (c.TB3 // 128) * c.D], BF16)
        p3tensors["lnscr"] = None
        p3tensors["ln3T"] = sb("ln3T", [128, 2 * c.DC * c.TB3], BF16)
        p3tensors["m_y_done"] = m_y_done
        p3tensors["m_prol"] = m_prol
        phase3(M_END, ms, p3tensors)

        P.do("sync", lambda e: e.nop(), waits=wall(phase3.out_dmas))

        P.emit(block, semh, first_waits=[("gp", m_init)])
    return nc


def kernel(L0_attn_x, L0_attn_w_qkv, L0_attn_w_out, L0_ffn_w1, L0_ffn_b1,
           L0_ffn_w2, L0_ffn_b2):
    x = np.ascontiguousarray(np.asarray(L0_attn_x, dtype=np.float32))
    wqkv = np.ascontiguousarray(np.asarray(L0_attn_w_qkv, dtype=np.float32))
    wout = np.ascontiguousarray(np.asarray(L0_attn_w_out, dtype=np.float32))
    w1 = np.ascontiguousarray(np.asarray(L0_ffn_w1, dtype=np.float32))
    b1 = np.ascontiguousarray(np.asarray(L0_ffn_b1, dtype=np.float32))
    w2 = np.ascontiguousarray(np.asarray(L0_ffn_w2, dtype=np.float32))
    b2 = np.ascontiguousarray(np.asarray(L0_ffn_b2, dtype=np.float32))

    cfg = Cfg(T=4096, D=1024, H=16, F=4096)
    nc = build_nc(cfg)
    in_maps = []
    for b in range(8):
        in_maps.append({"x": x[b], "w_qkv": wqkv, "w_out": wout,
                        "w1": w1, "b1": b1, "w2": w2, "b2": b2})
    res = run_bass_kernel_spmd(nc, in_maps, core_ids=list(range(8)))
    out = np.stack([res.results[b]["out"] for b in range(8)], axis=0)
    return out.astype(np.float32)


# version 30
# speedup vs baseline: 1.1826x; 1.0355x over previous
import math
from contextlib import ExitStack

import numpy as np
import concourse.bass as bass
import concourse.mybir as mybir
from concourse.alu_op_type import AluOpType
from concourse.bass_utils import run_bass_kernel_spmd

F32 = mybir.dt.float32
BF16 = mybir.dt.bfloat16
FP16 = mybir.dt.float16
FP8 = mybir.dt.float8e4
AF = mybir.ActivationFunctionType
AX = mybir.AxisListType

ENGINES = ("tensor", "vector", "scalar", "gpsimd", "sync")
SEMS = (("pe", "dv", "ac", "gp", "out")
        + tuple(f"qs{i}" for i in range(8)) + tuple(f"qg{i}" for i in range(8)))


def wall(tuples):
    d = {}
    for s_, v in tuples:
        if v:
            d[s_] = max(d.get(s_, 0), v)
    return list(d.items())


class Cfg:
    def __init__(self, T=4096, D=1024, H=16, F=4096):
        self.T, self.D, self.H, self.F = T, D, H, F
        self.DH = 64
        assert H * self.DH == D
        self.DC = D // 128
        self.FC = F // 128
        self.KT = T // 128
        self.G = 2
        self.HG = H // 2
        self.CH = self.HG * 64 // 128
        self.TBLK = 512
        self.NTB = T // self.TBLK
        self.TB3 = 256
        self.NT3 = T // self.TB3
        self.VW = self.HG * 65
        self.scale = 1.0 / math.sqrt(D)
        self.kgroups = []
        k = 0
        while k < self.KT:
            n = min(3, self.KT - k)
            self.kgroups.append((k, n))
            k += n


def apx(view, offset, dims):
    p = view.ap[0]
    return bass.AP(view.tensor, offset, [[p[0], p[1]]] + [list(d) for d in dims])


class Prog:
    def __init__(self):
        self.ops = {e: [] for e in ENGINES}
        self.cnt = {s: 0 for s in SEMS}

    def do(self, eng, fn, waits=(), inc=None, n=1):
        self.ops[eng].append((fn, tuple(waits), inc, n))
        if inc is not None:
            self.cnt[inc] += n
            return self.cnt[inc]
        return None

    def emit(self, block, sems, first_waits=()):
        for ename in ENGINES:
            def body(eng, ename=ename):
                waited = {}
                if ename != "gpsimd":
                    for s, v in first_waits:
                        if v:
                            eng.wait_ge(sems[s], v)
                            waited[s] = v
                for fn, waits, inc, n in self.ops[ename]:
                    for s, v in waits:
                        if v and waited.get(s, 0) < v:
                            eng.wait_ge(sems[s], v)
                            waited[s] = v
                    instr = fn(eng)
                    if inc is not None:
                        instr.then_inc(sems[inc], n)
            getattr(block, ename)(body)


def build_nc(cfg: Cfg):
    c = cfg
    nc = bass.Bass("TRN2", target_bir_lowering=False)

    X = nc.dram_tensor("x", [c.T, c.D], F32, kind="ExternalInput").ap()
    Wqkv = nc.dram_tensor("w_qkv", [c.D, 3 * c.D], F32, kind="ExternalInput").ap()
    Wout = nc.dram_tensor("w_out", [c.D, c.D], F32, kind="ExternalInput").ap()
    W1 = nc.dram_tensor("w1", [c.F, c.D], F32, kind="ExternalInput").ap()
    B1 = nc.dram_tensor("b1", [c.F], F32, kind="ExternalInput").ap()
    W2 = nc.dram_tensor("w2", [c.D, c.F], F32, kind="ExternalInput").ap()
    B2 = nc.dram_tensor("b2", [c.D], F32, kind="ExternalInput").ap()
    OUT = nc.dram_tensor("out", [c.T, c.D], F32, kind="ExternalOutput").ap()

    P = Prog()
    with ExitStack() as st:
        def sb(name, shape, dtype):
            h = st.enter_context(nc.sbuf_tensor(name, shape, dtype))
            return h[tuple(slice(None) for _ in shape)]

        id32 = sb("id32", [128, 128], F32)
        id_bf = sb("id_bf", [128, 128], BF16)
        id16 = sb("id16", [128, 128], FP16)
        ones16 = sb("ones16", [1, 128], FP16)
        ones_bf = sb("ones_bf", [128, 1], BF16)
        eps_t = sb("eps_t", [128, 1], F32)
        mhalf = sb("mhalf", [128, 1], F32)
        alpha1 = sb("alpha1", [128, c.FC], F32)
        alpha2 = sb("alpha2", [128, c.DC], F32)
        b1s = sb("b1s", [128, c.FC], F32)
        b2s = sb("b2s", [128, c.DC], F32)
        y_acc = sb("y_acc", [128, (c.T // 128) * c.D], FP16)
        kmaxs = sb("kmaxs", [1, 2], F32)

        stats = sb("stats", [128, 80], F32)
        qmax4 = sb("qmax4", [128, 8], F32)
        sc1 = sb("sc1", [1, 8], F32)
        negBv = sb("negBv", [128, 1], F32)

        st2 = ExitStack()

        def sb2(name, shape, dtype):
            h = st2.enter_context(nc.sbuf_tensor(name, shape, dtype))
            return h[tuple(slice(None) for _ in shape)]

        kT = sb2("kT", [128, c.CH * c.T], BF16)
        vP = sb2("vP", [128, c.KT * c.VW], BF16)
        wsh = sb2("wsh", [128, c.DC * c.HG * 64 + (c.HG // 2) * c.D], BF16)
        wkv = wsh[:, 0:c.DC * c.HG * 128]
        wq = wsh[:, 0:c.DC * c.HG * 64]
        wo = wsh[:, c.DC * c.HG * 64:]
        xs = sb2("xs", [128, 4 * c.D], F32)
        lnx16 = sb2("lnx16", [128, 4 * c.D], BF16)
        lnxT = sb2("lnxT", [128, c.DC * c.TBLK], BF16)
        qT = sb2("qT", [128, 2 * c.CH * c.TBLK], BF16)
        oT = sb2("oT", [128, (c.HG // 2) * c.TBLK], BF16)
        pT = sb2("pT", [128, 4 * 1024], BF16)
        escr = sb2("escr", [128, 1024], F32)
        otmp = escr[0:64, :]
        recl = escr[64:65, :]
        vtmp16 = pT[:, 0:512]
        recl16 = sb2("recl16", [1, 520], FP16)

        psh = st.enter_context(nc.psum_tensor("ps", [128, 4096], F32))
        ps = psh[:, :]

        def bank(i, lo=0, width=512):
            return ps[:, 512 * i + lo: 512 * i + lo + width]

        semh = {s: st.enter_context(nc.semaphore(s)) for s in SEMS}
        block = st.enter_context(nc.Block())

        SC = c.scale
        BW = {i: [] for i in range(8)}

        rr = {"sync": 0, "gpsimd": 0}

        def dma(q, out, in_, waits=()):
            pre = "qs" if q == "sync" else "qg"
            sem = f"{pre}{rr[q] % 8}"
            rr[q] += 1
            chain = [(sem, P.cnt[sem])]
            v = P.do(q, lambda e, o=out, i=in_: e.dma_start(out=o, in_=i),
                     waits=list(waits) + chain, inc=sem, n=16)
            return (sem, v)

        m0 = P.do("gpsimd", lambda g: g.memset(id32, 0.0), inc="gp")

        P.do("gpsimd", lambda g: g.affine_select(
            out=id32, in_=id32, compare_op=AluOpType.not_equal, fill=1.0,
            base=0, pattern=[[-1, 128]], channel_multiplier=1),
            waits=[("gp", m0)], inc="gp")
        mib = P.do("gpsimd", lambda g: g.memset(id_bf, 0.0), inc="gp")
        P.do("gpsimd", lambda g: g.affine_select(
            out=id_bf, in_=id_bf, compare_op=AluOpType.not_equal, fill=1.0,
            base=0, pattern=[[-1, 128]], channel_multiplier=1),
            waits=[("gp", mib)], inc="gp")
        mih = P.do("gpsimd", lambda g: g.memset(id16, 0.0), inc="gp")
        P.do("gpsimd", lambda g: g.affine_select(
            out=id16, in_=id16, compare_op=AluOpType.not_equal, fill=1.0,
            base=0, pattern=[[-1, 128]], channel_multiplier=1),
            waits=[("gp", mih)], inc="gp")
        P.do("gpsimd", lambda g: g.memset(ones16, 1.0), inc="gp")
        P.do("gpsimd", lambda g: g.memset(ones_bf, 1.0), inc="gp")
        P.do("gpsimd", lambda g: g.memset(eps_t, 1e-5), inc="gp")
        P.do("gpsimd", lambda g: g.memset(mhalf, -0.5), inc="gp")
        P.do("gpsimd", lambda g: g.memset(
            apx(vP, 64, [[c.VW, c.KT], [65, c.HG]]), 1.0), inc="gp")
        m_init = P.do("gpsimd", lambda g: g.memset(oT, 0.0), inc="gp")

        m_bload = []
        for fc in range(c.FC):
            m_bload.append(dma("sync", b1s[:, fc:fc + 1],
                               bass.AP(B1.tensor, fc * 128, [[1, 128], [1, 1]])))
        for dc in range(c.DC):
            m_bload.append(dma("sync", b2s[:, dc:dc + 1],
                               bass.AP(B2.tensor, dc * 128, [[1, 128], [1, 1]])))
        m_bload = wall(m_bload)

        slot_pow = {}
        slot_norm = {}

        def ln_stats(src, slot, wait_src):
            o = slot * 20
            ts_ = min(512, c.D)
            nh = c.D // ts_
            for i in range(nh):
                P.do("vector",
                     lambda v, i=i, o=o, s=src, ts_=ts_: v.bn_stats(
                         stats[:, o + 6 * i:o + 6 * i + 6],
                         s[:, ts_ * i:ts_ * (i + 1)]),
                     waits=list(wait_src) + [("gp", slot_pow.get(slot, 0)),
                                             ("dv", slot_norm.get(slot, 0))]
                     if i == 0 else (),
                     inc="dv" if i == nh - 1 else None)
            m1 = P.cnt["dv"]
            m2 = P.do("vector", lambda v, o=o, n=nh: v.bn_aggr(
                stats[:, o + 12:o + 14], stats[:, o:o + 6 * n]),
                waits=[("dv", m1)], inc="dv")
            m3 = P.do("vector", lambda v, o=o: v.tensor_scalar(
                stats[:, o + 14:o + 15], stats[:, o + 13:o + 14], eps_t[:, 0:1],
                None, AluOpType.add, AluOpType.bypass),
                waits=[("dv", m2), ("gp", slot_pow.get(slot, 0))], inc="dv")
            m4 = P.do("gpsimd", lambda g, o=o: g.tensor_tensor(
                stats[:, o + 15:o + 16], stats[:, o + 14:o + 15], mhalf,
                AluOpType.pow),
                waits=[("dv", m3), ("gp", m_init),
                       ("dv", slot_norm.get(slot, 0))], inc="gp")
            slot_pow[slot] = m4
            return m4

        def ln_norm(src, dst, slot, m4, extra_waits=()):
            o = slot * 20
            m5 = P.do("vector", lambda v, o=o: v.scalar_tensor_tensor(
                stats[:, o + 16:o + 17], stats[:, o + 12:o + 13], -1.0,
                stats[:, o + 15:o + 16], AluOpType.mult, AluOpType.mult),
                waits=[("gp", m4)], inc="dv")
            m6 = P.do("vector", lambda v, o=o, s=src, d=dst: v.tensor_scalar(
                d, s, stats[:, o + 15:o + 16], stats[:, o + 16:o + 17],
                AluOpType.mult, AluOpType.add),
                waits=[("dv", m5)] + list(extra_waits), inc="dv")
            slot_norm[slot] = m6
            return m6

        def transp(src, n, b0, dst_of, wait_first, evac_eng="vector", nb=2,
                   ident=None):
            sem = "dv" if evac_eng == "vector" else "ac"
            idt = id32 if ident is None else ident
            step = 4
            last = None
            g0 = 0
            while g0 < n:
                g1 = min(g0 + step, n)
                bi = b0 + ((g0 // 4) % nb if step == 4 else 0)
                bw = BW[bi] if step == 4 else BW[b0] + BW[b0 + 1]
                for j in range(g0, g1):
                    w = []
                    if j == 0:
                        w += list(wait_first)
                    if j == g0:
                        w += bw
                    P.do("tensor", lambda t, j=j, bi=bi, g0=g0, s=src, idt=idt:
                         (t.transpose(
                             bank(bi + (j - g0) // 4, 128 * ((j - g0) % 4), 128),
                             s[:, 128 * j:128 * (j + 1)], idt)
                          if idt is id32 else
                          t.matmul(
                             bank(bi + (j - g0) // 4, 128 * ((j - g0) % 4), 128),
                             s[:, 128 * j:128 * (j + 1)], idt,
                             start=True, stop=True)),
                         waits=w, inc="pe" if j == g1 - 1 else None)
                mpe = P.cnt["pe"]
                cn = g1 - g0
                if evac_eng == "vector":
                    last = P.do("vector", lambda v, bi=bi, g0=g0, cn=cn: v.tensor_copy(
                        dst_of(g0, cn), bank(bi, 0, 128 * cn)),
                        waits=[("pe", mpe)], inc="dv")
                else:
                    last = P.do("scalar", lambda e, bi=bi, g0=g0, cn=cn: e.activation(
                        dst_of(g0, cn), bank(bi, 0, 128 * cn), AF.Copy),
                        waits=[("pe", mpe)], inc="ac")
                if step == 4:
                    BW[bi] = [(sem, last)]
                else:
                    BW[b0] = [(sem, last)]
                    BW[b0 + 1] = [(sem, last)]
                g0 = g1
            return (sem, last), P.cnt["pe"]

        st_blk = {"xw": [[] for _ in range(4)],
                  "l16w": [[] for _ in range(4)],
                  "lnxT_users": [],
                  "m_lnxT_sem": "dv", "evac_eng": "vector"}

        def ln1_part1(t0_base):
            nsub = c.TBLK // 128
            pows, loads = [], []
            for s in range(nsub):
                mx = dma("sync", xs[:, s * c.D:(s + 1) * c.D],
                         X[t0_base + s * 128: t0_base + s * 128 + 128, :],
                         waits=st_blk["xw"][s])
                loads.append(mx)
            for s in range(nsub):
                pows.append(ln_stats(xs[:, s * c.D:(s + 1) * c.D], s, [loads[s]]))
            norms = []
            for s in range(nsub):
                xsl = xs[:, s * c.D:(s + 1) * c.D]
                l16 = lnx16[:, s * c.D:(s + 1) * c.D]
                mn = ln_norm(xsl, l16, s, pows[s],
                             extra_waits=st_blk["l16w"][s])
                st_blk["xw"][s] = [("dv", mn)]
                norms.append(mn)
            return norms

        def ln1_part2(norms, b0=0):
            nsub = c.TBLK // 128
            mlast = None
            for s in range(nsub):
                l16 = lnx16[:, s * c.D:(s + 1) * c.D]
                def dst(j0, cn, s=s):
                    return apx(lnxT, j0 * c.TBLK + s * 128, [[c.TBLK, cn], [1, 128]])
                (semn, mev), mtp = transp(
                    l16, c.DC, b0, dst,
                    wait_first=[("dv", norms[s])]
                    + [("pe", u) for u in st_blk["lnxT_users"][-1:]],
                    evac_eng=st_blk["evac_eng"], ident=id_bf)
                st_blk["l16w"][s] = [("pe", mtp)]
                st_blk["m_lnxT_sem"] = semn
                mlast = mev
            return mlast

        war_wkv = {"v": []}
        war_kT = {"v": []}
        war_vP = {"v": []}
        m_kv_done = {}

        def phase_A(g):
            W2G = c.HG * 128
            m_w = []
            for dc in range(c.DC):
                m_w.append(dma("gpsimd", wkv[:, dc * W2G: dc * W2G + c.HG * 64],
                    Wqkv[dc * 128:(dc + 1) * 128,
                         c.D + g * c.HG * 64: c.D + (g + 1) * c.HG * 64],
                    waits=(war_wkv["v"] + [("gp", m_init)]) if dc == 0 else ()))
                m_w.append(dma("gpsimd", wkv[:, dc * W2G + c.HG * 64: (dc + 1) * W2G],
                    Wqkv[dc * 128:(dc + 1) * 128,
                         2 * c.D + g * c.HG * 64: 2 * c.D + (g + 1) * c.HG * 64]))
            m_w = wall(m_w)
            mk0 = P.do("vector", lambda v: v.memset(qmax4[:, 7:8], 0.0),
                       waits=war_kT["v"], inc="dv")
            kmax_run = mk0

            nmk = c.HG * 128 // 128
            nK = nmk // 2
            BW_vtmp[:] = [("dv", P.cnt["dv"]), ("pe", P.cnt["pe"])]
            st_blk["evac_eng"] = "scalar"
            norms = ln1_part1(0)
            for tb in range(c.NTB):
                m_lnxT = ln1_part2(norms)
                if tb + 1 < c.NTB:
                    norms = ln1_part1((tb + 1) * c.TBLK)
                for mc in range(nmk):
                    pb = 2 + (mc % 2)
                    is_v = mc >= nK
                    for dc in range(c.DC):
                        w = (BW[pb] + m_w
                             + [(st_blk["m_lnxT_sem"], m_lnxT)]) if dc == 0 else ()
                        P.do("tensor", lambda t, dc=dc, mc=mc, pb=pb: t.matmul(
                            bank(pb),
                            wkv[:, dc * W2G + mc * 128: dc * W2G + (mc + 1) * 128],
                            lnxT[:, dc * c.TBLK:(dc + 1) * c.TBLK],
                            start=(dc == 0), stop=(dc == c.DC - 1)),
                            waits=w, inc="pe" if dc == c.DC - 1 else None)
                    m_mm = P.cnt["pe"]
                    st_blk["lnxT_users"] = [m_mm]
                    if not is_v:
                        mea = P.do("scalar", lambda e, mc=mc, tb=tb, pb=pb: e.activation(
                            kT[:, mc * c.T + tb * c.TBLK: mc * c.T + (tb + 1) * c.TBLK],
                            bank(pb), AF.Copy),
                            waits=[("pe", m_mm)] + war_kT["v"], inc="ac")
                        mr = P.do("vector", lambda v, mc=mc, tb=tb: v.tensor_reduce(
                            qmax4[:, 6:7],
                            kT[:, mc * c.T + tb * c.TBLK: mc * c.T + (tb + 1) * c.TBLK],
                            axis=AX.X, op=AluOpType.max,
                            apply_absolute_value=True),
                            waits=[("ac", mea), ("dv", kmax_run)], inc="dv")
                        kmax_run = P.do("vector", lambda v: v.tensor_tensor(
                            qmax4[:, 7:8], qmax4[:, 7:8], qmax4[:, 6:7],
                            AluOpType.max), waits=[("dv", mr)], inc="dv")
                        BW[pb] = [("ac", mea)]
                    else:
                        mev = P.do("scalar", lambda e, pb=pb: e.activation(
                            vtmp16, bank(pb), AF.Copy),
                            waits=[("pe", m_mm), BW_vtmp[0]] if BW_vtmp else [("pe", m_mm)],
                            inc="ac")
                        BW[pb] = [("ac", mev)]
                        mcv = mc - nK
                        vb = 4 + (mcv % 2)
                        nsub = c.TBLK // 128
                        for j in range(nsub):
                            w = ([("ac", mev)] + BW[vb]) if j == 0 else ()
                            P.do("tensor", lambda t, j=j, vb=vb: t.matmul(
                                bank(vb, 128 * j, 128),
                                vtmp16[:, 128 * j:128 * (j + 1)], id_bf,
                                start=True, stop=True),
                                waits=w, inc="pe" if j == nsub - 1 else None)
                        m_vt = P.cnt["pe"]
                        BW_vtmp[:] = [("pe", m_vt)]
                        mlv = None
                        for h2 in range(2):
                            hl = 2 * mcv + h2
                            mlv = P.do("vector", lambda v, h2=h2, hl=hl, tb=tb,
                                       nsub=nsub, vb=vb:
                                       v.tensor_copy(
                                           apx(vP, tb * nsub * c.VW + hl * 65,
                                               [[c.VW, nsub], [1, 64]]),
                                           apx(ps, 512 * vb + h2 * 64,
                                               [[128, nsub], [1, 64]])),
                                       waits=([("pe", m_vt)] + war_vP["v"]) if h2 == 0 else (),
                                       inc="dv")
                        BW[vb] = [("dv", mlv)]
            war_wq["v"] = [("pe", P.cnt["pe"])]
            w5 = BW[5]
            P.do("tensor", lambda t: t.transpose(bank(5, 0, 128)[0:1, :],
                                                 qmax4[:, 7:8], id32),
                 waits=[("dv", kmax_run)] + w5, inc="pe")
            mt = P.cnt["pe"]
            mk = P.do("vector", lambda v, g=g: v.tensor_reduce(
                kmaxs[:, g:g + 1], bank(5)[0:1, 0:128], axis=AX.X,
                op=AluOpType.max), waits=[("pe", mt)], inc="dv")
            BW[5] = [("dv", mk)]
            m_kv_done[g] = mk

        BW_vtmp = []

        war_qT = {"v": []}
        war_oT = {"v": []}
        war_wq = {"v": []}
        war_otmp = {0: [], 1: []}
        war_recl = {0: [], 1: []}
        war_recl16 = {"v": []}
        tailstate = {"fn": None}
        prolstate = {"fn": None, "m_qT": 0}
        bgq = []
        bgq_meta = {"ob0": -1}
        pvstate = {"m_pv_0": 0, "m_pv_1": 0}
        ringstate = {"exp": {}, "pv": {}, "i": 0, "pgc": 0}

        def phase_B(g):
            m_w = []
            for dc in range(c.DC):
                m_w.append(dma("gpsimd", wq[:, dc * c.HG * 64:(dc + 1) * c.HG * 64],
                    Wqkv[dc * 128:(dc + 1) * 128,
                         g * c.HG * 64:(g + 1) * c.HG * 64],
                    waits=war_wq["v"] if dc == 0 else ()))
            for hp in range(c.HG // 2):
                m_w.append(dma("gpsimd", wo[:, hp * c.D:(hp + 1) * c.D],
                    Wout[(g * c.HG + 2 * hp) * 64:
                         (g * c.HG + 2 * hp) * 64 + 128, :],
                    waits=([("gp", m_init)] + war_wq["v"]) if hp == 0 else ()))
            m_w = wall(m_w)
            m_pw = [("pe", P.cnt["pe"])]
            m_sq = P.do("scalar", lambda e: e.activation(
                lnxT[:, 0:c.DC * c.HG * 64], wq, AF.Square),
                waits=m_w + m_pw + [("ac", P.cnt["ac"])], inc="ac")
            NQ = c.HG * 64
            for dc in range(c.DC):
                w = ([("ac", m_sq)] + BW[0]) if dc == 0 else ()
                P.do("tensor", lambda t, dc=dc: t.matmul(
                    bank(0, 0, NQ)[0:1, :], ones_bf,
                    lnxT[:, dc * NQ:(dc + 1) * NQ],
                    start=(dc == 0), stop=(dc == c.DC - 1)),
                    waits=w, inc="pe" if dc == c.DC - 1 else None)
            m_cn = P.cnt["pe"]
            st_blk["lnxT_users"] = [m_cn]
            pvstate["pT_free"] = 0
            mr2 = P.do("vector", lambda v: v.tensor_reduce(
                sc1[:, 2:3], bank(0, 0, NQ)[0:1, :], axis=AX.X,
                op=AluOpType.max), waits=[("pe", m_cn)], inc="dv")
            BW[0] = [("dv", mr2)]
            ml1 = P.do("scalar", lambda e: e.activation(
                sc1[:, 3:4], sc1[:, 2:3], AF.Ln), waits=[("dv", mr2)], inc="ac")
            ml2 = P.do("scalar", lambda e: e.activation(
                sc1[:, 3:4], sc1[:, 3:4], AF.Exp, scale=0.5),
                waits=[("ac", ml1)], inc="ac")
            mr3 = P.do("vector", lambda v, g=g: v.scalar_tensor_tensor(
                sc1[:, 1:2], sc1[:, 3:4],
                -float(c.DH) * SC * math.sqrt(c.D), kmaxs[:, g:g + 1],
                AluOpType.mult, AluOpType.mult),
                waits=[("ac", ml2), ("dv", m_kv_done[g])], inc="dv")
            mr4 = P.do("vector", lambda v: v.tensor_copy(
                recl16[0:1, 512:513], sc1[:, 1:2]),
                waits=[("dv", mr3)] + war_recl16["v"], inc="dv")
            w1b = BW[1]
            P.do("tensor", lambda t: t.matmul(
                bank(1, 0, 1), ones16, recl16[0:1, 512:513],
                start=True, stop=True),
                waits=[("dv", mr4)] + w1b, inc="pe")
            mbc = P.cnt["pe"]
            m_negB = P.do("vector", lambda v: v.tensor_copy(
                negBv, bank(1, 0, 1)),
                waits=[("pe", mbc), ("ac", P.cnt["ac"])], inc="dv")
            BW[1] = BW[1] + [("dv", m_negB)]

            def emit_prologueB(qb, norms, g=None, pbanks=(2, 3)):
                m_lnxT = ln1_part2(norms, b0=pbanks[0])
                mq = 0
                qsl = (qb % 2) * c.CH * c.TBLK
                for mc in range(c.CH):
                    pb = pbanks[mc % 2]
                    for dc in range(c.DC):
                        w = (BW[pb] + m_w + [("dv", m_lnxT)]) if dc == 0 else ()
                        P.do("tensor", lambda t, dc=dc, mc=mc, pb=pb: t.matmul(
                            bank(pb),
                            wq[:, dc * c.HG * 64 + mc * 128: dc * c.HG * 64 + (mc + 1) * 128],
                            lnxT[:, dc * c.TBLK:(dc + 1) * c.TBLK],
                            start=(dc == 0), stop=(dc == c.DC - 1)),
                            waits=w, inc="pe" if dc == c.DC - 1 else None)
                    m_mm = P.cnt["pe"]
                    st_blk["lnxT_users"] = [m_mm]
                    if mc % 2 == 1 or mc == c.CH - 1:
                        nmc = 2 if mc % 2 == 1 else 1
                        mc0 = mc - nmc + 1
                        me = P.do("vector", lambda v, mc0=mc0, nmc=nmc, qsl=qsl:
                                  v.tensor_copy(
                                      qT[:, qsl + mc0 * c.TBLK:
                                         qsl + (mc0 + nmc) * c.TBLK],
                                      bank(pbanks[0] if mc0 % 2 == 0 else pbanks[1],
                                           0, nmc * 512)),
                                  waits=[("pe", m_mm)] + war_qT["v"], inc="dv")
                        mq = me
                        BW[pbanks[0]] = [("dv", me)]
                        BW[pbanks[1]] = [("dv", me)]
                return mq

            def emit_outproj(qb, g, ob0, spread=True):
                m_oT = P.cnt["dv"]
                nsub = c.TBLK // 128
                OW = min(512, c.D)
                ops = []
                for tc in range(nsub):
                    for oh in range(c.D // OW):
                        pb = ob0 + (tc * (c.D // OW) + oh) % 2
                        grp = []
                        for hl in range(c.HG // 2):
                            first = hl == 0
                            last = hl == c.HG // 2 - 1

                            def mm(tc=tc, oh=oh, hl=hl, pb=pb, first=first,
                                   last=last, qb=qb, g=g):
                                w = ([("dv", m_oT)] + m_w + BW[pb]) if first else ()
                                P.do("tensor", lambda t: t.matmul(
                                    bank(pb, 0, OW),
                                    oT[:, hl * c.TBLK + tc * 128:
                                       hl * c.TBLK + (tc + 1) * 128],
                                    wo[:, hl * c.D + oh * OW:
                                       hl * c.D + (oh + 1) * OW],
                                    start=first, stop=last),
                                    waits=w, inc="pe" if last else None)
                                if last:
                                    m_mm = P.cnt["pe"]
                                    war_oT["v"] = [("pe", m_mm)]
                                    dsty = y_acc[:, (qb * nsub + tc) * c.D + oh * OW:
                                                 (qb * nsub + tc) * c.D + (oh + 1) * OW]
                                    if g == 0:
                                        mev = P.do("vector",
                                                   lambda v, d=dsty, pb=pb: v.tensor_copy(
                                                       d, bank(pb, 0, OW)),
                                                   waits=[("pe", m_mm)], inc="dv")
                                    else:
                                        mev = P.do("vector",
                                                   lambda v, d=dsty, pb=pb: v.tensor_tensor(
                                                       d, d, bank(pb, 0, OW),
                                                       AluOpType.add),
                                                   waits=[("pe", m_mm),
                                                          ("ac", m_yg0[0])], inc="dv")
                                    BW[pb] = [("dv", mev)]
                            grp.append(mm)
                        ops.append(lambda grp=grp: [f() for f in grp])
                if spread:
                    bgq_meta["ob0"] = ob0
                    bgq.extend(ops)
                else:
                    for op in ops:
                        op()

            st_blk["evac_eng"] = "vector"
            for _h in range(2):            st_blk["evac_eng"] = "vector"
            for _h in range(2):            st_blk["evac_eng"] = "vector"
            for _h in range(2):            st_blk["evac_eng"] = "vector"
            for _h in range(2):
                war_otmp[_h] = war_otmp[_h] + [("ac", P.cnt["ac"])]
                war_recl[_h] = war_recl[_h] + [("ac", P.cnt["ac"])]
            normsB = ln1_part1(0)
            prol = emit_prologueB(0, normsB, g=g)
            for qb in range(c.NTB):
                m_qT = prol
                if qb + 1 < c.NTB:
                    normsB = ln1_part1((qb + 1) * c.TBLK)
                    if c.CH >= 3:
                        def mk_prol(qb=qb, g=g, nb_=normsB):
                            def fn():
                                prolstate["m_qT"] = emit_prologueB(
                                    qb + 1, nb_, g=g, pbanks=(4, 5))
                            return fn
                        prolstate["fn"] = mk_prol()
                def emit_epilogue(p, obase, m_pv):
                    mults = []
                    for half in range(2):
                        ob = obase + half
                        mo = P.do("vector", lambda v, ob=ob, half=half:
                                  v.tensor_copy(
                                      otmp[:, half * 512:(half + 1) * 512],
                                      bank(ob)[0:64, :]),
                                  waits=[("pe", m_pv)]
                                  + war_otmp[half], inc="dv")
                        mrc = P.do("vector", lambda v, ob=ob, half=half:
                                   v.reciprocal(
                                       recl[0:1, half * 512:(half + 1) * 512],
                                       bank(ob)[64:65, :]),
                                   waits=[("pe", m_pv),
                                          ("dv", mo)] + war_recl[half], inc="dv")
                        mrc2 = P.do("vector", lambda v, half=half: v.tensor_copy(
                            recl16[0:1, 0:512],
                            recl[0:1, half * 512:(half + 1) * 512]),
                            waits=[("dv", mrc)] + war_recl16["v"], inc="dv")
                        P.do("tensor", lambda t, ob=ob: t.matmul(
                            bank(ob), ones16,
                            recl16[0:1, 0:512],
                            start=True, stop=True),
                            waits=[("dv", mrc2)], inc="pe")
                        mb = P.cnt["pe"]
                        war_recl16["v"] = [("pe", mb)]
                        mm_ = P.do("vector", lambda v, ob=ob, half=half, p=p:
                                   v.tensor_tensor(
                                       oT[64 * half:64 * half + 64,
                                          p * c.TBLK:(p + 1) * c.TBLK],
                                       otmp[:, half * 512:(half + 1) * 512],
                                       bank(ob)[0:64, :], AluOpType.mult),
                                   waits=[("pe", mb)] + war_oT["v"], inc="dv")
                        BW[ob] = [("dv", mm_)]
                        war_otmp[half] = [("dv", mm_)]
                        war_recl[half] = [("pe", mb)]
                        mults.append(mm_)
                    return mults

                pending_epi = None
                EH, PH = ringstate["exp"], ringstate["pv"]
                qsl = (qb % 2) * c.CH * c.TBLK
                for p in range(c.CH):
                    hlA, hlB = 2 * p, 2 * p + 1
                    obase = 6 if (ringstate["pgc"] % 2 == 0) else 4
                    ringstate["pgc"] += 1
                    base = ringstate["i"]
                    if obase == bgq_meta["ob0"]:
                        while bgq:
                            bgq.pop(0)()

                    def emit_qk(t_, p=p, base=base, qsl=qsl):
                        gi = base + t_
                        sp = gi % 2
                        for half in range(2):
                            rb = 2 * sp + half
                            r0 = 64 * half
                            w = []
                            if half == 0:
                                w = [("ac", EH.get(gi - 2, 0))]
                                if t_ == 0:
                                    w += [("dv", m_qT)] + BW[0] + BW[1]
                                elif t_ == 1:
                                    w += BW[2] + BW[3]
                            P.do("tensor",
                                 lambda t, t_=t_, p=p, r0=r0, rb=rb, qsl=qsl: t.matmul(
                                     bank(rb),
                                     kT[r0:r0 + 64,
                                        p * c.T + t_ * 128: p * c.T + (t_ + 1) * 128],
                                     qT[r0:r0 + 64,
                                        qsl + p * c.TBLK: qsl + (p + 1) * c.TBLK],
                                     start=True, stop=True),
                                 waits=w, inc="pe" if half == 1 else None)
                        EH[("qk", gi)] = P.cnt["pe"]

                    def emit_exp(t_, base=base):
                        gi = base + t_
                        sp = gi % 2
                        sl4 = gi % 4
                        EH[gi] = P.do(
                            "scalar", lambda e, sp=sp, sl4=sl4: e.activation(
                                pT[:, sl4 * 1024: (sl4 + 1) * 1024],
                                bank(2 * sp, 0, 1024),
                                AF.Exp, scale=SC, bias=negBv[:, 0:1]),
                            waits=[("pe", EH[("qk", gi)]),
                                   ("pe", PH.get(gi - 4, 0)),
                                   ("pe", pvstate.get("pT_free", 0)),
                                   ("dv", m_negB)], inc="ac")

                    def emit_pv(t_, p=p, obase=obase, base=base):
                        gi = base + t_
                        sl4 = gi % 4
                        hls = (2 * p, 2 * p + 1)
                        for half in range(2):
                            ob = obase + half
                            hl = hls[half]
                            w = []
                            if half == 0:
                                w = [("ac", EH[gi])]
                                if t_ == 0:
                                    w += BW[obase] + BW[obase + 1]
                            P.do("tensor",
                                 lambda t, t_=t_, hl=hl, sl4=sl4, ob=ob, half=half:
                                 t.matmul(
                                     bank(ob, 0, 512)[0:65, :],
                                     vP[:, t_ * c.VW + hl * 65:
                                        t_ * c.VW + hl * 65 + 65],
                                     pT[:, sl4 * 1024 + half * 512:
                                        sl4 * 1024 + (half + 1) * 512],
                                     start=(t_ == 0), stop=(t_ == c.KT - 1)),
                                 waits=w, inc="pe" if half == 1 else None)
                        PH[gi] = P.cnt["pe"]
                        if (bgq and obase != bgq_meta["ob0"]
                                and t_ % 4 == 0):
                            bgq.pop(0)()

                    for u in range(c.KT // 2):
                        emit_qk(2 * u)
                        emit_exp(2 * u)
                        emit_qk(2 * u + 1)
                        if u > 0:
                            emit_pv(2 * u - 2)
                        emit_exp(2 * u + 1)
                        if u > 0:
                            emit_pv(2 * u - 1)
                        if (u == min(2, c.KT // 2 - 1)
                                and pending_epi is not None):
                            pending_epi()
                            pending_epi = None
                        if (u == min(4, c.KT // 2 - 1) and p == 0
                                and tailstate["fn"] is not None):
                            tailstate["fn"]()
                            tailstate["fn"] = None
                        if (u == min(3, c.KT // 2 - 1) and p == 2
                                and prolstate["fn"] is not None):
                            prolstate["fn"]()
                            prolstate["fn"] = None
                    emit_pv(c.KT - 2)
                    emit_pv(c.KT - 1)
                    ringstate["i"] += c.KT
                    pending_epi = (lambda p=p, obase=obase, m=P.cnt["pe"]:
                                   emit_epilogue(p, obase, m))
                while bgq:
                    bgq.pop(0)()
                war_qT["v"] = [("pe", P.cnt["pe"])]
                mac_all = P.cnt["ac"]
                for b_ in range(4):
                    BW[b_] = [("ac", mac_all)]
                if qb + 1 < c.NTB:
                    if c.CH >= 3:
                        assert prolstate["fn"] is None
                        prol = prolstate["m_qT"]
                    else:
                        prol = emit_prologueB(qb + 1, normsB, g=g)
                    ob0_now = 6 if ((ringstate["pgc"] - 1) % 2 == 0) else 4
                    def mk_tail(qb=qb, g=g, ep=pending_epi, ob0=ob0_now):
                        def fn():
                            if ep is not None:
                                ep()
                            emit_outproj(qb, g, ob0)
                        return fn
                    tailstate["fn"] = mk_tail()
                    pending_epi = None
                else:
                    assert tailstate["fn"] is None
                    if pending_epi is not None:
                        pending_epi()
                        pending_epi = None
                    emit_outproj(qb, g,
                                 6 if ((ringstate["pgc"] - 1) % 2 == 0) else 4,
                                 spread=False)
            war_wkv["v"] = [("pe", P.cnt["pe"])]
            war_kT["v"] = [("pe", P.cnt["pe"])]
            war_vP["v"] = [("pe", P.cnt["pe"])]
            war_qT["v"] = [("pe", P.cnt["pe"])]

        def phase3_prologue(M_END, tz):
            s1T, s2T, wch, wch2, sgn = (tz["s1T"], tz["s2T"], tz["wch"],
                                        tz["wch2"], tz["sgn"])
            war_wch = {0: list(BW[0]) + list(BW[1]) + M_END, 1: list(M_END)}
            war_sgn = {0: list(M_END), 1: list(M_END)}
            sgn_chain = {"v": list(M_END)}
            war_red = []
            ms = {}
            for fc in range(c.FC):
                sl = fc % 2
                WC = wch[:, sl * c.D:(sl + 1) * c.D]
                SG = sgn[:, 0:c.D]
                mw = dma("sync", WC, W1[fc * 128:(fc + 1) * 128, :],
                         waits=war_wch[sl])
                mrd = P.do("vector", lambda v, WC=WC: v.tensor_reduce(
                    stats[:, 18:19], WC, axis=AX.X, op=AluOpType.add,
                    apply_absolute_value=True), waits=[mw] + war_red, inc="dv")
                mal = P.do("vector", lambda v, fc=fc: v.tensor_scalar(
                    alpha1[:, fc:fc + 1], stats[:, 18:19], 1.0 / c.D, None,
                    AluOpType.mult, AluOpType.bypass), waits=[("dv", mrd)], inc="dv")
                war_red = [("dv", mal)]
                msg = P.do("scalar", lambda e, WC=WC, SG=SG: e.activation(
                    SG, WC, AF.Sign),
                    waits=[mw] + sgn_chain["v"], inc="ac")
                war_wch[sl] = [("dv", mrd), ("ac", msg)]

                def dst(j0, cn, fc=fc):
                    return apx(s1T, j0 * c.F + fc * 128, [[c.F, cn], [1, 128]])
                (semn, mev), mtp = transp(SG, c.DC, 0 if sl == 0 else 2, dst,
                                          wait_first=[("ac", msg)], ident=id_bf)
                sgn_chain["v"] = [("pe", mtp)]
            ms["s1T"] = P.cnt["dv"]
            war_wch = {0: war_wch[0] + war_wch[1], 1: war_wch[0] + war_wch[1]}
            for dc in range(c.DC):
                sl = dc % 2
                WC = wch2[:, sl * c.F:(sl + 1) * c.F]
                SG = sgn[:, 0:c.F]
                mw = dma("sync", WC, W2[dc * 128:(dc + 1) * 128, :],
                         waits=war_wch[sl])
                mrd = P.do("vector", lambda v, WC=WC: v.tensor_reduce(
                    stats[:, 18:19], WC, axis=AX.X, op=AluOpType.add,
                    apply_absolute_value=True), waits=[mw] + war_red, inc="dv")
                mal = P.do("vector", lambda v, dc=dc: v.tensor_scalar(
                    alpha2[:, dc:dc + 1], stats[:, 18:19], 1.0 / c.F, None,
                    AluOpType.mult, AluOpType.bypass), waits=[("dv", mrd)], inc="dv")
                war_red = [("dv", mal)]
                msg = P.do("scalar", lambda e, WC=WC, SG=SG: e.activation(
                    SG, WC, AF.Sign),
                    waits=[mw] + sgn_chain["v"], inc="ac")
                war_wch[sl] = [("dv", mrd), ("ac", msg)]

                def dst(j0, cn, dc=dc):
                    return apx(s2T, j0 * c.D + dc * 128, [[c.D, cn], [1, 128]])
                (semn, mev), mtp = transp(SG, c.FC, 0 if sl == 0 else 2, dst,
                                          wait_first=[("ac", msg)], ident=id_bf)
                sgn_chain["v"] = [("pe", mtp)]
            ms["s2T"] = P.cnt["dv"]
            ms["alpha"] = P.cnt["dv"]
            return ms

        def phase3(M_END, ms, tz):
            s1T, s2T, hT = tz["s1T"], tz["s2T"], tz["hT"]
            x3, x1, ln3T = tz["x3"], tz["x1"], tz["ln3T"]
            ln3f, ln3b = tz["ln3f"], tz["ln3b"]
            o2tm, oscr = tz["o2tm"], tz["oscr"]
            out_dmas = phase3.out_dmas = []
            m_y_done = tz["m_y_done"]
            MP = tz["m_prol"]
            war_x3 = {0: list(M_END) + MP, 1: list(M_END) + MP}
            war_x1 = {0: MP, 1: MP}
            war_ln3f = {k: MP for k in range(2 * (c.TB3 // 128))}
            war_ln3b = {k: MP for k in range(2 * (c.TB3 // 128))}
            war_ln3T = {0: MP, 1: MP}
            war_hT = {"v": []}
            war_o2 = []
            nsub = c.TB3 // 128
            DCT = c.DC * c.TB3

            def front(tb):
                t0 = tb * c.TB3
                sl = tb % 2
                X3 = lambda s: x3[:, (sl * nsub + s) * c.D:(sl * nsub + s + 1) * c.D]
                X1 = lambda s: x1[:, (sl * nsub + s) * c.D:(sl * nsub + s + 1) * c.D]
                L3s = lambda s: ln3f[:, (sl * nsub + s) * c.D:(sl * nsub + s + 1) * c.D]
                mx = []
                for s in range(nsub):
                    mx.append(dma("sync", X3(s),
                                  X[t0 + s * 128: t0 + 128 * (s + 1), :],
                                  waits=war_x3[sl] if s == 0 else ()))
                pows = []
                for s in range(nsub):
                    ysl = y_acc[:, (t0 // 128 + s) * c.D:(t0 // 128 + s + 1) * c.D]
                    pows.append(ln_stats(ysl, sl * 2 + s,
                                         m_y_done if tb == 0 else []))
                m_x1 = []
                for s in range(nsub):
                    ysl = y_acc[:, (t0 // 128 + s) * c.D:(t0 // 128 + s + 1) * c.D]
                    mn = ln_norm(ysl, L3s(s), sl * 2 + s, pows[s],
                                 extra_waits=war_ln3f[sl * nsub + s])
                    ma = P.do("vector", lambda v, s=s, X3=X3, X1=X1, L3s=L3s:
                              v.tensor_tensor(X1(s), X3(s), L3s(s), AluOpType.add),
                              waits=[("dv", mn), mx[s]]
                              + (war_x1[sl] if s == 0 else []), inc="dv")
                    war_ln3f[sl * nsub + s] = [("dv", ma)]
                    m_x1.append(ma)
                pows = []
                for s in range(nsub):
                    pows.append(ln_stats(X1(s), sl * 2 + s, [("dv", m_x1[s])]))
                norms = []
                for s in range(nsub):
                    L3 = ln3b[:, (sl * nsub + s) * c.D:(sl * nsub + s + 1) * c.D]
                    mn = ln_norm(X1(s), L3, sl * 2 + s, pows[s],
                                 extra_waits=war_ln3b[sl * nsub + s])
                    norms.append(mn)
                return norms

            def emit_ln3T(tb, norms):
                sl = tb % 2
                m_ln3 = None
                for s_ in range(nsub):
                    L3 = ln3b[:, (sl * nsub + s_) * c.D:(sl * nsub + s_ + 1) * c.D]

                    def dst(j0, cn, s_=s_, sl=sl):
                        return apx(ln3T, sl * DCT + j0 * c.TB3 + s_ * 128,
                                   [[c.TB3, cn], [1, 128]])
                    (semn, mev), mtp = transp(L3, c.DC, 0, dst,
                                              wait_first=[("dv", norms[s_])]
                                              + war_ln3T[sl], ident=id_bf)
                    war_ln3b[sl * nsub + s_] = [("pe", mtp)]
                    m_ln3 = mev
                return m_ln3

            f_norms = front(0)
            pend_t = {"norms": f_norms, "tb": 0}
            for tb in range(c.NT3):
                t0 = tb * c.TB3
                sl = tb % 2
                X3 = lambda s, sl=sl: x3[:, (sl * nsub + s) * c.D:
                                         (sl * nsub + s + 1) * c.D]
                X1 = lambda s, sl=sl: x1[:, (sl * nsub + s) * c.D:
                                         (sl * nsub + s + 1) * c.D]
                L3s = lambda s, sl=sl: ln3f[:, (sl * nsub + s) * c.D:
                                            (sl * nsub + s + 1) * c.D]
                if pend_t.get("m_ln3") is None:
                    m_ln3 = emit_ln3T(tb, pend_t["norms"])
                else:
                    m_ln3 = pend_t["m_ln3"]
                for fc in range(c.FC):
                    pb = 2 + (fc % 2)
                    for dc in range(c.DC):
                        w = (BW[pb] + [("dv", m_ln3),
                                       ("dv", ms["s1T"])]) if dc == 0 else ()
                        P.do("tensor", lambda t, fc=fc, dc=dc, pb=pb, sl=sl:
                             t.matmul(
                                 bank(pb, 0, c.TB3),
                                 s1T[:, dc * c.F + fc * 128: dc * c.F + (fc + 1) * 128],
                                 ln3T[:, sl * DCT + dc * c.TB3:
                                      sl * DCT + (dc + 1) * c.TB3],
                                 start=(dc == 0), stop=(dc == c.DC - 1)),
                             waits=w, inc="pe" if dc == c.DC - 1 else None)
                    m_mm = P.cnt["pe"]
                    war_ln3T[sl] = [("pe", m_mm)]
                    mg = P.do("scalar", lambda e, fc=fc, pb=pb: e.activation(
                        hT[:, fc * c.TB3:(fc + 1) * c.TB3], bank(pb, 0, c.TB3),
                        AF.Gelu, scale=alpha1[:, fc:fc + 1], bias=b1s[:, fc:fc + 1]),
                        waits=[("pe", m_mm), ("dv", ms["alpha"])] + m_bload
                        + war_hT["v"], inc="ac")
                    BW[pb] = [("ac", mg)]
                m_h = P.cnt["ac"]
                if tb + 1 < c.NT3:
                    nxt = front(tb + 1)
                    pend_t = {"norms": nxt, "tb": tb + 1, "m_ln3": None}
                for dc in range(c.DC):
                    pb = 4 + (dc % 2)
                    for fc in range(c.FC):
                        w = (BW[pb] + [("ac", m_h), ("dv", ms["s2T"])]) if fc == 0 else ()
                        P.do("tensor", lambda t, fc=fc, dc=dc, pb=pb: t.matmul(
                            bank(pb, 0, c.TB3),
                            s2T[:, fc * c.D + dc * 128: fc * c.D + (dc + 1) * 128],
                            hT[:, fc * c.TB3:(fc + 1) * c.TB3],
                            start=(fc == 0), stop=(fc == c.FC - 1)),
                            waits=w, inc="pe" if fc == c.FC - 1 else None)
                    m_mm = P.cnt["pe"]
                    war_hT["v"] = [("pe", m_mm)]
                    mo = P.do("scalar", lambda e, dc=dc, pb=pb: e.activation(
                        oscr, bank(pb, 0, c.TB3), AF.Identity,
                        scale=alpha2[:, dc:dc + 1], bias=b2s[:, dc:dc + 1]),
                        waits=[("pe", m_mm), ("dv", ms["alpha"])] + m_bload
                        + war_o2, inc="ac")
                    BW[pb] = [("ac", mo)]

                    def dst(j0, cn, dc=dc):
                        return apx(o2tm, j0 * c.D + dc * 128, [[c.D, cn], [1, 128]])
                    (semn, mev), mtp = transp(oscr, nsub, 6, dst,
                                              wait_first=[("ac", mo)], ident=id16)
                    war_o2 = [("pe", mtp)]
                    m_o2 = mev
                    if (dc == min(3, c.DC - 1) and tb + 1 < c.NT3
                            and pend_t["m_ln3"] is None):
                        pend_t["m_ln3"] = emit_ln3T(tb + 1, pend_t["norms"])
                pows = []
                for s in range(nsub):
                    pows.append(ln_stats(o2tm[:, s * c.D:(s + 1) * c.D],
                                         sl * 2 + s, [("dv", m_o2)]))
                for s in range(nsub):
                    mn = ln_norm(o2tm[:, s * c.D:(s + 1) * c.D], L3s(s),
                                 sl * 2 + s, pows[s],
                                 extra_waits=war_ln3f[sl * nsub + s])
                    ma = P.do("vector", lambda v, s=s, X3=X3, X1=X1, L3s=L3s:
                              v.tensor_tensor(X3(s), X1(s), L3s(s), AluOpType.add),
                              waits=[("dv", mn)], inc="dv")
                    war_ln3f[sl * nsub + s] = [("dv", ma)]
                    mo_ = dma("sync", OUT[t0 + s * 128: t0 + (s + 1) * 128, :],
                              X3(s), waits=[("dv", ma)])
                    out_dmas.append(mo_)
                war_x3[sl] = wall(out_dmas[-nsub:])
                war_x1[sl] = [("dv", P.cnt["dv"])]

        m_yg0 = [0]
        phase_A(0)
        phase_B(0)
        m_yg0[0] = P.cnt["ac"]
        phase_A(1)
        phase_B(1)
        M_END = [("pe", P.cnt["pe"]), ("dv", P.cnt["dv"]), ("ac", P.cnt["ac"]),
                 ("gp", P.cnt["gp"])]
        m_y_done = [("dv", P.cnt["dv"]), ("ac", P.cnt["ac"])]
        st2.close()
        s1T = sb("s1T", [128, c.DC * c.F], FP8)
        s2T = sb("s2T", [128, c.FC * c.D], FP8)
        hT = sb("hT", [128, c.FC * c.TB3], BF16)
        o2tm = sb("o2tm", [128, (c.TB3 // 128) * c.D], FP16)
        oscr = sb("oscr", [128, c.TB3], FP16)
        st3 = ExitStack()
        wch = st3.enter_context(nc.sbuf_tensor("wch", [128, 2 * c.D], F32))[:, :]
        wch2 = st3.enter_context(nc.sbuf_tensor("wch2", [128, 2 * c.F], F32))[:, :]
        sgn = st3.enter_context(
            nc.sbuf_tensor("sgn", [128, max(c.D, c.F)], BF16))[:, :]
        p3tensors = dict(s1T=s1T, s2T=s2T, hT=hT, o2tm=o2tm, oscr=oscr,
                         wch=wch, wch2=wch2, sgn=sgn)
        ms = phase3_prologue(M_END, p3tensors)
        m_prol = [("pe", P.cnt["pe"]), ("dv", P.cnt["dv"]), ("ac", P.cnt["ac"])]
        st3.close()
        nsub3 = c.TB3 // 128
        p3tensors["x3"] = sb("x3", [128, 2 * nsub3 * c.D], F32)
        p3tensors["x1"] = p3tensors["x3"]
        p3tensors["ln3f"] = sb("ln3f", [128, 2 * (c.TB3 // 128) * c.D], F32)
        p3tensors["ln3b"] = sb("ln3b", [128, 2 * (c.TB3 // 128) * c.D], BF16)
        p3tensors["lnscr"] = None
        p3tensors["ln3T"] = sb("ln3T", [128, 2 * c.DC * c.TB3], BF16)
        p3tensors["m_y_done"] = m_y_done
        p3tensors["m_prol"] = m_prol
        phase3(M_END, ms, p3tensors)

        P.do("sync", lambda e: e.nop(), waits=wall(phase3.out_dmas))

        P.emit(block, semh, first_waits=[("gp", m_init)])
    return nc


def kernel(L0_attn_x, L0_attn_w_qkv, L0_attn_w_out, L0_ffn_w1, L0_ffn_b1,
           L0_ffn_w2, L0_ffn_b2):
    x = np.ascontiguousarray(np.asarray(L0_attn_x, dtype=np.float32))
    wqkv = np.ascontiguousarray(np.asarray(L0_attn_w_qkv, dtype=np.float32))
    wout = np.ascontiguousarray(np.asarray(L0_attn_w_out, dtype=np.float32))
    w1 = np.ascontiguousarray(np.asarray(L0_ffn_w1, dtype=np.float32))
    b1 = np.ascontiguousarray(np.asarray(L0_ffn_b1, dtype=np.float32))
    w2 = np.ascontiguousarray(np.asarray(L0_ffn_w2, dtype=np.float32))
    b2 = np.ascontiguousarray(np.asarray(L0_ffn_b2, dtype=np.float32))

    cfg = Cfg(T=4096, D=1024, H=16, F=4096)
    nc = build_nc(cfg)
    in_maps = []
    for b in range(8):
        in_maps.append({"x": x[b], "w_qkv": wqkv, "w_out": wout,
                        "w1": w1, "b1": b1, "w2": w2, "b2": b2})
    res = run_bass_kernel_spmd(nc, in_maps, core_ids=list(range(8)))
    out = np.stack([res.results[b]["out"] for b in range(8)], axis=0)
    return out.astype(np.float32)


# version 31
# speedup vs baseline: 1.2193x; 1.0310x over previous
import math
from contextlib import ExitStack

import numpy as np
import concourse.bass as bass
import concourse.mybir as mybir
from concourse.alu_op_type import AluOpType
from concourse.bass_utils import run_bass_kernel_spmd

F32 = mybir.dt.float32
BF16 = mybir.dt.bfloat16
FP16 = mybir.dt.float16
FP8 = mybir.dt.float8e4
AF = mybir.ActivationFunctionType
AX = mybir.AxisListType

ENGINES = ("tensor", "vector", "scalar", "gpsimd", "sync")
SEMS = (("pe", "dv", "ac", "gp", "out")
        + tuple(f"qs{i}" for i in range(8)) + tuple(f"qg{i}" for i in range(8)))


def wall(tuples):
    d = {}
    for s_, v in tuples:
        if v:
            d[s_] = max(d.get(s_, 0), v)
    return list(d.items())


class Cfg:
    def __init__(self, T=4096, D=1024, H=16, F=4096):
        self.T, self.D, self.H, self.F = T, D, H, F
        self.DH = 64
        assert H * self.DH == D
        self.DC = D // 128
        self.FC = F // 128
        self.KT = T // 128
        self.G = 2
        self.HG = H // 2
        self.CH = self.HG * 64 // 128
        self.TBLK = 512
        self.NTB = T // self.TBLK
        self.TB3 = 256
        self.NT3 = T // self.TB3
        self.VW = self.HG * 65
        self.scale = 1.0 / math.sqrt(D)
        self.kgroups = []
        k = 0
        while k < self.KT:
            n = min(3, self.KT - k)
            self.kgroups.append((k, n))
            k += n


def apx(view, offset, dims):
    p = view.ap[0]
    return bass.AP(view.tensor, offset, [[p[0], p[1]]] + [list(d) for d in dims])


class Prog:
    def __init__(self):
        self.ops = {e: [] for e in ENGINES}
        self.cnt = {s: 0 for s in SEMS}

    def do(self, eng, fn, waits=(), inc=None, n=1):
        self.ops[eng].append((fn, tuple(waits), inc, n))
        if inc is not None:
            self.cnt[inc] += n
            return self.cnt[inc]
        return None

    def emit(self, block, sems, first_waits=()):
        for ename in ENGINES:
            def body(eng, ename=ename):
                waited = {}
                if ename != "gpsimd":
                    for s, v in first_waits:
                        if v:
                            eng.wait_ge(sems[s], v)
                            waited[s] = v
                for fn, waits, inc, n in self.ops[ename]:
                    for s, v in waits:
                        if v and waited.get(s, 0) < v:
                            eng.wait_ge(sems[s], v)
                            waited[s] = v
                    instr = fn(eng)
                    if inc is not None:
                        instr.then_inc(sems[inc], n)
            getattr(block, ename)(body)


def build_nc(cfg: Cfg):
    c = cfg
    nc = bass.Bass("TRN2", target_bir_lowering=False)

    X = nc.dram_tensor("x", [c.T, c.D], F32, kind="ExternalInput").ap()
    Wqkv = nc.dram_tensor("w_qkv", [c.D, 3 * c.D], F32, kind="ExternalInput").ap()
    Wout = nc.dram_tensor("w_out", [c.D, c.D], F32, kind="ExternalInput").ap()
    W1 = nc.dram_tensor("w1", [c.F, c.D], F32, kind="ExternalInput").ap()
    B1 = nc.dram_tensor("b1", [c.F], F32, kind="ExternalInput").ap()
    W2 = nc.dram_tensor("w2", [c.D, c.F], F32, kind="ExternalInput").ap()
    B2 = nc.dram_tensor("b2", [c.D], F32, kind="ExternalInput").ap()
    OUT = nc.dram_tensor("out", [c.T, c.D], F32, kind="ExternalOutput").ap()

    P = Prog()
    with ExitStack() as st:
        def sb(name, shape, dtype):
            h = st.enter_context(nc.sbuf_tensor(name, shape, dtype))
            return h[tuple(slice(None) for _ in shape)]

        id32 = sb("id32", [128, 128], F32)
        id_bf = sb("id_bf", [128, 128], BF16)
        id16 = sb("id16", [128, 128], FP16)
        ones16 = sb("ones16", [1, 128], FP16)
        ones_bf = sb("ones_bf", [128, 1], BF16)
        eps_t = sb("eps_t", [128, 1], F32)
        mhalf = sb("mhalf", [128, 1], F32)
        alpha1 = sb("alpha1", [128, c.FC], F32)
        alpha2 = sb("alpha2", [128, c.DC], F32)
        b1s = sb("b1s", [128, c.FC], F32)
        b2s = sb("b2s", [128, c.DC], F32)
        y_acc = sb("y_acc", [128, (c.T // 128) * c.D], FP16)
        kmaxs = sb("kmaxs", [1, 2], F32)

        stats = sb("stats", [128, 80], F32)
        qmax4 = sb("qmax4", [128, 8], F32)
        sc1 = sb("sc1", [1, 8], F32)
        negBv = sb("negBv", [128, 1], F32)

        st2 = ExitStack()

        def sb2(name, shape, dtype):
            h = st2.enter_context(nc.sbuf_tensor(name, shape, dtype))
            return h[tuple(slice(None) for _ in shape)]

        kT = sb2("kT", [128, c.CH * c.T], BF16)
        vP = sb2("vP", [128, c.KT * c.VW], BF16)
        wsh = sb2("wsh", [128, c.DC * c.HG * 64 + (c.HG // 2) * c.D], BF16)
        wkv = wsh[:, 0:c.DC * c.HG * 128]
        wq = wsh[:, 0:c.DC * c.HG * 64]
        wo = wsh[:, c.DC * c.HG * 64:]
        xs = sb2("xs", [128, 4 * c.D], F32)
        lnx16 = sb2("lnx16", [128, 4 * c.D], BF16)
        lnxT = sb2("lnxT", [128, c.DC * c.TBLK], BF16)
        qT = sb2("qT", [128, 2 * c.CH * c.TBLK], BF16)
        oT = sb2("oT", [128, (c.HG // 2) * c.TBLK], BF16)
        pT = sb2("pT", [128, 4 * 1024], BF16)
        escr = sb2("escr", [128, 1024], F32)
        otmp = escr[0:64, :]
        recl = escr[64:65, :]
        vtmp16 = pT[:, 0:512]
        recl16 = sb2("recl16", [1, 520], FP16)

        psh = st.enter_context(nc.psum_tensor("ps", [128, 4096], F32))
        ps = psh[:, :]

        def bank(i, lo=0, width=512):
            return ps[:, 512 * i + lo: 512 * i + lo + width]

        semh = {s: st.enter_context(nc.semaphore(s)) for s in SEMS}
        block = st.enter_context(nc.Block())

        SC = c.scale
        BW = {i: [] for i in range(8)}

        rr = {"sync": 0, "gpsimd": 0}

        def dma(q, out, in_, waits=()):
            pre = "qs" if q == "sync" else "qg"
            sem = f"{pre}{rr[q] % 8}"
            rr[q] += 1
            chain = [(sem, P.cnt[sem])]
            v = P.do(q, lambda e, o=out, i=in_: e.dma_start(out=o, in_=i),
                     waits=list(waits) + chain, inc=sem, n=16)
            return (sem, v)

        m0 = P.do("gpsimd", lambda g: g.memset(id32, 0.0), inc="gp")

        P.do("gpsimd", lambda g: g.affine_select(
            out=id32, in_=id32, compare_op=AluOpType.not_equal, fill=1.0,
            base=0, pattern=[[-1, 128]], channel_multiplier=1),
            waits=[("gp", m0)], inc="gp")
        mib = P.do("gpsimd", lambda g: g.memset(id_bf, 0.0), inc="gp")
        P.do("gpsimd", lambda g: g.affine_select(
            out=id_bf, in_=id_bf, compare_op=AluOpType.not_equal, fill=1.0,
            base=0, pattern=[[-1, 128]], channel_multiplier=1),
            waits=[("gp", mib)], inc="gp")
        mih = P.do("gpsimd", lambda g: g.memset(id16, 0.0), inc="gp")
        P.do("gpsimd", lambda g: g.affine_select(
            out=id16, in_=id16, compare_op=AluOpType.not_equal, fill=1.0,
            base=0, pattern=[[-1, 128]], channel_multiplier=1),
            waits=[("gp", mih)], inc="gp")
        P.do("gpsimd", lambda g: g.memset(ones16, 1.0), inc="gp")
        P.do("gpsimd", lambda g: g.memset(ones_bf, 1.0), inc="gp")
        P.do("gpsimd", lambda g: g.memset(eps_t, 1e-5), inc="gp")
        P.do("gpsimd", lambda g: g.memset(mhalf, -0.5), inc="gp")
        P.do("gpsimd", lambda g: g.memset(
            apx(vP, 64, [[c.VW, c.KT], [65, c.HG]]), 1.0), inc="gp")
        m_init = P.do("gpsimd", lambda g: g.memset(oT, 0.0), inc="gp")

        m_bload = []
        for fc in range(c.FC):
            m_bload.append(dma("sync", b1s[:, fc:fc + 1],
                               bass.AP(B1.tensor, fc * 128, [[1, 128], [1, 1]])))
        for dc in range(c.DC):
            m_bload.append(dma("sync", b2s[:, dc:dc + 1],
                               bass.AP(B2.tensor, dc * 128, [[1, 128], [1, 1]])))
        m_bload = wall(m_bload)

        slot_pow = {}
        slot_norm = {}

        def ln_stats(src, slot, wait_src):
            o = slot * 20
            ts_ = min(512, c.D)
            nh = c.D // ts_
            for i in range(nh):
                P.do("vector",
                     lambda v, i=i, o=o, s=src, ts_=ts_: v.bn_stats(
                         stats[:, o + 6 * i:o + 6 * i + 6],
                         s[:, ts_ * i:ts_ * (i + 1)]),
                     waits=list(wait_src) + [("gp", slot_pow.get(slot, 0)),
                                             ("dv", slot_norm.get(slot, 0))]
                     if i == 0 else (),
                     inc="dv" if i == nh - 1 else None)
            m1 = P.cnt["dv"]
            m2 = P.do("vector", lambda v, o=o, n=nh: v.bn_aggr(
                stats[:, o + 12:o + 14], stats[:, o:o + 6 * n]),
                waits=[("dv", m1)], inc="dv")
            m3 = P.do("vector", lambda v, o=o: v.tensor_scalar(
                stats[:, o + 14:o + 15], stats[:, o + 13:o + 14], eps_t[:, 0:1],
                None, AluOpType.add, AluOpType.bypass),
                waits=[("dv", m2), ("gp", slot_pow.get(slot, 0))], inc="dv")
            m4 = P.do("gpsimd", lambda g, o=o: g.tensor_tensor(
                stats[:, o + 15:o + 16], stats[:, o + 14:o + 15], mhalf,
                AluOpType.pow),
                waits=[("dv", m3), ("gp", m_init),
                       ("dv", slot_norm.get(slot, 0))], inc="gp")
            slot_pow[slot] = m4
            return m4

        def ln_norm(src, dst, slot, m4, extra_waits=()):
            o = slot * 20
            m5 = P.do("vector", lambda v, o=o: v.scalar_tensor_tensor(
                stats[:, o + 16:o + 17], stats[:, o + 12:o + 13], -1.0,
                stats[:, o + 15:o + 16], AluOpType.mult, AluOpType.mult),
                waits=[("gp", m4)], inc="dv")
            m6 = P.do("vector", lambda v, o=o, s=src, d=dst: v.tensor_scalar(
                d, s, stats[:, o + 15:o + 16], stats[:, o + 16:o + 17],
                AluOpType.mult, AluOpType.add),
                waits=[("dv", m5)] + list(extra_waits), inc="dv")
            slot_norm[slot] = m6
            return m6

        def transp(src, n, b0, dst_of, wait_first, evac_eng="vector", nb=2,
                   ident=None):
            sem = "dv" if evac_eng == "vector" else "ac"
            idt = id32 if ident is None else ident
            step = 4
            last = None
            g0 = 0
            while g0 < n:
                g1 = min(g0 + step, n)
                bi = b0 + ((g0 // 4) % nb if step == 4 else 0)
                bw = BW[bi] if step == 4 else BW[b0] + BW[b0 + 1]
                for j in range(g0, g1):
                    w = []
                    if j == 0:
                        w += list(wait_first)
                    if j == g0:
                        w += bw
                    P.do("tensor", lambda t, j=j, bi=bi, g0=g0, s=src, idt=idt:
                         (t.transpose(
                             bank(bi + (j - g0) // 4, 128 * ((j - g0) % 4), 128),
                             s[:, 128 * j:128 * (j + 1)], idt)
                          if idt is id32 else
                          t.matmul(
                             bank(bi + (j - g0) // 4, 128 * ((j - g0) % 4), 128),
                             s[:, 128 * j:128 * (j + 1)], idt,
                             start=True, stop=True)),
                         waits=w, inc="pe" if j == g1 - 1 else None)
                mpe = P.cnt["pe"]
                cn = g1 - g0
                if evac_eng == "vector":
                    last = P.do("vector", lambda v, bi=bi, g0=g0, cn=cn: v.tensor_copy(
                        dst_of(g0, cn), bank(bi, 0, 128 * cn)),
                        waits=[("pe", mpe)], inc="dv")
                else:
                    last = P.do("scalar", lambda e, bi=bi, g0=g0, cn=cn: e.activation(
                        dst_of(g0, cn), bank(bi, 0, 128 * cn), AF.Copy),
                        waits=[("pe", mpe)], inc="ac")
                if step == 4:
                    BW[bi] = [(sem, last)]
                else:
                    BW[b0] = [(sem, last)]
                    BW[b0 + 1] = [(sem, last)]
                g0 = g1
            return (sem, last), P.cnt["pe"]

        st_blk = {"xw": [[] for _ in range(4)],
                  "l16w": [[] for _ in range(4)],
                  "lnxT_users": [],
                  "m_lnxT_sem": "dv", "evac_eng": "vector"}

        def ln1_part1(t0_base):
            nsub = c.TBLK // 128
            pows, loads = [], []
            for s in range(nsub):
                mx = dma("sync", xs[:, s * c.D:(s + 1) * c.D],
                         X[t0_base + s * 128: t0_base + s * 128 + 128, :],
                         waits=st_blk["xw"][s])
                loads.append(mx)
            for s in range(nsub):
                pows.append(ln_stats(xs[:, s * c.D:(s + 1) * c.D], s, [loads[s]]))
            norms = []
            for s in range(nsub):
                xsl = xs[:, s * c.D:(s + 1) * c.D]
                l16 = lnx16[:, s * c.D:(s + 1) * c.D]
                mn = ln_norm(xsl, l16, s, pows[s],
                             extra_waits=st_blk["l16w"][s])
                st_blk["xw"][s] = [("dv", mn)]
                norms.append(mn)
            return norms

        def ln1_part2(norms, b0=0):
            nsub = c.TBLK // 128
            mlast = None
            for s in range(nsub):
                l16 = lnx16[:, s * c.D:(s + 1) * c.D]
                def dst(j0, cn, s=s):
                    return apx(lnxT, j0 * c.TBLK + s * 128, [[c.TBLK, cn], [1, 128]])
                (semn, mev), mtp = transp(
                    l16, c.DC, b0, dst,
                    wait_first=[("dv", norms[s])]
                    + [("pe", u) for u in st_blk["lnxT_users"][-1:]],
                    evac_eng=st_blk["evac_eng"], ident=id_bf)
                st_blk["l16w"][s] = [("pe", mtp)]
                st_blk["m_lnxT_sem"] = semn
                mlast = mev
            return mlast

        war_wkv = {"v": []}
        war_kT = {"v": []}
        war_vP = {"v": []}
        m_kv_done = {}

        def phase_A(g):
            W2G = c.HG * 128
            m_w = []
            for dc in range(c.DC):
                m_w.append(dma("gpsimd", wkv[:, dc * W2G: dc * W2G + c.HG * 64],
                    Wqkv[dc * 128:(dc + 1) * 128,
                         c.D + g * c.HG * 64: c.D + (g + 1) * c.HG * 64],
                    waits=(war_wkv["v"] + [("gp", m_init)]) if dc == 0 else ()))
                m_w.append(dma("gpsimd", wkv[:, dc * W2G + c.HG * 64: (dc + 1) * W2G],
                    Wqkv[dc * 128:(dc + 1) * 128,
                         2 * c.D + g * c.HG * 64: 2 * c.D + (g + 1) * c.HG * 64]))
            m_w = wall(m_w)
            mk0 = P.do("vector", lambda v: v.memset(qmax4[:, 7:8], 0.0),
                       waits=war_kT["v"], inc="dv")
            kmax_run = mk0

            nmk = c.HG * 128 // 128
            nK = nmk // 2
            BW_vtmp[:] = [("dv", P.cnt["dv"]), ("pe", P.cnt["pe"])]
            st_blk["evac_eng"] = "scalar"
            norms = ln1_part1(0)
            for tb in range(c.NTB):
                m_lnxT = ln1_part2(norms)
                if tb + 1 < c.NTB:
                    norms = ln1_part1((tb + 1) * c.TBLK)
                for mc in range(nmk):
                    pb = 2 + (mc % 2)
                    is_v = mc >= nK
                    for dc in range(c.DC):
                        w = (BW[pb] + m_w
                             + [(st_blk["m_lnxT_sem"], m_lnxT)]) if dc == 0 else ()
                        P.do("tensor", lambda t, dc=dc, mc=mc, pb=pb: t.matmul(
                            bank(pb),
                            wkv[:, dc * W2G + mc * 128: dc * W2G + (mc + 1) * 128],
                            lnxT[:, dc * c.TBLK:(dc + 1) * c.TBLK],
                            start=(dc == 0), stop=(dc == c.DC - 1)),
                            waits=w, inc="pe" if dc == c.DC - 1 else None)
                    m_mm = P.cnt["pe"]
                    st_blk["lnxT_users"] = [m_mm]
                    if not is_v:
                        mea = P.do("scalar", lambda e, mc=mc, tb=tb, pb=pb: e.activation(
                            kT[:, mc * c.T + tb * c.TBLK: mc * c.T + (tb + 1) * c.TBLK],
                            bank(pb), AF.Copy),
                            waits=[("pe", m_mm)] + war_kT["v"], inc="ac")
                        mr = P.do("vector", lambda v, mc=mc, tb=tb: v.tensor_reduce(
                            qmax4[:, 6:7],
                            kT[:, mc * c.T + tb * c.TBLK: mc * c.T + (tb + 1) * c.TBLK],
                            axis=AX.X, op=AluOpType.max,
                            apply_absolute_value=True),
                            waits=[("ac", mea), ("dv", kmax_run)], inc="dv")
                        kmax_run = P.do("vector", lambda v: v.tensor_tensor(
                            qmax4[:, 7:8], qmax4[:, 7:8], qmax4[:, 6:7],
                            AluOpType.max), waits=[("dv", mr)], inc="dv")
                        BW[pb] = [("ac", mea)]
                    else:
                        mev = P.do("scalar", lambda e, pb=pb: e.activation(
                            vtmp16, bank(pb), AF.Copy),
                            waits=[("pe", m_mm), BW_vtmp[0]] if BW_vtmp else [("pe", m_mm)],
                            inc="ac")
                        BW[pb] = [("ac", mev)]
                        mcv = mc - nK
                        vb = 4 + (mcv % 2)
                        nsub = c.TBLK // 128
                        for j in range(nsub):
                            w = ([("ac", mev)] + BW[vb]) if j == 0 else ()
                            P.do("tensor", lambda t, j=j, vb=vb: t.matmul(
                                bank(vb, 128 * j, 128),
                                vtmp16[:, 128 * j:128 * (j + 1)], id_bf,
                                start=True, stop=True),
                                waits=w, inc="pe" if j == nsub - 1 else None)
                        m_vt = P.cnt["pe"]
                        BW_vtmp[:] = [("pe", m_vt)]
                        mlv = None
                        for h2 in range(2):
                            hl = 2 * mcv + h2
                            mlv = P.do("vector", lambda v, h2=h2, hl=hl, tb=tb,
                                       nsub=nsub, vb=vb:
                                       v.tensor_copy(
                                           apx(vP, tb * nsub * c.VW + hl * 65,
                                               [[c.VW, nsub], [1, 64]]),
                                           apx(ps, 512 * vb + h2 * 64,
                                               [[128, nsub], [1, 64]])),
                                       waits=([("pe", m_vt)] + war_vP["v"]) if h2 == 0 else (),
                                       inc="dv")
                        BW[vb] = [("dv", mlv)]
            war_wq["v"] = [("pe", P.cnt["pe"])]
            w5 = BW[5]
            P.do("tensor", lambda t: t.transpose(bank(5, 0, 128)[0:1, :],
                                                 qmax4[:, 7:8], id32),
                 waits=[("dv", kmax_run)] + w5, inc="pe")
            mt = P.cnt["pe"]
            mk = P.do("vector", lambda v, g=g: v.tensor_reduce(
                kmaxs[:, g:g + 1], bank(5)[0:1, 0:128], axis=AX.X,
                op=AluOpType.max), waits=[("pe", mt)], inc="dv")
            BW[5] = [("dv", mk)]
            m_kv_done[g] = mk

        BW_vtmp = []

        war_qT = {"v": []}
        war_oT = {"v": []}
        war_wq = {"v": []}
        war_otmp = {0: [], 1: []}
        war_recl = {0: [], 1: []}
        war_recl16 = {"v": []}
        tailstate = {"fn": None}
        prolstate = {"chunks": [], "m_qT": 0}
        bgq = []
        bgq_meta = {"ob0": -1}
        pvstate = {"m_pv_0": 0, "m_pv_1": 0}
        ringstate = {"exp": {}, "pv": {}, "i": 0, "pgc": 0}

        def phase_B(g):
            m_w = []
            for dc in range(c.DC):
                m_w.append(dma("gpsimd", wq[:, dc * c.HG * 64:(dc + 1) * c.HG * 64],
                    Wqkv[dc * 128:(dc + 1) * 128,
                         g * c.HG * 64:(g + 1) * c.HG * 64],
                    waits=war_wq["v"] if dc == 0 else ()))
            for hp in range(c.HG // 2):
                m_w.append(dma("gpsimd", wo[:, hp * c.D:(hp + 1) * c.D],
                    Wout[(g * c.HG + 2 * hp) * 64:
                         (g * c.HG + 2 * hp) * 64 + 128, :],
                    waits=([("gp", m_init)] + war_wq["v"]) if hp == 0 else ()))
            m_w = wall(m_w)
            m_pw = [("pe", P.cnt["pe"])]
            m_sq = P.do("scalar", lambda e: e.activation(
                lnxT[:, 0:c.DC * c.HG * 64], wq, AF.Square),
                waits=m_w + m_pw + [("ac", P.cnt["ac"])], inc="ac")
            NQ = c.HG * 64
            for dc in range(c.DC):
                w = ([("ac", m_sq)] + BW[0]) if dc == 0 else ()
                P.do("tensor", lambda t, dc=dc: t.matmul(
                    bank(0, 0, NQ)[0:1, :], ones_bf,
                    lnxT[:, dc * NQ:(dc + 1) * NQ],
                    start=(dc == 0), stop=(dc == c.DC - 1)),
                    waits=w, inc="pe" if dc == c.DC - 1 else None)
            m_cn = P.cnt["pe"]
            st_blk["lnxT_users"] = [m_cn]
            pvstate["pT_free"] = 0
            mr2 = P.do("vector", lambda v: v.tensor_reduce(
                sc1[:, 2:3], bank(0, 0, NQ)[0:1, :], axis=AX.X,
                op=AluOpType.max), waits=[("pe", m_cn)], inc="dv")
            BW[0] = [("dv", mr2)]
            ml1 = P.do("scalar", lambda e: e.activation(
                sc1[:, 3:4], sc1[:, 2:3], AF.Ln), waits=[("dv", mr2)], inc="ac")
            ml2 = P.do("scalar", lambda e: e.activation(
                sc1[:, 3:4], sc1[:, 3:4], AF.Exp, scale=0.5),
                waits=[("ac", ml1)], inc="ac")
            mr3 = P.do("vector", lambda v, g=g: v.scalar_tensor_tensor(
                sc1[:, 1:2], sc1[:, 3:4],
                -float(c.DH) * SC * math.sqrt(c.D), kmaxs[:, g:g + 1],
                AluOpType.mult, AluOpType.mult),
                waits=[("ac", ml2), ("dv", m_kv_done[g])], inc="dv")
            mr4 = P.do("vector", lambda v: v.tensor_copy(
                recl16[0:1, 512:513], sc1[:, 1:2]),
                waits=[("dv", mr3)] + war_recl16["v"], inc="dv")
            w1b = BW[1]
            P.do("tensor", lambda t: t.matmul(
                bank(1, 0, 1), ones16, recl16[0:1, 512:513],
                start=True, stop=True),
                waits=[("dv", mr4)] + w1b, inc="pe")
            mbc = P.cnt["pe"]
            m_negB = P.do("vector", lambda v: v.tensor_copy(
                negBv, bank(1, 0, 1)),
                waits=[("pe", mbc), ("ac", P.cnt["ac"])], inc="dv")
            BW[1] = BW[1] + [("dv", m_negB)]

            def emit_prologueB(qb, norms, g=None, pbanks=(2, 3)):
                m_lnxT = ln1_part2(norms, b0=pbanks[0])
                mq = 0
                qsl = (qb % 2) * c.CH * c.TBLK
                for mc in range(c.CH):
                    pb = pbanks[mc % 2]
                    for dc in range(c.DC):
                        w = (BW[pb] + m_w + [("dv", m_lnxT)]) if dc == 0 else ()
                        P.do("tensor", lambda t, dc=dc, mc=mc, pb=pb: t.matmul(
                            bank(pb),
                            wq[:, dc * c.HG * 64 + mc * 128: dc * c.HG * 64 + (mc + 1) * 128],
                            lnxT[:, dc * c.TBLK:(dc + 1) * c.TBLK],
                            start=(dc == 0), stop=(dc == c.DC - 1)),
                            waits=w, inc="pe" if dc == c.DC - 1 else None)
                    m_mm = P.cnt["pe"]
                    st_blk["lnxT_users"] = [m_mm]
                    if mc % 2 == 1 or mc == c.CH - 1:
                        nmc = 2 if mc % 2 == 1 else 1
                        mc0 = mc - nmc + 1
                        me = P.do("vector", lambda v, mc0=mc0, nmc=nmc, qsl=qsl:
                                  v.tensor_copy(
                                      qT[:, qsl + mc0 * c.TBLK:
                                         qsl + (mc0 + nmc) * c.TBLK],
                                      bank(pbanks[0] if mc0 % 2 == 0 else pbanks[1],
                                           0, nmc * 512)),
                                  waits=[("pe", m_mm)] + war_qT["v"], inc="dv")
                        mq = me
                        BW[pbanks[0]] = [("dv", me)]
                        BW[pbanks[1]] = [("dv", me)]
                return mq

            def prologueB_chunks(qb, norms, g=None, pbanks=(4, 5)):
                state = {"m_lnxT": 0}
                cl = []
                nsub = c.TBLK // 128
                qsl = (qb % 2) * c.CH * c.TBLK
                for s in range(nsub):
                    def tch(s=s):
                        l16 = lnx16[:, s * c.D:(s + 1) * c.D]
                        def dst(j0, cn, s=s):
                            return apx(lnxT, j0 * c.TBLK + s * 128,
                                       [[c.TBLK, cn], [1, 128]])
                        (semn, mev), mtp = transp(
                            l16, c.DC, pbanks[0], dst,
                            wait_first=[("dv", norms[s])]
                            + [("pe", u) for u in st_blk["lnxT_users"][-1:]],
                            evac_eng=st_blk["evac_eng"], ident=id_bf)
                        st_blk["l16w"][s] = [("pe", mtp)]
                        st_blk["m_lnxT_sem"] = semn
                        state["m_lnxT"] = mev
                    cl.append(tch)
                for mc in range(c.CH):
                    def qch(mc=mc):
                        pb = pbanks[mc % 2]
                        for dc in range(c.DC):
                            w = (BW[pb] + m_w
                                 + [(st_blk["m_lnxT_sem"],
                                     state["m_lnxT"])]) if dc == 0 else ()
                            P.do("tensor", lambda t, dc=dc, mc=mc, pb=pb:
                                 t.matmul(
                                     bank(pb),
                                     wq[:, dc * c.HG * 64 + mc * 128:
                                        dc * c.HG * 64 + (mc + 1) * 128],
                                     lnxT[:, dc * c.TBLK:(dc + 1) * c.TBLK],
                                     start=(dc == 0), stop=(dc == c.DC - 1)),
                                 waits=w, inc="pe" if dc == c.DC - 1 else None)
                        m_mm = P.cnt["pe"]
                        st_blk["lnxT_users"] = [m_mm]
                        if mc % 2 == 1 or mc == c.CH - 1:
                            nmc = 2 if mc % 2 == 1 else 1
                            mc0 = mc - nmc + 1
                            me = P.do("vector",
                                      lambda v, mc0=mc0, nmc=nmc, qsl=qsl:
                                      v.tensor_copy(
                                          qT[:, qsl + mc0 * c.TBLK:
                                             qsl + (mc0 + nmc) * c.TBLK],
                                          bank(pbanks[0] if mc0 % 2 == 0
                                               else pbanks[1], 0, nmc * 512)),
                                      waits=[("pe", m_mm)] + war_qT["v"],
                                      inc="dv")
                            prolstate["m_qT"] = me
                            BW[pbanks[0]] = [("dv", me)]
                            BW[pbanks[1]] = [("dv", me)]
                    cl.append(qch)
                return cl

            def emit_outproj(qb, g, ob0, spread=True):
                m_oT = P.cnt["dv"]
                nsub = c.TBLK // 128
                OW = min(512, c.D)
                ops = []
                for tc in range(nsub):
                    for oh in range(c.D // OW):
                        pb = ob0 + (tc * (c.D // OW) + oh) % 2
                        grp = []
                        for hl in range(c.HG // 2):
                            first = hl == 0
                            last = hl == c.HG // 2 - 1

                            def mm(tc=tc, oh=oh, hl=hl, pb=pb, first=first,
                                   last=last, qb=qb, g=g):
                                w = ([("dv", m_oT)] + m_w + BW[pb]) if first else ()
                                P.do("tensor", lambda t: t.matmul(
                                    bank(pb, 0, OW),
                                    oT[:, hl * c.TBLK + tc * 128:
                                       hl * c.TBLK + (tc + 1) * 128],
                                    wo[:, hl * c.D + oh * OW:
                                       hl * c.D + (oh + 1) * OW],
                                    start=first, stop=last),
                                    waits=w, inc="pe" if last else None)
                                if last:
                                    m_mm = P.cnt["pe"]
                                    war_oT["v"] = [("pe", m_mm)]
                                    dsty = y_acc[:, (qb * nsub + tc) * c.D + oh * OW:
                                                 (qb * nsub + tc) * c.D + (oh + 1) * OW]
                                    if g == 0:
                                        mev = P.do("vector",
                                                   lambda v, d=dsty, pb=pb: v.tensor_copy(
                                                       d, bank(pb, 0, OW)),
                                                   waits=[("pe", m_mm)], inc="dv")
                                    else:
                                        mev = P.do("vector",
                                                   lambda v, d=dsty, pb=pb: v.tensor_tensor(
                                                       d, d, bank(pb, 0, OW),
                                                       AluOpType.add),
                                                   waits=[("pe", m_mm),
                                                          ("ac", m_yg0[0])], inc="dv")
                                    BW[pb] = [("dv", mev)]
                            grp.append(mm)
                        ops.append(lambda grp=grp: [f() for f in grp])
                if spread:
                    bgq_meta["ob0"] = ob0
                    bgq.extend(ops)
                else:
                    for op in ops:
                        op()

            st_blk["evac_eng"] = "vector"
            for _h in range(2):            st_blk["evac_eng"] = "vector"
            for _h in range(2):            st_blk["evac_eng"] = "vector"
            for _h in range(2):            st_blk["evac_eng"] = "vector"
            for _h in range(2):
                war_otmp[_h] = war_otmp[_h] + [("ac", P.cnt["ac"])]
                war_recl[_h] = war_recl[_h] + [("ac", P.cnt["ac"])]
            normsB = ln1_part1(0)
            prol = emit_prologueB(0, normsB, g=g)
            for qb in range(c.NTB):
                m_qT = prol
                if qb + 1 < c.NTB:
                    normsB = ln1_part1((qb + 1) * c.TBLK)
                    if c.CH >= 3:
                        prolstate["chunks"] = prologueB_chunks(
                            qb + 1, normsB, g=g, pbanks=(4, 5))
                def emit_epilogue(p, obase, m_pv):
                    mults = []
                    for half in range(2):
                        ob = obase + half
                        mo = P.do("vector", lambda v, ob=ob, half=half:
                                  v.tensor_copy(
                                      otmp[:, half * 512:(half + 1) * 512],
                                      bank(ob)[0:64, :]),
                                  waits=[("pe", m_pv)]
                                  + war_otmp[half], inc="dv")
                        mrc = P.do("vector", lambda v, ob=ob, half=half:
                                   v.reciprocal(
                                       recl[0:1, half * 512:(half + 1) * 512],
                                       bank(ob)[64:65, :]),
                                   waits=[("pe", m_pv),
                                          ("dv", mo)] + war_recl[half], inc="dv")
                        mrc2 = P.do("vector", lambda v, half=half: v.tensor_copy(
                            recl16[0:1, 0:512],
                            recl[0:1, half * 512:(half + 1) * 512]),
                            waits=[("dv", mrc)] + war_recl16["v"], inc="dv")
                        P.do("tensor", lambda t, ob=ob: t.matmul(
                            bank(ob), ones16,
                            recl16[0:1, 0:512],
                            start=True, stop=True),
                            waits=[("dv", mrc2)], inc="pe")
                        mb = P.cnt["pe"]
                        war_recl16["v"] = [("pe", mb)]
                        mm_ = P.do("vector", lambda v, ob=ob, half=half, p=p:
                                   v.tensor_tensor(
                                       oT[64 * half:64 * half + 64,
                                          p * c.TBLK:(p + 1) * c.TBLK],
                                       otmp[:, half * 512:(half + 1) * 512],
                                       bank(ob)[0:64, :], AluOpType.mult),
                                   waits=[("pe", mb)] + war_oT["v"], inc="dv")
                        BW[ob] = [("dv", mm_)]
                        war_otmp[half] = [("dv", mm_)]
                        war_recl[half] = [("pe", mb)]
                        mults.append(mm_)
                    return mults

                pending_epi = None
                EH, PH = ringstate["exp"], ringstate["pv"]
                qsl = (qb % 2) * c.CH * c.TBLK
                for p in range(c.CH):
                    hlA, hlB = 2 * p, 2 * p + 1
                    obase = 6 if (ringstate["pgc"] % 2 == 0) else 4
                    ringstate["pgc"] += 1
                    base = ringstate["i"]
                    if obase == bgq_meta["ob0"]:
                        while bgq:
                            bgq.pop(0)()

                    def emit_qk(t_, p=p, base=base, qsl=qsl):
                        gi = base + t_
                        sp = gi % 2
                        for half in range(2):
                            rb = 2 * sp + half
                            r0 = 64 * half
                            w = []
                            if half == 0:
                                w = [("ac", EH.get(gi - 2, 0))]
                                if t_ == 0:
                                    w += [("dv", m_qT)] + BW[0] + BW[1]
                                elif t_ == 1:
                                    w += BW[2] + BW[3]
                            P.do("tensor",
                                 lambda t, t_=t_, p=p, r0=r0, rb=rb, qsl=qsl: t.matmul(
                                     bank(rb),
                                     kT[r0:r0 + 64,
                                        p * c.T + t_ * 128: p * c.T + (t_ + 1) * 128],
                                     qT[r0:r0 + 64,
                                        qsl + p * c.TBLK: qsl + (p + 1) * c.TBLK],
                                     start=True, stop=True),
                                 waits=w, inc="pe" if half == 1 else None)
                        EH[("qk", gi)] = P.cnt["pe"]

                    def emit_exp(t_, base=base):
                        gi = base + t_
                        sp = gi % 2
                        sl4 = gi % 4
                        EH[gi] = P.do(
                            "scalar", lambda e, sp=sp, sl4=sl4: e.activation(
                                pT[:, sl4 * 1024: (sl4 + 1) * 1024],
                                bank(2 * sp, 0, 1024),
                                AF.Exp, scale=SC, bias=negBv[:, 0:1]),
                            waits=[("pe", EH[("qk", gi)]),
                                   ("pe", PH.get(gi - 4, 0)),
                                   ("pe", pvstate.get("pT_free", 0)),
                                   ("dv", m_negB)], inc="ac")

                    def emit_pv(t_, p=p, obase=obase, base=base):
                        gi = base + t_
                        sl4 = gi % 4
                        hls = (2 * p, 2 * p + 1)
                        for half in range(2):
                            ob = obase + half
                            hl = hls[half]
                            w = []
                            if half == 0:
                                w = [("ac", EH[gi])]
                                if t_ == 0:
                                    w += BW[obase] + BW[obase + 1]
                            P.do("tensor",
                                 lambda t, t_=t_, hl=hl, sl4=sl4, ob=ob, half=half:
                                 t.matmul(
                                     bank(ob, 0, 512)[0:65, :],
                                     vP[:, t_ * c.VW + hl * 65:
                                        t_ * c.VW + hl * 65 + 65],
                                     pT[:, sl4 * 1024 + half * 512:
                                        sl4 * 1024 + (half + 1) * 512],
                                     start=(t_ == 0), stop=(t_ == c.KT - 1)),
                                 waits=w, inc="pe" if half == 1 else None)
                        PH[gi] = P.cnt["pe"]
                        if (bgq and obase != bgq_meta["ob0"]
                                and t_ % 4 == 0):
                            bgq.pop(0)()

                    for u in range(c.KT // 2):
                        emit_qk(2 * u)
                        emit_exp(2 * u)
                        emit_qk(2 * u + 1)
                        if u > 0:
                            emit_pv(2 * u - 2)
                        emit_exp(2 * u + 1)
                        if u > 0:
                            emit_pv(2 * u - 1)
                        if (u == min(2, c.KT // 2 - 1)
                                and pending_epi is not None):
                            pending_epi()
                            pending_epi = None
                        if (u == min(4, c.KT // 2 - 1) and p == 0
                                and tailstate["fn"] is not None):
                            tailstate["fn"]()
                            tailstate["fn"] = None
                        if p == 2 and u >= 3 and prolstate["chunks"]:
                            prolstate["chunks"].pop(0)()
                    emit_pv(c.KT - 2)
                    emit_pv(c.KT - 1)
                    if p == 2:
                        while prolstate["chunks"]:
                            prolstate["chunks"].pop(0)()
                    ringstate["i"] += c.KT
                    pending_epi = (lambda p=p, obase=obase, m=P.cnt["pe"]:
                                   emit_epilogue(p, obase, m))
                while bgq:
                    bgq.pop(0)()
                war_qT["v"] = [("pe", P.cnt["pe"])]
                mac_all = P.cnt["ac"]
                for b_ in range(4):
                    BW[b_] = [("ac", mac_all)]
                if qb + 1 < c.NTB:
                    if c.CH >= 3:
                        assert not prolstate["chunks"]
                        prol = prolstate["m_qT"]
                    else:
                        prol = emit_prologueB(qb + 1, normsB, g=g)
                    ob0_now = 6 if ((ringstate["pgc"] - 1) % 2 == 0) else 4
                    def mk_tail(qb=qb, g=g, ep=pending_epi, ob0=ob0_now):
                        def fn():
                            if ep is not None:
                                ep()
                            emit_outproj(qb, g, ob0)
                        return fn
                    tailstate["fn"] = mk_tail()
                    pending_epi = None
                else:
                    assert tailstate["fn"] is None
                    if pending_epi is not None:
                        pending_epi()
                        pending_epi = None
                    emit_outproj(qb, g,
                                 6 if ((ringstate["pgc"] - 1) % 2 == 0) else 4,
                                 spread=False)
            war_wkv["v"] = [("pe", P.cnt["pe"])]
            war_kT["v"] = [("pe", P.cnt["pe"])]
            war_vP["v"] = [("pe", P.cnt["pe"])]
            war_qT["v"] = [("pe", P.cnt["pe"])]

        def phase3_prologue(M_END, tz):
            s1T, s2T, wch, wch2, sgn = (tz["s1T"], tz["s2T"], tz["wch"],
                                        tz["wch2"], tz["sgn"])
            war_wch = {0: list(BW[0]) + list(BW[1]) + M_END, 1: list(M_END)}
            war_sgn = {0: list(M_END), 1: list(M_END)}
            sgn_chain = {"v": list(M_END)}
            war_red = []
            ms = {}
            for fc in range(c.FC):
                sl = fc % 2
                WC = wch[:, sl * c.D:(sl + 1) * c.D]
                SG = sgn[:, 0:c.D]
                mw = dma("sync", WC, W1[fc * 128:(fc + 1) * 128, :],
                         waits=war_wch[sl])
                mrd = P.do("vector", lambda v, WC=WC: v.tensor_reduce(
                    stats[:, 18:19], WC, axis=AX.X, op=AluOpType.add,
                    apply_absolute_value=True), waits=[mw] + war_red, inc="dv")
                mal = P.do("vector", lambda v, fc=fc: v.tensor_scalar(
                    alpha1[:, fc:fc + 1], stats[:, 18:19], 1.0 / c.D, None,
                    AluOpType.mult, AluOpType.bypass), waits=[("dv", mrd)], inc="dv")
                war_red = [("dv", mal)]
                msg = P.do("scalar", lambda e, WC=WC, SG=SG: e.activation(
                    SG, WC, AF.Sign),
                    waits=[mw] + sgn_chain["v"], inc="ac")
                war_wch[sl] = [("dv", mrd), ("ac", msg)]

                def dst(j0, cn, fc=fc):
                    return apx(s1T, j0 * c.F + fc * 128, [[c.F, cn], [1, 128]])
                (semn, mev), mtp = transp(SG, c.DC, 0 if sl == 0 else 2, dst,
                                          wait_first=[("ac", msg)], ident=id_bf)
                sgn_chain["v"] = [("pe", mtp)]
            ms["s1T"] = P.cnt["dv"]
            war_wch = {0: war_wch[0] + war_wch[1], 1: war_wch[0] + war_wch[1]}
            for dc in range(c.DC):
                sl = dc % 2
                WC = wch2[:, sl * c.F:(sl + 1) * c.F]
                SG = sgn[:, 0:c.F]
                mw = dma("sync", WC, W2[dc * 128:(dc + 1) * 128, :],
                         waits=war_wch[sl])
                mrd = P.do("vector", lambda v, WC=WC: v.tensor_reduce(
                    stats[:, 18:19], WC, axis=AX.X, op=AluOpType.add,
                    apply_absolute_value=True), waits=[mw] + war_red, inc="dv")
                mal = P.do("vector", lambda v, dc=dc: v.tensor_scalar(
                    alpha2[:, dc:dc + 1], stats[:, 18:19], 1.0 / c.F, None,
                    AluOpType.mult, AluOpType.bypass), waits=[("dv", mrd)], inc="dv")
                war_red = [("dv", mal)]
                msg = P.do("scalar", lambda e, WC=WC, SG=SG: e.activation(
                    SG, WC, AF.Sign),
                    waits=[mw] + sgn_chain["v"], inc="ac")
                war_wch[sl] = [("dv", mrd), ("ac", msg)]

                def dst(j0, cn, dc=dc):
                    return apx(s2T, j0 * c.D + dc * 128, [[c.D, cn], [1, 128]])
                (semn, mev), mtp = transp(SG, c.FC, 0 if sl == 0 else 2, dst,
                                          wait_first=[("ac", msg)], ident=id_bf)
                sgn_chain["v"] = [("pe", mtp)]
            ms["s2T"] = P.cnt["dv"]
            ms["alpha"] = P.cnt["dv"]
            return ms

        def phase3(M_END, ms, tz):
            s1T, s2T, hT = tz["s1T"], tz["s2T"], tz["hT"]
            x3, x1, ln3T = tz["x3"], tz["x1"], tz["ln3T"]
            ln3f, ln3b = tz["ln3f"], tz["ln3b"]
            o2tm, oscr = tz["o2tm"], tz["oscr"]
            out_dmas = phase3.out_dmas = []
            m_y_done = tz["m_y_done"]
            MP = tz["m_prol"]
            war_x3 = {0: list(M_END) + MP, 1: list(M_END) + MP}
            war_x1 = {0: MP, 1: MP}
            war_ln3f = {k: MP for k in range(2 * (c.TB3 // 128))}
            war_ln3b = {k: MP for k in range(2 * (c.TB3 // 128))}
            war_ln3T = {0: MP, 1: MP}
            war_hT = {"v": []}
            war_o2 = []
            nsub = c.TB3 // 128
            DCT = c.DC * c.TB3

            def front(tb):
                t0 = tb * c.TB3
                sl = tb % 2
                X3 = lambda s: x3[:, (sl * nsub + s) * c.D:(sl * nsub + s + 1) * c.D]
                X1 = lambda s: x1[:, (sl * nsub + s) * c.D:(sl * nsub + s + 1) * c.D]
                L3s = lambda s: ln3f[:, (sl * nsub + s) * c.D:(sl * nsub + s + 1) * c.D]
                mx = []
                for s in range(nsub):
                    mx.append(dma("sync", X3(s),
                                  X[t0 + s * 128: t0 + 128 * (s + 1), :],
                                  waits=war_x3[sl] if s == 0 else ()))
                pows = []
                for s in range(nsub):
                    ysl = y_acc[:, (t0 // 128 + s) * c.D:(t0 // 128 + s + 1) * c.D]
                    pows.append(ln_stats(ysl, sl * 2 + s,
                                         m_y_done if tb == 0 else []))
                m_x1 = []
                for s in range(nsub):
                    ysl = y_acc[:, (t0 // 128 + s) * c.D:(t0 // 128 + s + 1) * c.D]
                    mn = ln_norm(ysl, L3s(s), sl * 2 + s, pows[s],
                                 extra_waits=war_ln3f[sl * nsub + s])
                    ma = P.do("vector", lambda v, s=s, X3=X3, X1=X1, L3s=L3s:
                              v.tensor_tensor(X1(s), X3(s), L3s(s), AluOpType.add),
                              waits=[("dv", mn), mx[s]]
                              + (war_x1[sl] if s == 0 else []), inc="dv")
                    war_ln3f[sl * nsub + s] = [("dv", ma)]
                    m_x1.append(ma)
                pows = []
                for s in range(nsub):
                    pows.append(ln_stats(X1(s), sl * 2 + s, [("dv", m_x1[s])]))
                norms = []
                for s in range(nsub):
                    L3 = ln3b[:, (sl * nsub + s) * c.D:(sl * nsub + s + 1) * c.D]
                    mn = ln_norm(X1(s), L3, sl * 2 + s, pows[s],
                                 extra_waits=war_ln3b[sl * nsub + s])
                    norms.append(mn)
                return norms

            def emit_ln3T(tb, norms):
                sl = tb % 2
                m_ln3 = None
                for s_ in range(nsub):
                    L3 = ln3b[:, (sl * nsub + s_) * c.D:(sl * nsub + s_ + 1) * c.D]

                    def dst(j0, cn, s_=s_, sl=sl):
                        return apx(ln3T, sl * DCT + j0 * c.TB3 + s_ * 128,
                                   [[c.TB3, cn], [1, 128]])
                    (semn, mev), mtp = transp(L3, c.DC, 0, dst,
                                              wait_first=[("dv", norms[s_])]
                                              + war_ln3T[sl], ident=id_bf)
                    war_ln3b[sl * nsub + s_] = [("pe", mtp)]
                    m_ln3 = mev
                return m_ln3

            f_norms = front(0)
            pend_t = {"norms": f_norms, "tb": 0}
            for tb in range(c.NT3):
                t0 = tb * c.TB3
                sl = tb % 2
                X3 = lambda s, sl=sl: x3[:, (sl * nsub + s) * c.D:
                                         (sl * nsub + s + 1) * c.D]
                X1 = lambda s, sl=sl: x1[:, (sl * nsub + s) * c.D:
                                         (sl * nsub + s + 1) * c.D]
                L3s = lambda s, sl=sl: ln3f[:, (sl * nsub + s) * c.D:
                                            (sl * nsub + s + 1) * c.D]
                if pend_t.get("m_ln3") is None:
                    m_ln3 = emit_ln3T(tb, pend_t["norms"])
                else:
                    m_ln3 = pend_t["m_ln3"]
                for fc in range(c.FC):
                    pb = 2 + (fc % 2)
                    for dc in range(c.DC):
                        w = (BW[pb] + [("dv", m_ln3),
                                       ("dv", ms["s1T"])]) if dc == 0 else ()
                        P.do("tensor", lambda t, fc=fc, dc=dc, pb=pb, sl=sl:
                             t.matmul(
                                 bank(pb, 0, c.TB3),
                                 s1T[:, dc * c.F + fc * 128: dc * c.F + (fc + 1) * 128],
                                 ln3T[:, sl * DCT + dc * c.TB3:
                                      sl * DCT + (dc + 1) * c.TB3],
                                 start=(dc == 0), stop=(dc == c.DC - 1)),
                             waits=w, inc="pe" if dc == c.DC - 1 else None)
                    m_mm = P.cnt["pe"]
                    war_ln3T[sl] = [("pe", m_mm)]
                    mg = P.do("scalar", lambda e, fc=fc, pb=pb: e.activation(
                        hT[:, fc * c.TB3:(fc + 1) * c.TB3], bank(pb, 0, c.TB3),
                        AF.Gelu, scale=alpha1[:, fc:fc + 1], bias=b1s[:, fc:fc + 1]),
                        waits=[("pe", m_mm), ("dv", ms["alpha"])] + m_bload
                        + war_hT["v"], inc="ac")
                    BW[pb] = [("ac", mg)]
                m_h = P.cnt["ac"]
                if tb + 1 < c.NT3:
                    nxt = front(tb + 1)
                    pend_t = {"norms": nxt, "tb": tb + 1, "m_ln3": None}
                for dc in range(c.DC):
                    pb = 4 + (dc % 2)
                    for fc in range(c.FC):
                        w = (BW[pb] + [("ac", m_h), ("dv", ms["s2T"])]) if fc == 0 else ()
                        P.do("tensor", lambda t, fc=fc, dc=dc, pb=pb: t.matmul(
                            bank(pb, 0, c.TB3),
                            s2T[:, fc * c.D + dc * 128: fc * c.D + (dc + 1) * 128],
                            hT[:, fc * c.TB3:(fc + 1) * c.TB3],
                            start=(fc == 0), stop=(fc == c.FC - 1)),
                            waits=w, inc="pe" if fc == c.FC - 1 else None)
                    m_mm = P.cnt["pe"]
                    war_hT["v"] = [("pe", m_mm)]
                    mo = P.do("scalar", lambda e, dc=dc, pb=pb: e.activation(
                        oscr, bank(pb, 0, c.TB3), AF.Identity,
                        scale=alpha2[:, dc:dc + 1], bias=b2s[:, dc:dc + 1]),
                        waits=[("pe", m_mm), ("dv", ms["alpha"])] + m_bload
                        + war_o2, inc="ac")
                    BW[pb] = [("ac", mo)]

                    def dst(j0, cn, dc=dc):
                        return apx(o2tm, j0 * c.D + dc * 128, [[c.D, cn], [1, 128]])
                    (semn, mev), mtp = transp(oscr, nsub, 6, dst,
                                              wait_first=[("ac", mo)], ident=id16)
                    war_o2 = [("pe", mtp)]
                    m_o2 = mev
                    if (dc == min(3, c.DC - 1) and tb + 1 < c.NT3
                            and pend_t["m_ln3"] is None):
                        pend_t["m_ln3"] = emit_ln3T(tb + 1, pend_t["norms"])
                pows = []
                for s in range(nsub):
                    pows.append(ln_stats(o2tm[:, s * c.D:(s + 1) * c.D],
                                         sl * 2 + s, [("dv", m_o2)]))
                for s in range(nsub):
                    mn = ln_norm(o2tm[:, s * c.D:(s + 1) * c.D], L3s(s),
                                 sl * 2 + s, pows[s],
                                 extra_waits=war_ln3f[sl * nsub + s])
                    ma = P.do("vector", lambda v, s=s, X3=X3, X1=X1, L3s=L3s:
                              v.tensor_tensor(X3(s), X1(s), L3s(s), AluOpType.add),
                              waits=[("dv", mn)], inc="dv")
                    war_ln3f[sl * nsub + s] = [("dv", ma)]
                    mo_ = dma("sync", OUT[t0 + s * 128: t0 + (s + 1) * 128, :],
                              X3(s), waits=[("dv", ma)])
                    out_dmas.append(mo_)
                war_x3[sl] = wall(out_dmas[-nsub:])
                war_x1[sl] = [("dv", P.cnt["dv"])]

        m_yg0 = [0]
        phase_A(0)
        phase_B(0)
        m_yg0[0] = P.cnt["ac"]
        phase_A(1)
        phase_B(1)
        M_END = [("pe", P.cnt["pe"]), ("dv", P.cnt["dv"]), ("ac", P.cnt["ac"]),
                 ("gp", P.cnt["gp"])]
        m_y_done = [("dv", P.cnt["dv"]), ("ac", P.cnt["ac"])]
        st2.close()
        s1T = sb("s1T", [128, c.DC * c.F], FP8)
        s2T = sb("s2T", [128, c.FC * c.D], FP8)
        hT = sb("hT", [128, c.FC * c.TB3], BF16)
        o2tm = sb("o2tm", [128, (c.TB3 // 128) * c.D], FP16)
        oscr = sb("oscr", [128, c.TB3], FP16)
        st3 = ExitStack()
        wch = st3.enter_context(nc.sbuf_tensor("wch", [128, 2 * c.D], F32))[:, :]
        wch2 = st3.enter_context(nc.sbuf_tensor("wch2", [128, 2 * c.F], F32))[:, :]
        sgn = st3.enter_context(
            nc.sbuf_tensor("sgn", [128, max(c.D, c.F)], BF16))[:, :]
        p3tensors = dict(s1T=s1T, s2T=s2T, hT=hT, o2tm=o2tm, oscr=oscr,
                         wch=wch, wch2=wch2, sgn=sgn)
        ms = phase3_prologue(M_END, p3tensors)
        m_prol = [("pe", P.cnt["pe"]), ("dv", P.cnt["dv"]), ("ac", P.cnt["ac"])]
        st3.close()
        nsub3 = c.TB3 // 128
        p3tensors["x3"] = sb("x3", [128, 2 * nsub3 * c.D], F32)
        p3tensors["x1"] = p3tensors["x3"]
        p3tensors["ln3f"] = sb("ln3f", [128, 2 * (c.TB3 // 128) * c.D], F32)
        p3tensors["ln3b"] = sb("ln3b", [128, 2 * (c.TB3 // 128) * c.D], BF16)
        p3tensors["lnscr"] = None
        p3tensors["ln3T"] = sb("ln3T", [128, 2 * c.DC * c.TB3], BF16)
        p3tensors["m_y_done"] = m_y_done
        p3tensors["m_prol"] = m_prol
        phase3(M_END, ms, p3tensors)

        P.do("sync", lambda e: e.nop(), waits=wall(phase3.out_dmas))

        P.emit(block, semh, first_waits=[("gp", m_init)])
    return nc


def kernel(L0_attn_x, L0_attn_w_qkv, L0_attn_w_out, L0_ffn_w1, L0_ffn_b1,
           L0_ffn_w2, L0_ffn_b2):
    x = np.ascontiguousarray(np.asarray(L0_attn_x, dtype=np.float32))
    wqkv = np.ascontiguousarray(np.asarray(L0_attn_w_qkv, dtype=np.float32))
    wout = np.ascontiguousarray(np.asarray(L0_attn_w_out, dtype=np.float32))
    w1 = np.ascontiguousarray(np.asarray(L0_ffn_w1, dtype=np.float32))
    b1 = np.ascontiguousarray(np.asarray(L0_ffn_b1, dtype=np.float32))
    w2 = np.ascontiguousarray(np.asarray(L0_ffn_w2, dtype=np.float32))
    b2 = np.ascontiguousarray(np.asarray(L0_ffn_b2, dtype=np.float32))

    cfg = Cfg(T=4096, D=1024, H=16, F=4096)
    nc = build_nc(cfg)
    in_maps = []
    for b in range(8):
        in_maps.append({"x": x[b], "w_qkv": wqkv, "w_out": wout,
                        "w1": w1, "b1": b1, "w2": w2, "b2": b2})
    res = run_bass_kernel_spmd(nc, in_maps, core_ids=list(range(8)))
    out = np.stack([res.results[b]["out"] for b in range(8)], axis=0)
    return out.astype(np.float32)


# version 32 (final)
# speedup vs baseline: 1.2470x; 1.0227x over previous
import math
from contextlib import ExitStack

import numpy as np
import concourse.bass as bass
import concourse.mybir as mybir
from concourse.alu_op_type import AluOpType
from concourse.bass_utils import run_bass_kernel_spmd

F32 = mybir.dt.float32
BF16 = mybir.dt.bfloat16
FP16 = mybir.dt.float16
FP8 = mybir.dt.float8e4
AF = mybir.ActivationFunctionType
AX = mybir.AxisListType

ENGINES = ("tensor", "vector", "scalar", "gpsimd", "sync")
SEMS = (("pe", "dv", "ac", "gp", "out")
        + tuple(f"qs{i}" for i in range(8)) + tuple(f"qg{i}" for i in range(8)))


def wall(tuples):
    d = {}
    for s_, v in tuples:
        if v:
            d[s_] = max(d.get(s_, 0), v)
    return list(d.items())


class Cfg:
    def __init__(self, T=4096, D=1024, H=16, F=4096):
        self.T, self.D, self.H, self.F = T, D, H, F
        self.DH = 64
        assert H * self.DH == D
        self.DC = D // 128
        self.FC = F // 128
        self.KT = T // 128
        self.G = 2
        self.HG = H // 2
        self.CH = self.HG * 64 // 128
        self.TBLK = 512
        self.NTB = T // self.TBLK
        self.TB3 = 256
        self.NT3 = T // self.TB3
        self.VW = self.HG * 65
        self.scale = 1.0 / math.sqrt(D)
        self.kgroups = []
        k = 0
        while k < self.KT:
            n = min(3, self.KT - k)
            self.kgroups.append((k, n))
            k += n


def apx(view, offset, dims):
    p = view.ap[0]
    return bass.AP(view.tensor, offset, [[p[0], p[1]]] + [list(d) for d in dims])


class Prog:
    def __init__(self):
        self.ops = {e: [] for e in ENGINES}
        self.cnt = {s: 0 for s in SEMS}

    def do(self, eng, fn, waits=(), inc=None, n=1):
        self.ops[eng].append((fn, tuple(waits), inc, n))
        if inc is not None:
            self.cnt[inc] += n
            return self.cnt[inc]
        return None

    def emit(self, block, sems, first_waits=()):
        for ename in ENGINES:
            def body(eng, ename=ename):
                waited = {}
                if ename != "gpsimd":
                    for s, v in first_waits:
                        if v:
                            eng.wait_ge(sems[s], v)
                            waited[s] = v
                for fn, waits, inc, n in self.ops[ename]:
                    for s, v in waits:
                        if v and waited.get(s, 0) < v:
                            eng.wait_ge(sems[s], v)
                            waited[s] = v
                    instr = fn(eng)
                    if inc is not None:
                        instr.then_inc(sems[inc], n)
            getattr(block, ename)(body)


def build_nc(cfg: Cfg):
    c = cfg
    nc = bass.Bass("TRN2", target_bir_lowering=False)

    X = nc.dram_tensor("x", [c.T, c.D], F32, kind="ExternalInput").ap()
    Wqkv = nc.dram_tensor("w_qkv", [c.D, 3 * c.D], F32, kind="ExternalInput").ap()
    Wout = nc.dram_tensor("w_out", [c.D, c.D], F32, kind="ExternalInput").ap()
    W1 = nc.dram_tensor("w1", [c.F, c.D], F32, kind="ExternalInput").ap()
    B1 = nc.dram_tensor("b1", [c.F], F32, kind="ExternalInput").ap()
    W2 = nc.dram_tensor("w2", [c.D, c.F], F32, kind="ExternalInput").ap()
    B2 = nc.dram_tensor("b2", [c.D], F32, kind="ExternalInput").ap()
    OUT = nc.dram_tensor("out", [c.T, c.D], F32, kind="ExternalOutput").ap()

    P = Prog()
    with ExitStack() as st:
        def sb(name, shape, dtype):
            h = st.enter_context(nc.sbuf_tensor(name, shape, dtype))
            return h[tuple(slice(None) for _ in shape)]

        id32 = sb("id32", [128, 128], F32)
        id_bf = sb("id_bf", [128, 128], BF16)
        id16 = sb("id16", [128, 128], FP16)
        ones16 = sb("ones16", [1, 128], FP16)
        ones_bf = sb("ones_bf", [128, 1], BF16)
        eps_t = sb("eps_t", [128, 1], F32)
        mhalf = sb("mhalf", [128, 1], F32)
        alpha1 = sb("alpha1", [128, c.FC], F32)
        alpha2 = sb("alpha2", [128, c.DC], F32)
        b1s = sb("b1s", [128, c.FC], F32)
        b2s = sb("b2s", [128, c.DC], F32)
        y_acc = sb("y_acc", [128, (c.T // 128) * c.D], FP16)
        kmaxs = sb("kmaxs", [1, 2], F32)

        stats = sb("stats", [128, 80], F32)
        qmax4 = sb("qmax4", [128, 8], F32)
        sc1 = sb("sc1", [1, 8], F32)
        negBv = sb("negBv", [128, 1], F32)

        st2 = ExitStack()

        def sb2(name, shape, dtype):
            h = st2.enter_context(nc.sbuf_tensor(name, shape, dtype))
            return h[tuple(slice(None) for _ in shape)]

        kT = sb2("kT", [128, c.CH * c.T], BF16)
        vP = sb2("vP", [128, c.KT * c.VW], BF16)
        wsh = sb2("wsh", [128, c.DC * c.HG * 64 + (c.HG // 2) * c.D], BF16)
        wkv = wsh[:, 0:c.DC * c.HG * 128]
        wq = wsh[:, 0:c.DC * c.HG * 64]
        wo = wsh[:, c.DC * c.HG * 64:]
        xs = sb2("xs", [128, 4 * c.D], F32)
        lnx16 = sb2("lnx16", [128, 4 * c.D], BF16)
        lnxT = sb2("lnxT", [128, c.DC * c.TBLK], BF16)
        qT = sb2("qT", [128, 2 * c.CH * c.TBLK], BF16)
        oT = sb2("oT", [128, (c.HG // 2) * c.TBLK], BF16)
        pT = sb2("pT", [128, 4 * 1024], BF16)
        escr = sb2("escr", [128, 1024], F32)
        otmp = escr[0:64, :]
        recl = escr[64:65, :]
        vtmp16 = pT[:, 0:512]
        recl16 = sb2("recl16", [1, 520], FP16)

        psh = st.enter_context(nc.psum_tensor("ps", [128, 4096], F32))
        ps = psh[:, :]

        def bank(i, lo=0, width=512):
            return ps[:, 512 * i + lo: 512 * i + lo + width]

        semh = {s: st.enter_context(nc.semaphore(s)) for s in SEMS}
        block = st.enter_context(nc.Block())

        SC = c.scale
        BW = {i: [] for i in range(8)}

        rr = {"sync": 0, "gpsimd": 0}

        def dma(q, out, in_, waits=()):
            pre = "qs" if q == "sync" else "qg"
            sem = f"{pre}{rr[q] % 8}"
            rr[q] += 1
            chain = [(sem, P.cnt[sem])]
            v = P.do(q, lambda e, o=out, i=in_: e.dma_start(out=o, in_=i),
                     waits=list(waits) + chain, inc=sem, n=16)
            return (sem, v)

        m0 = P.do("gpsimd", lambda g: g.memset(id32, 0.0), inc="gp")

        P.do("gpsimd", lambda g: g.affine_select(
            out=id32, in_=id32, compare_op=AluOpType.not_equal, fill=1.0,
            base=0, pattern=[[-1, 128]], channel_multiplier=1),
            waits=[("gp", m0)], inc="gp")
        mib = P.do("gpsimd", lambda g: g.memset(id_bf, 0.0), inc="gp")
        P.do("gpsimd", lambda g: g.affine_select(
            out=id_bf, in_=id_bf, compare_op=AluOpType.not_equal, fill=1.0,
            base=0, pattern=[[-1, 128]], channel_multiplier=1),
            waits=[("gp", mib)], inc="gp")
        mih = P.do("gpsimd", lambda g: g.memset(id16, 0.0), inc="gp")
        P.do("gpsimd", lambda g: g.affine_select(
            out=id16, in_=id16, compare_op=AluOpType.not_equal, fill=1.0,
            base=0, pattern=[[-1, 128]], channel_multiplier=1),
            waits=[("gp", mih)], inc="gp")
        P.do("gpsimd", lambda g: g.memset(ones16, 1.0), inc="gp")
        P.do("gpsimd", lambda g: g.memset(ones_bf, 1.0), inc="gp")
        P.do("gpsimd", lambda g: g.memset(eps_t, 1e-5), inc="gp")
        P.do("gpsimd", lambda g: g.memset(mhalf, -0.5), inc="gp")
        P.do("gpsimd", lambda g: g.memset(
            apx(vP, 64, [[c.VW, c.KT], [65, c.HG]]), 1.0), inc="gp")
        m_init = P.do("gpsimd", lambda g: g.memset(oT, 0.0), inc="gp")

        m_bload = []
        for fc in range(c.FC):
            m_bload.append(dma("sync", b1s[:, fc:fc + 1],
                               bass.AP(B1.tensor, fc * 128, [[1, 128], [1, 1]])))
        for dc in range(c.DC):
            m_bload.append(dma("sync", b2s[:, dc:dc + 1],
                               bass.AP(B2.tensor, dc * 128, [[1, 128], [1, 1]])))
        m_bload = wall(m_bload)

        slot_pow = {}
        slot_norm = {}

        def ln_stats(src, slot, wait_src):
            o = slot * 20
            ts_ = min(512, c.D)
            nh = c.D // ts_
            for i in range(nh):
                P.do("vector",
                     lambda v, i=i, o=o, s=src, ts_=ts_: v.bn_stats(
                         stats[:, o + 6 * i:o + 6 * i + 6],
                         s[:, ts_ * i:ts_ * (i + 1)]),
                     waits=list(wait_src) + [("gp", slot_pow.get(slot, 0)),
                                             ("dv", slot_norm.get(slot, 0))]
                     if i == 0 else (),
                     inc="dv" if i == nh - 1 else None)
            m1 = P.cnt["dv"]
            m2 = P.do("vector", lambda v, o=o, n=nh: v.bn_aggr(
                stats[:, o + 12:o + 14], stats[:, o:o + 6 * n]),
                waits=[("dv", m1)], inc="dv")
            m3 = P.do("vector", lambda v, o=o: v.tensor_scalar(
                stats[:, o + 14:o + 15], stats[:, o + 13:o + 14], eps_t[:, 0:1],
                None, AluOpType.add, AluOpType.bypass),
                waits=[("dv", m2), ("gp", slot_pow.get(slot, 0))], inc="dv")
            m4 = P.do("gpsimd", lambda g, o=o: g.tensor_tensor(
                stats[:, o + 15:o + 16], stats[:, o + 14:o + 15], mhalf,
                AluOpType.pow),
                waits=[("dv", m3), ("gp", m_init),
                       ("dv", slot_norm.get(slot, 0))], inc="gp")
            slot_pow[slot] = m4
            return m4

        def ln_norm(src, dst, slot, m4, extra_waits=()):
            o = slot * 20
            m5 = P.do("vector", lambda v, o=o: v.scalar_tensor_tensor(
                stats[:, o + 16:o + 17], stats[:, o + 12:o + 13], -1.0,
                stats[:, o + 15:o + 16], AluOpType.mult, AluOpType.mult),
                waits=[("gp", m4)], inc="dv")
            m6 = P.do("vector", lambda v, o=o, s=src, d=dst: v.tensor_scalar(
                d, s, stats[:, o + 15:o + 16], stats[:, o + 16:o + 17],
                AluOpType.mult, AluOpType.add),
                waits=[("dv", m5)] + list(extra_waits), inc="dv")
            slot_norm[slot] = m6
            return m6

        def transp(src, n, b0, dst_of, wait_first, evac_eng="vector", nb=2,
                   ident=None):
            sem = "dv" if evac_eng == "vector" else "ac"
            idt = id32 if ident is None else ident
            step = 4
            last = None
            g0 = 0
            while g0 < n:
                g1 = min(g0 + step, n)
                bi = b0 + ((g0 // 4) % nb if step == 4 else 0)
                bw = BW[bi] if step == 4 else BW[b0] + BW[b0 + 1]
                for j in range(g0, g1):
                    w = []
                    if j == 0:
                        w += list(wait_first)
                    if j == g0:
                        w += bw
                    P.do("tensor", lambda t, j=j, bi=bi, g0=g0, s=src, idt=idt:
                         (t.transpose(
                             bank(bi + (j - g0) // 4, 128 * ((j - g0) % 4), 128),
                             s[:, 128 * j:128 * (j + 1)], idt)
                          if idt is id32 else
                          t.matmul(
                             bank(bi + (j - g0) // 4, 128 * ((j - g0) % 4), 128),
                             s[:, 128 * j:128 * (j + 1)], idt,
                             start=True, stop=True)),
                         waits=w, inc="pe" if j == g1 - 1 else None)
                mpe = P.cnt["pe"]
                cn = g1 - g0
                if evac_eng == "vector":
                    last = P.do("vector", lambda v, bi=bi, g0=g0, cn=cn: v.tensor_copy(
                        dst_of(g0, cn), bank(bi, 0, 128 * cn)),
                        waits=[("pe", mpe)], inc="dv")
                else:
                    last = P.do("scalar", lambda e, bi=bi, g0=g0, cn=cn: e.activation(
                        dst_of(g0, cn), bank(bi, 0, 128 * cn), AF.Copy),
                        waits=[("pe", mpe)], inc="ac")
                if step == 4:
                    BW[bi] = [(sem, last)]
                else:
                    BW[b0] = [(sem, last)]
                    BW[b0 + 1] = [(sem, last)]
                g0 = g1
            return (sem, last), P.cnt["pe"]

        st_blk = {"xw": [[] for _ in range(4)],
                  "l16w": [[] for _ in range(4)],
                  "lnxT_users": [],
                  "m_lnxT_sem": "dv", "evac_eng": "vector"}

        def ln1_part1(t0_base):
            nsub = c.TBLK // 128
            pows, loads = [], []
            for s in range(nsub):
                mx = dma("sync", xs[:, s * c.D:(s + 1) * c.D],
                         X[t0_base + s * 128: t0_base + s * 128 + 128, :],
                         waits=st_blk["xw"][s])
                loads.append(mx)
            for s in range(nsub):
                pows.append(ln_stats(xs[:, s * c.D:(s + 1) * c.D], s, [loads[s]]))
            norms = []
            for s in range(nsub):
                xsl = xs[:, s * c.D:(s + 1) * c.D]
                l16 = lnx16[:, s * c.D:(s + 1) * c.D]
                mn = ln_norm(xsl, l16, s, pows[s],
                             extra_waits=st_blk["l16w"][s])
                st_blk["xw"][s] = [("dv", mn)]
                norms.append(mn)
            return norms

        def ln1_part2(norms, b0=0):
            nsub = c.TBLK // 128
            mlast = None
            for s in range(nsub):
                l16 = lnx16[:, s * c.D:(s + 1) * c.D]
                def dst(j0, cn, s=s):
                    return apx(lnxT, j0 * c.TBLK + s * 128, [[c.TBLK, cn], [1, 128]])
                (semn, mev), mtp = transp(
                    l16, c.DC, b0, dst,
                    wait_first=[("dv", norms[s])]
                    + [("pe", u) for u in st_blk["lnxT_users"][-1:]],
                    evac_eng=st_blk["evac_eng"], ident=id_bf)
                st_blk["l16w"][s] = [("pe", mtp)]
                st_blk["m_lnxT_sem"] = semn
                mlast = mev
            return mlast

        war_wkv = {"v": []}
        war_kT = {"v": []}
        war_vP = {"v": []}
        m_kv_done = {}

        def phase_A(g):
            W2G = c.HG * 128
            m_w = []
            for dc in range(c.DC):
                m_w.append(dma("gpsimd", wkv[:, dc * W2G: dc * W2G + c.HG * 64],
                    Wqkv[dc * 128:(dc + 1) * 128,
                         c.D + g * c.HG * 64: c.D + (g + 1) * c.HG * 64],
                    waits=(war_wkv["v"] + [("gp", m_init)]) if dc == 0 else ()))
                m_w.append(dma("gpsimd", wkv[:, dc * W2G + c.HG * 64: (dc + 1) * W2G],
                    Wqkv[dc * 128:(dc + 1) * 128,
                         2 * c.D + g * c.HG * 64: 2 * c.D + (g + 1) * c.HG * 64]))
            m_w = wall(m_w)
            mk0 = P.do("vector", lambda v: v.memset(qmax4[:, 7:8], 0.0),
                       waits=war_kT["v"], inc="dv")
            kmax_run = mk0

            nmk = c.HG * 128 // 128
            nK = nmk // 2
            BW_vtmp[:] = [("dv", P.cnt["dv"]), ("pe", P.cnt["pe"])]
            st_blk["evac_eng"] = "scalar"
            norms = ln1_part1(0)
            for tb in range(c.NTB):
                m_lnxT = ln1_part2(norms)
                if tb + 1 < c.NTB:
                    norms = ln1_part1((tb + 1) * c.TBLK)
                for mc in range(nmk):
                    pb = 2 + (mc % 2)
                    is_v = mc >= nK
                    for dc in range(c.DC):
                        w = (BW[pb] + m_w
                             + [(st_blk["m_lnxT_sem"], m_lnxT)]) if dc == 0 else ()
                        P.do("tensor", lambda t, dc=dc, mc=mc, pb=pb: t.matmul(
                            bank(pb),
                            wkv[:, dc * W2G + mc * 128: dc * W2G + (mc + 1) * 128],
                            lnxT[:, dc * c.TBLK:(dc + 1) * c.TBLK],
                            start=(dc == 0), stop=(dc == c.DC - 1)),
                            waits=w, inc="pe" if dc == c.DC - 1 else None)
                    m_mm = P.cnt["pe"]
                    st_blk["lnxT_users"] = [m_mm]
                    if not is_v:
                        mea = P.do("scalar", lambda e, mc=mc, tb=tb, pb=pb: e.activation(
                            kT[:, mc * c.T + tb * c.TBLK: mc * c.T + (tb + 1) * c.TBLK],
                            bank(pb), AF.Copy),
                            waits=[("pe", m_mm)] + war_kT["v"], inc="ac")
                        mr = P.do("vector", lambda v, mc=mc, tb=tb: v.tensor_reduce(
                            qmax4[:, 6:7],
                            kT[:, mc * c.T + tb * c.TBLK: mc * c.T + (tb + 1) * c.TBLK],
                            axis=AX.X, op=AluOpType.max,
                            apply_absolute_value=True),
                            waits=[("ac", mea), ("dv", kmax_run)], inc="dv")
                        kmax_run = P.do("vector", lambda v: v.tensor_tensor(
                            qmax4[:, 7:8], qmax4[:, 7:8], qmax4[:, 6:7],
                            AluOpType.max), waits=[("dv", mr)], inc="dv")
                        BW[pb] = [("ac", mea)]
                    else:
                        mev = P.do("scalar", lambda e, pb=pb: e.activation(
                            vtmp16, bank(pb), AF.Copy),
                            waits=[("pe", m_mm), BW_vtmp[0]] if BW_vtmp else [("pe", m_mm)],
                            inc="ac")
                        BW[pb] = [("ac", mev)]
                        mcv = mc - nK
                        vb = 4 + (mcv % 2)
                        nsub = c.TBLK // 128
                        for j in range(nsub):
                            w = ([("ac", mev)] + BW[vb]) if j == 0 else ()
                            P.do("tensor", lambda t, j=j, vb=vb: t.matmul(
                                bank(vb, 128 * j, 128),
                                vtmp16[:, 128 * j:128 * (j + 1)], id_bf,
                                start=True, stop=True),
                                waits=w, inc="pe" if j == nsub - 1 else None)
                        m_vt = P.cnt["pe"]
                        BW_vtmp[:] = [("pe", m_vt)]
                        mlv = None
                        for h2 in range(2):
                            hl = 2 * mcv + h2
                            mlv = P.do("vector", lambda v, h2=h2, hl=hl, tb=tb,
                                       nsub=nsub, vb=vb:
                                       v.tensor_copy(
                                           apx(vP, tb * nsub * c.VW + hl * 65,
                                               [[c.VW, nsub], [1, 64]]),
                                           apx(ps, 512 * vb + h2 * 64,
                                               [[128, nsub], [1, 64]])),
                                       waits=([("pe", m_vt)] + war_vP["v"]) if h2 == 0 else (),
                                       inc="dv")
                        BW[vb] = [("dv", mlv)]
            war_wq["v"] = [("pe", P.cnt["pe"])]
            w5 = BW[5]
            P.do("tensor", lambda t: t.transpose(bank(5, 0, 128)[0:1, :],
                                                 qmax4[:, 7:8], id32),
                 waits=[("dv", kmax_run)] + w5, inc="pe")
            mt = P.cnt["pe"]
            mk = P.do("vector", lambda v, g=g: v.tensor_reduce(
                kmaxs[:, g:g + 1], bank(5)[0:1, 0:128], axis=AX.X,
                op=AluOpType.max), waits=[("pe", mt)], inc="dv")
            BW[5] = [("dv", mk)]
            m_kv_done[g] = mk

        BW_vtmp = []

        war_qT = {"v": []}
        war_oT = {"v": []}
        war_wq = {"v": []}
        war_otmp = {0: [], 1: []}
        war_recl = {0: [], 1: []}
        war_recl16 = {"v": []}
        tailstate = {"fn": None}
        prolstate = {"chunks": [], "m_qT": 0}
        bgq = []
        bgq_meta = {"ob0": -1}
        pvstate = {"m_pv_0": 0, "m_pv_1": 0}
        ringstate = {"exp": {}, "pv": {}, "i": 0, "pgc": 0}

        def phase_B(g):
            m_w = []
            for dc in range(c.DC):
                m_w.append(dma("gpsimd", wq[:, dc * c.HG * 64:(dc + 1) * c.HG * 64],
                    Wqkv[dc * 128:(dc + 1) * 128,
                         g * c.HG * 64:(g + 1) * c.HG * 64],
                    waits=war_wq["v"] if dc == 0 else ()))
            for hp in range(c.HG // 2):
                m_w.append(dma("gpsimd", wo[:, hp * c.D:(hp + 1) * c.D],
                    Wout[(g * c.HG + 2 * hp) * 64:
                         (g * c.HG + 2 * hp) * 64 + 128, :],
                    waits=([("gp", m_init)] + war_wq["v"]) if hp == 0 else ()))
            m_w = wall(m_w)
            m_pw = [("pe", P.cnt["pe"])]
            m_sq = P.do("scalar", lambda e: e.activation(
                lnxT[:, 0:c.DC * c.HG * 64], wq, AF.Square),
                waits=m_w + m_pw + [("ac", P.cnt["ac"])], inc="ac")
            NQ = c.HG * 64
            for dc in range(c.DC):
                w = ([("ac", m_sq)] + BW[0]) if dc == 0 else ()
                P.do("tensor", lambda t, dc=dc: t.matmul(
                    bank(0, 0, NQ)[0:1, :], ones_bf,
                    lnxT[:, dc * NQ:(dc + 1) * NQ],
                    start=(dc == 0), stop=(dc == c.DC - 1)),
                    waits=w, inc="pe" if dc == c.DC - 1 else None)
            m_cn = P.cnt["pe"]
            st_blk["lnxT_users"] = [m_cn]
            pvstate["pT_free"] = 0
            mr2 = P.do("vector", lambda v: v.tensor_reduce(
                sc1[:, 2:3], bank(0, 0, NQ)[0:1, :], axis=AX.X,
                op=AluOpType.max), waits=[("pe", m_cn)], inc="dv")
            BW[0] = [("dv", mr2)]
            ml1 = P.do("scalar", lambda e: e.activation(
                sc1[:, 3:4], sc1[:, 2:3], AF.Ln), waits=[("dv", mr2)], inc="ac")
            ml2 = P.do("scalar", lambda e: e.activation(
                sc1[:, 3:4], sc1[:, 3:4], AF.Exp, scale=0.5),
                waits=[("ac", ml1)], inc="ac")
            mr3 = P.do("vector", lambda v, g=g: v.scalar_tensor_tensor(
                sc1[:, 1:2], sc1[:, 3:4],
                -float(c.DH) * SC * math.sqrt(c.D), kmaxs[:, g:g + 1],
                AluOpType.mult, AluOpType.mult),
                waits=[("ac", ml2), ("dv", m_kv_done[g])], inc="dv")
            mr4 = P.do("vector", lambda v: v.tensor_copy(
                recl16[0:1, 512:513], sc1[:, 1:2]),
                waits=[("dv", mr3)] + war_recl16["v"], inc="dv")
            w1b = BW[1]
            P.do("tensor", lambda t: t.matmul(
                bank(1, 0, 1), ones16, recl16[0:1, 512:513],
                start=True, stop=True),
                waits=[("dv", mr4)] + w1b, inc="pe")
            mbc = P.cnt["pe"]
            m_negB = P.do("vector", lambda v: v.tensor_copy(
                negBv, bank(1, 0, 1)),
                waits=[("pe", mbc), ("ac", P.cnt["ac"])], inc="dv")
            BW[1] = BW[1] + [("dv", m_negB)]

            def emit_prologueB(qb, norms, g=None, pbanks=(2, 3)):
                m_lnxT = ln1_part2(norms, b0=pbanks[0])
                mq = 0
                qsl = (qb % 2) * c.CH * c.TBLK
                for mc in range(c.CH):
                    pb = pbanks[mc % 2]
                    for dc in range(c.DC):
                        w = (BW[pb] + m_w + [("dv", m_lnxT)]) if dc == 0 else ()
                        P.do("tensor", lambda t, dc=dc, mc=mc, pb=pb: t.matmul(
                            bank(pb),
                            wq[:, dc * c.HG * 64 + mc * 128: dc * c.HG * 64 + (mc + 1) * 128],
                            lnxT[:, dc * c.TBLK:(dc + 1) * c.TBLK],
                            start=(dc == 0), stop=(dc == c.DC - 1)),
                            waits=w, inc="pe" if dc == c.DC - 1 else None)
                    m_mm = P.cnt["pe"]
                    st_blk["lnxT_users"] = [m_mm]
                    if mc % 2 == 1 or mc == c.CH - 1:
                        nmc = 2 if mc % 2 == 1 else 1
                        mc0 = mc - nmc + 1
                        me = P.do("vector", lambda v, mc0=mc0, nmc=nmc, qsl=qsl:
                                  v.tensor_copy(
                                      qT[:, qsl + mc0 * c.TBLK:
                                         qsl + (mc0 + nmc) * c.TBLK],
                                      bank(pbanks[0] if mc0 % 2 == 0 else pbanks[1],
                                           0, nmc * 512)),
                                  waits=[("pe", m_mm)] + war_qT["v"], inc="dv")
                        mq = me
                        BW[pbanks[0]] = [("dv", me)]
                        BW[pbanks[1]] = [("dv", me)]
                return mq

            def prologueB_chunks(qb, norms, g=None, pbanks=(4, 5)):
                state = {"m_lnxT": 0}
                cl = []
                nsub = c.TBLK // 128
                qsl = (qb % 2) * c.CH * c.TBLK
                for s in range(nsub):
                    def tch(s=s):
                        l16 = lnx16[:, s * c.D:(s + 1) * c.D]
                        def dst(j0, cn, s=s):
                            return apx(lnxT, j0 * c.TBLK + s * 128,
                                       [[c.TBLK, cn], [1, 128]])
                        (semn, mev), mtp = transp(
                            l16, c.DC, pbanks[0], dst,
                            wait_first=[("dv", norms[s])]
                            + [("pe", u) for u in st_blk["lnxT_users"][-1:]],
                            evac_eng=st_blk["evac_eng"], ident=id_bf)
                        st_blk["l16w"][s] = [("pe", mtp)]
                        st_blk["m_lnxT_sem"] = semn
                        state["m_lnxT"] = mev
                    cl.append(tch)
                for mc in range(c.CH):
                    def qch(mc=mc):
                        pb = pbanks[mc % 2]
                        for dc in range(c.DC):
                            w = (BW[pb] + m_w
                                 + [(st_blk["m_lnxT_sem"],
                                     state["m_lnxT"])]) if dc == 0 else ()
                            P.do("tensor", lambda t, dc=dc, mc=mc, pb=pb:
                                 t.matmul(
                                     bank(pb),
                                     wq[:, dc * c.HG * 64 + mc * 128:
                                        dc * c.HG * 64 + (mc + 1) * 128],
                                     lnxT[:, dc * c.TBLK:(dc + 1) * c.TBLK],
                                     start=(dc == 0), stop=(dc == c.DC - 1)),
                                 waits=w, inc="pe" if dc == c.DC - 1 else None)
                        m_mm = P.cnt["pe"]
                        st_blk["lnxT_users"] = [m_mm]
                        if mc % 2 == 1 or mc == c.CH - 1:
                            nmc = 2 if mc % 2 == 1 else 1
                            mc0 = mc - nmc + 1
                            me = P.do("vector",
                                      lambda v, mc0=mc0, nmc=nmc, qsl=qsl:
                                      v.tensor_copy(
                                          qT[:, qsl + mc0 * c.TBLK:
                                             qsl + (mc0 + nmc) * c.TBLK],
                                          bank(pbanks[0] if mc0 % 2 == 0
                                               else pbanks[1], 0, nmc * 512)),
                                      waits=[("pe", m_mm)] + war_qT["v"],
                                      inc="dv")
                            prolstate["m_qT"] = me
                            BW[pbanks[0]] = [("dv", me)]
                            BW[pbanks[1]] = [("dv", me)]
                    cl.append(qch)
                return cl

            def emit_outproj(qb, g, ob0, spread=True):
                m_oT = P.cnt["dv"]
                nsub = c.TBLK // 128
                OW = min(512, c.D)
                ops = []
                for tc in range(nsub):
                    for oh in range(c.D // OW):
                        pb = ob0 + (tc * (c.D // OW) + oh) % 2
                        grp = []
                        for hl in range(c.HG // 2):
                            first = hl == 0
                            last = hl == c.HG // 2 - 1

                            def mm(tc=tc, oh=oh, hl=hl, pb=pb, first=first,
                                   last=last, qb=qb, g=g):
                                w = ([("dv", m_oT)] + m_w + BW[pb]) if first else ()
                                P.do("tensor", lambda t: t.matmul(
                                    bank(pb, 0, OW),
                                    oT[:, hl * c.TBLK + tc * 128:
                                       hl * c.TBLK + (tc + 1) * 128],
                                    wo[:, hl * c.D + oh * OW:
                                       hl * c.D + (oh + 1) * OW],
                                    start=first, stop=last),
                                    waits=w, inc="pe" if last else None)
                                if last:
                                    m_mm = P.cnt["pe"]
                                    war_oT["v"] = [("pe", m_mm)]
                                    dsty = y_acc[:, (qb * nsub + tc) * c.D + oh * OW:
                                                 (qb * nsub + tc) * c.D + (oh + 1) * OW]
                                    if g == 0:
                                        mev = P.do("vector",
                                                   lambda v, d=dsty, pb=pb: v.tensor_copy(
                                                       d, bank(pb, 0, OW)),
                                                   waits=[("pe", m_mm)], inc="dv")
                                    else:
                                        mev = P.do("vector",
                                                   lambda v, d=dsty, pb=pb: v.tensor_tensor(
                                                       d, d, bank(pb, 0, OW),
                                                       AluOpType.add),
                                                   waits=[("pe", m_mm),
                                                          ("ac", m_yg0[0])], inc="dv")
                                    BW[pb] = [("dv", mev)]
                            grp.append(mm)
                        ops.append(lambda grp=grp: [f() for f in grp])
                if spread:
                    bgq_meta["ob0"] = ob0
                    bgq.extend(ops)
                else:
                    for op in ops:
                        op()

            st_blk["evac_eng"] = "vector"
            for _h in range(2):            st_blk["evac_eng"] = "vector"
            for _h in range(2):            st_blk["evac_eng"] = "vector"
            for _h in range(2):            st_blk["evac_eng"] = "vector"
            for _h in range(2):
                war_otmp[_h] = war_otmp[_h] + [("ac", P.cnt["ac"])]
                war_recl[_h] = war_recl[_h] + [("ac", P.cnt["ac"])]
            normsB = ln1_part1(0)
            prol = emit_prologueB(0, normsB, g=g)
            for qb in range(c.NTB):
                m_qT = prol
                if qb + 1 < c.NTB:
                    normsB = ln1_part1((qb + 1) * c.TBLK)
                    if c.CH >= 3:
                        prolstate["chunks"] = prologueB_chunks(
                            qb + 1, normsB, g=g, pbanks=(4, 5))
                def emit_epilogue(p, obase, m_pv, halves=(0, 1)):
                    mults = []
                    for half in halves:
                        ob = obase + half
                        mo = P.do("vector", lambda v, ob=ob, half=half:
                                  v.tensor_copy(
                                      otmp[:, half * 512:(half + 1) * 512],
                                      bank(ob)[0:64, :]),
                                  waits=[("pe", m_pv)]
                                  + war_otmp[half], inc="dv")
                        mrc = P.do("vector", lambda v, ob=ob, half=half:
                                   v.reciprocal(
                                       recl[0:1, half * 512:(half + 1) * 512],
                                       bank(ob)[64:65, :]),
                                   waits=[("pe", m_pv),
                                          ("dv", mo)] + war_recl[half], inc="dv")
                        mrc2 = P.do("vector", lambda v, half=half: v.tensor_copy(
                            recl16[0:1, 0:512],
                            recl[0:1, half * 512:(half + 1) * 512]),
                            waits=[("dv", mrc)] + war_recl16["v"], inc="dv")
                        P.do("tensor", lambda t, ob=ob: t.matmul(
                            bank(ob), ones16,
                            recl16[0:1, 0:512],
                            start=True, stop=True),
                            waits=[("dv", mrc2)], inc="pe")
                        mb = P.cnt["pe"]
                        war_recl16["v"] = [("pe", mb)]
                        mm_ = P.do("vector", lambda v, ob=ob, half=half, p=p:
                                   v.tensor_tensor(
                                       oT[64 * half:64 * half + 64,
                                          p * c.TBLK:(p + 1) * c.TBLK],
                                       otmp[:, half * 512:(half + 1) * 512],
                                       bank(ob)[0:64, :], AluOpType.mult),
                                   waits=[("pe", mb)] + war_oT["v"], inc="dv")
                        BW[ob] = [("dv", mm_)]
                        war_otmp[half] = [("dv", mm_)]
                        war_recl[half] = [("pe", mb)]
                        mults.append(mm_)
                    return mults

                pending_epi = []
                EH, PH = ringstate["exp"], ringstate["pv"]
                qsl = (qb % 2) * c.CH * c.TBLK
                for p in range(c.CH):
                    hlA, hlB = 2 * p, 2 * p + 1
                    obase = 6 if (ringstate["pgc"] % 2 == 0) else 4
                    ringstate["pgc"] += 1
                    base = ringstate["i"]
                    if obase == bgq_meta["ob0"]:
                        while bgq:
                            bgq.pop(0)()

                    def emit_qk(t_, p=p, base=base, qsl=qsl):
                        gi = base + t_
                        sp = gi % 2
                        for half in range(2):
                            rb = 2 * sp + half
                            r0 = 64 * half
                            w = []
                            if half == 0:
                                w = [("ac", EH.get(gi - 2, 0))]
                                if t_ == 0:
                                    w += [("dv", m_qT)] + BW[0] + BW[1]
                                elif t_ == 1:
                                    w += BW[2] + BW[3]
                            P.do("tensor",
                                 lambda t, t_=t_, p=p, r0=r0, rb=rb, qsl=qsl: t.matmul(
                                     bank(rb),
                                     kT[r0:r0 + 64,
                                        p * c.T + t_ * 128: p * c.T + (t_ + 1) * 128],
                                     qT[r0:r0 + 64,
                                        qsl + p * c.TBLK: qsl + (p + 1) * c.TBLK],
                                     start=True, stop=True),
                                 waits=w, inc="pe" if half == 1 else None)
                        EH[("qk", gi)] = P.cnt["pe"]

                    def emit_exp(t_, base=base):
                        gi = base + t_
                        sp = gi % 2
                        sl4 = gi % 4
                        EH[gi] = P.do(
                            "scalar", lambda e, sp=sp, sl4=sl4: e.activation(
                                pT[:, sl4 * 1024: (sl4 + 1) * 1024],
                                bank(2 * sp, 0, 1024),
                                AF.Exp, scale=SC, bias=negBv[:, 0:1]),
                            waits=[("pe", EH[("qk", gi)]),
                                   ("pe", PH.get(gi - 4, 0)),
                                   ("pe", pvstate.get("pT_free", 0)),
                                   ("dv", m_negB)], inc="ac")

                    def emit_pv(t_, p=p, obase=obase, base=base):
                        gi = base + t_
                        sl4 = gi % 4
                        hls = (2 * p, 2 * p + 1)
                        for half in range(2):
                            ob = obase + half
                            hl = hls[half]
                            w = []
                            if half == 0:
                                w = [("ac", EH[gi])]
                                if t_ == 0:
                                    w += BW[obase] + BW[obase + 1]
                            P.do("tensor",
                                 lambda t, t_=t_, hl=hl, sl4=sl4, ob=ob, half=half:
                                 t.matmul(
                                     bank(ob, 0, 512)[0:65, :],
                                     vP[:, t_ * c.VW + hl * 65:
                                        t_ * c.VW + hl * 65 + 65],
                                     pT[:, sl4 * 1024 + half * 512:
                                        sl4 * 1024 + (half + 1) * 512],
                                     start=(t_ == 0), stop=(t_ == c.KT - 1)),
                                 waits=w, inc="pe" if half == 1 else None)
                        PH[gi] = P.cnt["pe"]
                        if (bgq and obase != bgq_meta["ob0"]
                                and t_ % 4 == 0):
                            bgq.pop(0)()

                    for u in range(c.KT // 2):
                        emit_qk(2 * u)
                        emit_exp(2 * u)
                        emit_qk(2 * u + 1)
                        if u > 0:
                            emit_pv(2 * u - 2)
                        emit_exp(2 * u + 1)
                        if u > 0:
                            emit_pv(2 * u - 1)
                        if u >= min(2, c.KT // 2 - 1) and pending_epi:
                            pending_epi.pop(0)()
                        if (u == min(4, c.KT // 2 - 1) and p == 0
                                and tailstate["fn"] is not None):
                            tailstate["fn"]()
                            tailstate["fn"] = None
                        if p == 2 and u >= 3 and prolstate["chunks"]:
                            prolstate["chunks"].pop(0)()
                    emit_pv(c.KT - 2)
                    emit_pv(c.KT - 1)
                    if p == 2:
                        while prolstate["chunks"]:
                            prolstate["chunks"].pop(0)()
                    ringstate["i"] += c.KT
                    while pending_epi:
                        pending_epi.pop(0)()
                    pending_epi = [
                        (lambda p=p, obase=obase, m=P.cnt["pe"], h=h:
                         emit_epilogue(p, obase, m, halves=(h,)))
                        for h in range(2)]
                while bgq:
                    bgq.pop(0)()
                war_qT["v"] = [("pe", P.cnt["pe"])]
                mac_all = P.cnt["ac"]
                for b_ in range(4):
                    BW[b_] = [("ac", mac_all)]
                if qb + 1 < c.NTB:
                    if c.CH >= 3:
                        assert not prolstate["chunks"]
                        prol = prolstate["m_qT"]
                    else:
                        prol = emit_prologueB(qb + 1, normsB, g=g)
                    ob0_now = 6 if ((ringstate["pgc"] - 1) % 2 == 0) else 4
                    def mk_tail(qb=qb, g=g, ep=pending_epi, ob0=ob0_now):
                        def fn():
                            for f in ep:
                                f()
                            ep.clear()
                            emit_outproj(qb, g, ob0)
                        return fn
                    tailstate["fn"] = mk_tail()
                    pending_epi = []
                else:
                    assert tailstate["fn"] is None
                    while pending_epi:
                        pending_epi.pop(0)()
                    emit_outproj(qb, g,
                                 6 if ((ringstate["pgc"] - 1) % 2 == 0) else 4,
                                 spread=False)
            war_wkv["v"] = [("pe", P.cnt["pe"])]
            war_kT["v"] = [("pe", P.cnt["pe"])]
            war_vP["v"] = [("pe", P.cnt["pe"])]
            war_qT["v"] = [("pe", P.cnt["pe"])]

        def phase3_prologue(M_END, tz):
            s1T, s2T, wch, wch2, sgn = (tz["s1T"], tz["s2T"], tz["wch"],
                                        tz["wch2"], tz["sgn"])
            war_wch = {0: list(BW[0]) + list(BW[1]) + M_END, 1: list(M_END)}
            war_sgn = {0: list(M_END), 1: list(M_END)}
            sgn_chain = {"v": list(M_END)}
            war_red = []
            ms = {}
            for fc in range(c.FC):
                sl = fc % 2
                WC = wch[:, sl * c.D:(sl + 1) * c.D]
                SG = sgn[:, 0:c.D]
                mw = dma("sync", WC, W1[fc * 128:(fc + 1) * 128, :],
                         waits=war_wch[sl])
                mrd = P.do("vector", lambda v, WC=WC: v.tensor_reduce(
                    stats[:, 18:19], WC, axis=AX.X, op=AluOpType.add,
                    apply_absolute_value=True), waits=[mw] + war_red, inc="dv")
                mal = P.do("vector", lambda v, fc=fc: v.tensor_scalar(
                    alpha1[:, fc:fc + 1], stats[:, 18:19], 1.0 / c.D, None,
                    AluOpType.mult, AluOpType.bypass), waits=[("dv", mrd)], inc="dv")
                war_red = [("dv", mal)]
                msg = P.do("scalar", lambda e, WC=WC, SG=SG: e.activation(
                    SG, WC, AF.Sign),
                    waits=[mw] + sgn_chain["v"], inc="ac")
                war_wch[sl] = [("dv", mrd), ("ac", msg)]

                def dst(j0, cn, fc=fc):
                    return apx(s1T, j0 * c.F + fc * 128, [[c.F, cn], [1, 128]])
                (semn, mev), mtp = transp(SG, c.DC, 0 if sl == 0 else 2, dst,
                                          wait_first=[("ac", msg)], ident=id_bf)
                sgn_chain["v"] = [("pe", mtp)]
            ms["s1T"] = P.cnt["dv"]
            war_wch = {0: war_wch[0] + war_wch[1], 1: war_wch[0] + war_wch[1]}
            for dc in range(c.DC):
                sl = dc % 2
                WC = wch2[:, sl * c.F:(sl + 1) * c.F]
                SG = sgn[:, 0:c.F]
                mw = dma("sync", WC, W2[dc * 128:(dc + 1) * 128, :],
                         waits=war_wch[sl])
                mrd = P.do("vector", lambda v, WC=WC: v.tensor_reduce(
                    stats[:, 18:19], WC, axis=AX.X, op=AluOpType.add,
                    apply_absolute_value=True), waits=[mw] + war_red, inc="dv")
                mal = P.do("vector", lambda v, dc=dc: v.tensor_scalar(
                    alpha2[:, dc:dc + 1], stats[:, 18:19], 1.0 / c.F, None,
                    AluOpType.mult, AluOpType.bypass), waits=[("dv", mrd)], inc="dv")
                war_red = [("dv", mal)]
                msg = P.do("scalar", lambda e, WC=WC, SG=SG: e.activation(
                    SG, WC, AF.Sign),
                    waits=[mw] + sgn_chain["v"], inc="ac")
                war_wch[sl] = [("dv", mrd), ("ac", msg)]

                def dst(j0, cn, dc=dc):
                    return apx(s2T, j0 * c.D + dc * 128, [[c.D, cn], [1, 128]])
                (semn, mev), mtp = transp(SG, c.FC, 0 if sl == 0 else 2, dst,
                                          wait_first=[("ac", msg)], ident=id_bf)
                sgn_chain["v"] = [("pe", mtp)]
            ms["s2T"] = P.cnt["dv"]
            ms["alpha"] = P.cnt["dv"]
            return ms

        def phase3(M_END, ms, tz):
            s1T, s2T, hT = tz["s1T"], tz["s2T"], tz["hT"]
            x3, x1, ln3T = tz["x3"], tz["x1"], tz["ln3T"]
            ln3f, ln3b = tz["ln3f"], tz["ln3b"]
            o2tm, oscr = tz["o2tm"], tz["oscr"]
            out_dmas = phase3.out_dmas = []
            m_y_done = tz["m_y_done"]
            MP = tz["m_prol"]
            war_x3 = {0: list(M_END) + MP, 1: list(M_END) + MP}
            war_x1 = {0: MP, 1: MP}
            war_ln3f = {k: MP for k in range(2 * (c.TB3 // 128))}
            war_ln3b = {k: MP for k in range(2 * (c.TB3 // 128))}
            war_ln3T = {0: MP, 1: MP}
            war_hT = {"v": []}
            war_o2 = []
            nsub = c.TB3 // 128
            DCT = c.DC * c.TB3

            def front(tb):
                t0 = tb * c.TB3
                sl = tb % 2
                X3 = lambda s: x3[:, (sl * nsub + s) * c.D:(sl * nsub + s + 1) * c.D]
                X1 = lambda s: x1[:, (sl * nsub + s) * c.D:(sl * nsub + s + 1) * c.D]
                L3s = lambda s: ln3f[:, (sl * nsub + s) * c.D:(sl * nsub + s + 1) * c.D]
                mx = []
                for s in range(nsub):
                    mx.append(dma("sync", X3(s),
                                  X[t0 + s * 128: t0 + 128 * (s + 1), :],
                                  waits=war_x3[sl] if s == 0 else ()))
                pows = []
                for s in range(nsub):
                    ysl = y_acc[:, (t0 // 128 + s) * c.D:(t0 // 128 + s + 1) * c.D]
                    pows.append(ln_stats(ysl, sl * 2 + s,
                                         m_y_done if tb == 0 else []))
                m_x1 = []
                for s in range(nsub):
                    ysl = y_acc[:, (t0 // 128 + s) * c.D:(t0 // 128 + s + 1) * c.D]
                    mn = ln_norm(ysl, L3s(s), sl * 2 + s, pows[s],
                                 extra_waits=war_ln3f[sl * nsub + s])
                    ma = P.do("vector", lambda v, s=s, X3=X3, X1=X1, L3s=L3s:
                              v.tensor_tensor(X1(s), X3(s), L3s(s), AluOpType.add),
                              waits=[("dv", mn), mx[s]]
                              + (war_x1[sl] if s == 0 else []), inc="dv")
                    war_ln3f[sl * nsub + s] = [("dv", ma)]
                    m_x1.append(ma)
                pows = []
                for s in range(nsub):
                    pows.append(ln_stats(X1(s), sl * 2 + s, [("dv", m_x1[s])]))
                norms = []
                for s in range(nsub):
                    L3 = ln3b[:, (sl * nsub + s) * c.D:(sl * nsub + s + 1) * c.D]
                    mn = ln_norm(X1(s), L3, sl * 2 + s, pows[s],
                                 extra_waits=war_ln3b[sl * nsub + s])
                    norms.append(mn)
                return norms

            def emit_ln3T(tb, norms):
                sl = tb % 2
                m_ln3 = None
                for s_ in range(nsub):
                    L3 = ln3b[:, (sl * nsub + s_) * c.D:(sl * nsub + s_ + 1) * c.D]

                    def dst(j0, cn, s_=s_, sl=sl):
                        return apx(ln3T, sl * DCT + j0 * c.TB3 + s_ * 128,
                                   [[c.TB3, cn], [1, 128]])
                    (semn, mev), mtp = transp(L3, c.DC, 0, dst,
                                              wait_first=[("dv", norms[s_])]
                                              + war_ln3T[sl], ident=id_bf)
                    war_ln3b[sl * nsub + s_] = [("pe", mtp)]
                    m_ln3 = mev
                return m_ln3

            f_norms = front(0)
            pend_t = {"norms": f_norms, "tb": 0}
            for tb in range(c.NT3):
                t0 = tb * c.TB3
                sl = tb % 2
                X3 = lambda s, sl=sl: x3[:, (sl * nsub + s) * c.D:
                                         (sl * nsub + s + 1) * c.D]
                X1 = lambda s, sl=sl: x1[:, (sl * nsub + s) * c.D:
                                         (sl * nsub + s + 1) * c.D]
                L3s = lambda s, sl=sl: ln3f[:, (sl * nsub + s) * c.D:
                                            (sl * nsub + s + 1) * c.D]
                if pend_t.get("m_ln3") is None:
                    m_ln3 = emit_ln3T(tb, pend_t["norms"])
                else:
                    m_ln3 = pend_t["m_ln3"]
                for fc in range(c.FC):
                    pb = 2 + (fc % 2)
                    for dc in range(c.DC):
                        w = (BW[pb] + [("dv", m_ln3),
                                       ("dv", ms["s1T"])]) if dc == 0 else ()
                        P.do("tensor", lambda t, fc=fc, dc=dc, pb=pb, sl=sl:
                             t.matmul(
                                 bank(pb, 0, c.TB3),
                                 s1T[:, dc * c.F + fc * 128: dc * c.F + (fc + 1) * 128],
                                 ln3T[:, sl * DCT + dc * c.TB3:
                                      sl * DCT + (dc + 1) * c.TB3],
                                 start=(dc == 0), stop=(dc == c.DC - 1)),
                             waits=w, inc="pe" if dc == c.DC - 1 else None)
                    m_mm = P.cnt["pe"]
                    war_ln3T[sl] = [("pe", m_mm)]
                    mg = P.do("scalar", lambda e, fc=fc, pb=pb: e.activation(
                        hT[:, fc * c.TB3:(fc + 1) * c.TB3], bank(pb, 0, c.TB3),
                        AF.Gelu, scale=alpha1[:, fc:fc + 1], bias=b1s[:, fc:fc + 1]),
                        waits=[("pe", m_mm), ("dv", ms["alpha"])] + m_bload
                        + war_hT["v"], inc="ac")
                    BW[pb] = [("ac", mg)]
                m_h = P.cnt["ac"]
                if tb + 1 < c.NT3:
                    nxt = front(tb + 1)
                    pend_t = {"norms": nxt, "tb": tb + 1, "m_ln3": None}
                for dc in range(c.DC):
                    pb = 4 + (dc % 2)
                    for fc in range(c.FC):
                        w = (BW[pb] + [("ac", m_h), ("dv", ms["s2T"])]) if fc == 0 else ()
                        P.do("tensor", lambda t, fc=fc, dc=dc, pb=pb: t.matmul(
                            bank(pb, 0, c.TB3),
                            s2T[:, fc * c.D + dc * 128: fc * c.D + (dc + 1) * 128],
                            hT[:, fc * c.TB3:(fc + 1) * c.TB3],
                            start=(fc == 0), stop=(fc == c.FC - 1)),
                            waits=w, inc="pe" if fc == c.FC - 1 else None)
                    m_mm = P.cnt["pe"]
                    war_hT["v"] = [("pe", m_mm)]
                    mo = P.do("scalar", lambda e, dc=dc, pb=pb: e.activation(
                        oscr, bank(pb, 0, c.TB3), AF.Identity,
                        scale=alpha2[:, dc:dc + 1], bias=b2s[:, dc:dc + 1]),
                        waits=[("pe", m_mm), ("dv", ms["alpha"])] + m_bload
                        + war_o2, inc="ac")
                    BW[pb] = [("ac", mo)]

                    def dst(j0, cn, dc=dc):
                        return apx(o2tm, j0 * c.D + dc * 128, [[c.D, cn], [1, 128]])
                    (semn, mev), mtp = transp(oscr, nsub, 6, dst,
                                              wait_first=[("ac", mo)], ident=id16)
                    war_o2 = [("pe", mtp)]
                    m_o2 = mev
                    if (dc == min(3, c.DC - 1) and tb + 1 < c.NT3
                            and pend_t["m_ln3"] is None):
                        pend_t["m_ln3"] = emit_ln3T(tb + 1, pend_t["norms"])
                pows = []
                for s in range(nsub):
                    pows.append(ln_stats(o2tm[:, s * c.D:(s + 1) * c.D],
                                         sl * 2 + s, [("dv", m_o2)]))
                for s in range(nsub):
                    mn = ln_norm(o2tm[:, s * c.D:(s + 1) * c.D], L3s(s),
                                 sl * 2 + s, pows[s],
                                 extra_waits=war_ln3f[sl * nsub + s])
                    ma = P.do("vector", lambda v, s=s, X3=X3, X1=X1, L3s=L3s:
                              v.tensor_tensor(X3(s), X1(s), L3s(s), AluOpType.add),
                              waits=[("dv", mn)], inc="dv")
                    war_ln3f[sl * nsub + s] = [("dv", ma)]
                    mo_ = dma("sync", OUT[t0 + s * 128: t0 + (s + 1) * 128, :],
                              X3(s), waits=[("dv", ma)])
                    out_dmas.append(mo_)
                war_x3[sl] = wall(out_dmas[-nsub:])
                war_x1[sl] = [("dv", P.cnt["dv"])]

        m_yg0 = [0]
        phase_A(0)
        phase_B(0)
        m_yg0[0] = P.cnt["ac"]
        phase_A(1)
        phase_B(1)
        M_END = [("pe", P.cnt["pe"]), ("dv", P.cnt["dv"]), ("ac", P.cnt["ac"]),
                 ("gp", P.cnt["gp"])]
        m_y_done = [("dv", P.cnt["dv"]), ("ac", P.cnt["ac"])]
        st2.close()
        s1T = sb("s1T", [128, c.DC * c.F], FP8)
        s2T = sb("s2T", [128, c.FC * c.D], FP8)
        hT = sb("hT", [128, c.FC * c.TB3], BF16)
        o2tm = sb("o2tm", [128, (c.TB3 // 128) * c.D], FP16)
        oscr = sb("oscr", [128, c.TB3], FP16)
        st3 = ExitStack()
        wch = st3.enter_context(nc.sbuf_tensor("wch", [128, 2 * c.D], F32))[:, :]
        wch2 = st3.enter_context(nc.sbuf_tensor("wch2", [128, 2 * c.F], F32))[:, :]
        sgn = st3.enter_context(
            nc.sbuf_tensor("sgn", [128, max(c.D, c.F)], BF16))[:, :]
        p3tensors = dict(s1T=s1T, s2T=s2T, hT=hT, o2tm=o2tm, oscr=oscr,
                         wch=wch, wch2=wch2, sgn=sgn)
        ms = phase3_prologue(M_END, p3tensors)
        m_prol = [("pe", P.cnt["pe"]), ("dv", P.cnt["dv"]), ("ac", P.cnt["ac"])]
        st3.close()
        nsub3 = c.TB3 // 128
        p3tensors["x3"] = sb("x3", [128, 2 * nsub3 * c.D], F32)
        p3tensors["x1"] = p3tensors["x3"]
        p3tensors["ln3f"] = sb("ln3f", [128, 2 * (c.TB3 // 128) * c.D], F32)
        p3tensors["ln3b"] = sb("ln3b", [128, 2 * (c.TB3 // 128) * c.D], BF16)
        p3tensors["lnscr"] = None
        p3tensors["ln3T"] = sb("ln3T", [128, 2 * c.DC * c.TB3], BF16)
        p3tensors["m_y_done"] = m_y_done
        p3tensors["m_prol"] = m_prol
        phase3(M_END, ms, p3tensors)

        P.do("sync", lambda e: e.nop(), waits=wall(phase3.out_dmas))

        P.emit(block, semh, first_waits=[("gp", m_init)])
    return nc


def kernel(L0_attn_x, L0_attn_w_qkv, L0_attn_w_out, L0_ffn_w1, L0_ffn_b1,
           L0_ffn_w2, L0_ffn_b2):
    x = np.ascontiguousarray(np.asarray(L0_attn_x, dtype=np.float32))
    wqkv = np.ascontiguousarray(np.asarray(L0_attn_w_qkv, dtype=np.float32))
    wout = np.ascontiguousarray(np.asarray(L0_attn_w_out, dtype=np.float32))
    w1 = np.ascontiguousarray(np.asarray(L0_ffn_w1, dtype=np.float32))
    b1 = np.ascontiguousarray(np.asarray(L0_ffn_b1, dtype=np.float32))
    w2 = np.ascontiguousarray(np.asarray(L0_ffn_w2, dtype=np.float32))
    b2 = np.ascontiguousarray(np.asarray(L0_ffn_b2, dtype=np.float32))

    cfg = Cfg(T=4096, D=1024, H=16, F=4096)
    nc = build_nc(cfg)
    in_maps = []
    for b in range(8):
        in_maps.append({"x": x[b], "w_qkv": wqkv, "w_out": wout,
                        "w1": w1, "b1": b1, "w2": w2, "b2": b2})
    res = run_bass_kernel_spmd(nc, in_maps, core_ids=list(range(8)))
    out = np.stack([res.results[b]["out"] for b in range(8)], axis=0)
    return out.astype(np.float32)


# version 33
# speedup vs baseline: 1.2580x; 1.0088x over previous
import math
from contextlib import ExitStack

import numpy as np
import concourse.bass as bass
import concourse.mybir as mybir
from concourse.alu_op_type import AluOpType
from concourse.bass_utils import run_bass_kernel_spmd

F32 = mybir.dt.float32
BF16 = mybir.dt.bfloat16
FP16 = mybir.dt.float16
FP8 = mybir.dt.float8e4
AF = mybir.ActivationFunctionType
AX = mybir.AxisListType

ENGINES = ("tensor", "vector", "scalar", "gpsimd", "sync")
SEMS = (("pe", "dv", "ac", "gp", "out")
        + tuple(f"qs{i}" for i in range(8)) + tuple(f"qg{i}" for i in range(8)))


def wall(tuples):
    d = {}
    for s_, v in tuples:
        if v:
            d[s_] = max(d.get(s_, 0), v)
    return list(d.items())


class Cfg:
    def __init__(self, T=4096, D=1024, H=16, F=4096):
        self.T, self.D, self.H, self.F = T, D, H, F
        self.DH = 64
        assert H * self.DH == D
        self.DC = D // 128
        self.FC = F // 128
        self.KT = T // 128
        self.G = 2
        self.HG = H // 2
        self.CH = self.HG * 64 // 128
        self.TBLK = 512
        self.NTB = T // self.TBLK
        self.TB3 = 256
        self.NT3 = T // self.TB3
        self.VW = self.HG * 65
        self.scale = 1.0 / math.sqrt(D)
        self.kgroups = []
        k = 0
        while k < self.KT:
            n = min(3, self.KT - k)
            self.kgroups.append((k, n))
            k += n


def apx(view, offset, dims):
    p = view.ap[0]
    return bass.AP(view.tensor, offset, [[p[0], p[1]]] + [list(d) for d in dims])


class Prog:
    def __init__(self):
        self.ops = {e: [] for e in ENGINES}
        self.cnt = {s: 0 for s in SEMS}

    def do(self, eng, fn, waits=(), inc=None, n=1):
        self.ops[eng].append((fn, tuple(waits), inc, n))
        if inc is not None:
            self.cnt[inc] += n
            return self.cnt[inc]
        return None

    def emit(self, block, sems, first_waits=()):
        for ename in ENGINES:
            def body(eng, ename=ename):
                waited = {}
                if ename != "gpsimd":
                    for s, v in first_waits:
                        if v:
                            eng.wait_ge(sems[s], v)
                            waited[s] = v
                for fn, waits, inc, n in self.ops[ename]:
                    for s, v in waits:
                        if v and waited.get(s, 0) < v:
                            eng.wait_ge(sems[s], v)
                            waited[s] = v
                    instr = fn(eng)
                    if inc is not None:
                        instr.then_inc(sems[inc], n)
            getattr(block, ename)(body)


def build_nc(cfg: Cfg):
    c = cfg
    nc = bass.Bass("TRN2", target_bir_lowering=False)

    X = nc.dram_tensor("x", [c.T, c.D], F32, kind="ExternalInput").ap()
    Wqkv = nc.dram_tensor("w_qkv", [c.D, 3 * c.D], F32, kind="ExternalInput").ap()
    Wout = nc.dram_tensor("w_out", [c.D, c.D], F32, kind="ExternalInput").ap()
    W1 = nc.dram_tensor("w1", [c.F, c.D], F32, kind="ExternalInput").ap()
    B1 = nc.dram_tensor("b1", [c.F], F32, kind="ExternalInput").ap()
    W2 = nc.dram_tensor("w2", [c.D, c.F], F32, kind="ExternalInput").ap()
    B2 = nc.dram_tensor("b2", [c.D], F32, kind="ExternalInput").ap()
    OUT = nc.dram_tensor("out", [c.T, c.D], F32, kind="ExternalOutput").ap()

    P = Prog()
    with ExitStack() as st:
        def sb(name, shape, dtype):
            h = st.enter_context(nc.sbuf_tensor(name, shape, dtype))
            return h[tuple(slice(None) for _ in shape)]

        id32 = sb("id32", [128, 128], F32)
        id_bf = sb("id_bf", [128, 128], BF16)
        id16 = sb("id16", [128, 128], FP16)
        ones16 = sb("ones16", [1, 128], FP16)
        ones_bf = sb("ones_bf", [128, 1], BF16)
        eps_t = sb("eps_t", [128, 1], F32)
        mhalf = sb("mhalf", [128, 1], F32)
        alpha1 = sb("alpha1", [128, c.FC], F32)
        alpha2 = sb("alpha2", [128, c.DC], F32)
        b1s = sb("b1s", [128, c.FC], F32)
        b2s = sb("b2s", [128, c.DC], F32)
        y_acc = sb("y_acc", [128, (c.T // 128) * c.D], FP16)
        kmaxs = sb("kmaxs", [1, 2], F32)

        stats = sb("stats", [128, 80], F32)
        qmax4 = sb("qmax4", [128, 8], F32)
        sc1 = sb("sc1", [1, 8], F32)
        negBv = sb("negBv", [128, 1], F32)

        st2 = ExitStack()

        def sb2(name, shape, dtype):
            h = st2.enter_context(nc.sbuf_tensor(name, shape, dtype))
            return h[tuple(slice(None) for _ in shape)]

        kT = sb2("kT", [128, c.CH * c.T], BF16)
        vP = sb2("vP", [128, c.KT * c.VW], BF16)
        wsh = sb2("wsh", [128, c.DC * c.HG * 64 + (c.HG // 2) * c.D], BF16)
        wkv = wsh[:, 0:c.DC * c.HG * 128]
        wq = wsh[:, 0:c.DC * c.HG * 64]
        wo = wsh[:, c.DC * c.HG * 64:]
        xs = sb2("xs", [128, 4 * c.D], F32)
        lnx16 = sb2("lnx16", [128, 4 * c.D], BF16)
        lnxT = sb2("lnxT", [128, c.DC * c.TBLK], BF16)
        qT = sb2("qT", [128, 2 * c.CH * c.TBLK], BF16)
        oT = sb2("oT", [128, (c.HG // 2) * c.TBLK], BF16)
        pT = sb2("pT", [128, 4 * 1024], BF16)
        escr = sb2("escr", [128, 1024], F32)
        otmp = escr[0:64, :]
        recl = escr[64:65, :]
        vtmp16 = pT[:, 0:512]
        recl16 = sb2("recl16", [1, 520], FP16)

        psh = st.enter_context(nc.psum_tensor("ps", [128, 4096], F32))
        ps = psh[:, :]

        def bank(i, lo=0, width=512):
            return ps[:, 512 * i + lo: 512 * i + lo + width]

        semh = {s: st.enter_context(nc.semaphore(s)) for s in SEMS}
        block = st.enter_context(nc.Block())

        SC = c.scale
        BW = {i: [] for i in range(8)}

        rr = {"sync": 0, "gpsimd": 0}

        def dma(q, out, in_, waits=()):
            pre = "qs" if q == "sync" else "qg"
            sem = f"{pre}{rr[q] % 8}"
            rr[q] += 1
            chain = [(sem, P.cnt[sem])]
            v = P.do(q, lambda e, o=out, i=in_: e.dma_start(out=o, in_=i),
                     waits=list(waits) + chain, inc=sem, n=16)
            return (sem, v)

        m0 = P.do("gpsimd", lambda g: g.memset(id32, 0.0), inc="gp")

        P.do("gpsimd", lambda g: g.affine_select(
            out=id32, in_=id32, compare_op=AluOpType.not_equal, fill=1.0,
            base=0, pattern=[[-1, 128]], channel_multiplier=1),
            waits=[("gp", m0)], inc="gp")
        mib = P.do("gpsimd", lambda g: g.memset(id_bf, 0.0), inc="gp")
        P.do("gpsimd", lambda g: g.affine_select(
            out=id_bf, in_=id_bf, compare_op=AluOpType.not_equal, fill=1.0,
            base=0, pattern=[[-1, 128]], channel_multiplier=1),
            waits=[("gp", mib)], inc="gp")
        mih = P.do("gpsimd", lambda g: g.memset(id16, 0.0), inc="gp")
        P.do("gpsimd", lambda g: g.affine_select(
            out=id16, in_=id16, compare_op=AluOpType.not_equal, fill=1.0,
            base=0, pattern=[[-1, 128]], channel_multiplier=1),
            waits=[("gp", mih)], inc="gp")
        P.do("gpsimd", lambda g: g.memset(ones16, 1.0), inc="gp")
        P.do("gpsimd", lambda g: g.memset(ones_bf, 1.0), inc="gp")
        P.do("gpsimd", lambda g: g.memset(eps_t, 1e-5), inc="gp")
        P.do("gpsimd", lambda g: g.memset(mhalf, -0.5), inc="gp")
        P.do("gpsimd", lambda g: g.memset(
            apx(vP, 64, [[c.VW, c.KT], [65, c.HG]]), 1.0), inc="gp")
        m_init = P.do("gpsimd", lambda g: g.memset(oT, 0.0), inc="gp")

        m_bload = []
        for fc in range(c.FC):
            m_bload.append(dma("sync", b1s[:, fc:fc + 1],
                               bass.AP(B1.tensor, fc * 128, [[1, 128], [1, 1]])))
        for dc in range(c.DC):
            m_bload.append(dma("sync", b2s[:, dc:dc + 1],
                               bass.AP(B2.tensor, dc * 128, [[1, 128], [1, 1]])))
        m_bload = wall(m_bload)

        slot_pow = {}
        slot_norm = {}

        def ln_stats(src, slot, wait_src):
            o = slot * 20
            ts_ = min(512, c.D)
            nh = c.D // ts_
            for i in range(nh):
                P.do("vector",
                     lambda v, i=i, o=o, s=src, ts_=ts_: v.bn_stats(
                         stats[:, o + 6 * i:o + 6 * i + 6],
                         s[:, ts_ * i:ts_ * (i + 1)]),
                     waits=list(wait_src) + [("gp", slot_pow.get(slot, 0)),
                                             ("dv", slot_norm.get(slot, 0))]
                     if i == 0 else (),
                     inc="dv" if i == nh - 1 else None)
            m1 = P.cnt["dv"]
            m2 = P.do("vector", lambda v, o=o, n=nh: v.bn_aggr(
                stats[:, o + 12:o + 14], stats[:, o:o + 6 * n]),
                waits=[("dv", m1)], inc="dv")
            m3 = P.do("vector", lambda v, o=o: v.tensor_scalar(
                stats[:, o + 14:o + 15], stats[:, o + 13:o + 14], eps_t[:, 0:1],
                None, AluOpType.add, AluOpType.bypass),
                waits=[("dv", m2), ("gp", slot_pow.get(slot, 0))], inc="dv")
            m4 = P.do("gpsimd", lambda g, o=o: g.tensor_tensor(
                stats[:, o + 15:o + 16], stats[:, o + 14:o + 15], mhalf,
                AluOpType.pow),
                waits=[("dv", m3), ("gp", m_init),
                       ("dv", slot_norm.get(slot, 0))], inc="gp")
            slot_pow[slot] = m4
            return m4

        def ln_norm(src, dst, slot, m4, extra_waits=()):
            o = slot * 20
            m5 = P.do("vector", lambda v, o=o: v.scalar_tensor_tensor(
                stats[:, o + 16:o + 17], stats[:, o + 12:o + 13], -1.0,
                stats[:, o + 15:o + 16], AluOpType.mult, AluOpType.mult),
                waits=[("gp", m4)], inc="dv")
            m6 = P.do("vector", lambda v, o=o, s=src, d=dst: v.tensor_scalar(
                d, s, stats[:, o + 15:o + 16], stats[:, o + 16:o + 17],
                AluOpType.mult, AluOpType.add),
                waits=[("dv", m5)] + list(extra_waits), inc="dv")
            slot_norm[slot] = m6
            return m6

        def transp(src, n, b0, dst_of, wait_first, evac_eng="vector", nb=2,
                   ident=None):
            sem = "dv" if evac_eng == "vector" else "ac"
            idt = id32 if ident is None else ident
            step = 4
            last = None
            g0 = 0
            while g0 < n:
                g1 = min(g0 + step, n)
                bi = b0 + ((g0 // 4) % nb if step == 4 else 0)
                bw = BW[bi] if step == 4 else BW[b0] + BW[b0 + 1]
                for j in range(g0, g1):
                    w = []
                    if j == 0:
                        w += list(wait_first)
                    if j == g0:
                        w += bw
                    P.do("tensor", lambda t, j=j, bi=bi, g0=g0, s=src, idt=idt:
                         (t.transpose(
                             bank(bi + (j - g0) // 4, 128 * ((j - g0) % 4), 128),
                             s[:, 128 * j:128 * (j + 1)], idt)
                          if idt is id32 else
                          t.matmul(
                             bank(bi + (j - g0) // 4, 128 * ((j - g0) % 4), 128),
                             s[:, 128 * j:128 * (j + 1)], idt,
                             start=True, stop=True)),
                         waits=w, inc="pe" if j == g1 - 1 else None)
                mpe = P.cnt["pe"]
                cn = g1 - g0
                if evac_eng == "vector":
                    last = P.do("vector", lambda v, bi=bi, g0=g0, cn=cn: v.tensor_copy(
                        dst_of(g0, cn), bank(bi, 0, 128 * cn)),
                        waits=[("pe", mpe)], inc="dv")
                else:
                    last = P.do("scalar", lambda e, bi=bi, g0=g0, cn=cn: e.activation(
                        dst_of(g0, cn), bank(bi, 0, 128 * cn), AF.Copy),
                        waits=[("pe", mpe)], inc="ac")
                if step == 4:
                    BW[bi] = [(sem, last)]
                else:
                    BW[b0] = [(sem, last)]
                    BW[b0 + 1] = [(sem, last)]
                g0 = g1
            return (sem, last), P.cnt["pe"]

        st_blk = {"xw": [[] for _ in range(4)],
                  "l16w": [[] for _ in range(4)],
                  "lnxT_users": [],
                  "m_lnxT_sem": "dv", "evac_eng": "vector"}

        def ln1_part1(t0_base):
            nsub = c.TBLK // 128
            pows, loads = [], []
            for s in range(nsub):
                mx = dma("sync", xs[:, s * c.D:(s + 1) * c.D],
                         X[t0_base + s * 128: t0_base + s * 128 + 128, :],
                         waits=st_blk["xw"][s])
                loads.append(mx)
            for s in range(nsub):
                pows.append(ln_stats(xs[:, s * c.D:(s + 1) * c.D], s, [loads[s]]))
            norms = []
            for s in range(nsub):
                xsl = xs[:, s * c.D:(s + 1) * c.D]
                l16 = lnx16[:, s * c.D:(s + 1) * c.D]
                mn = ln_norm(xsl, l16, s, pows[s],
                             extra_waits=st_blk["l16w"][s])
                st_blk["xw"][s] = [("dv", mn)]
                norms.append(mn)
            return norms

        def ln1_part2(norms, b0=0):
            nsub = c.TBLK // 128
            mlast = None
            for s in range(nsub):
                l16 = lnx16[:, s * c.D:(s + 1) * c.D]
                def dst(j0, cn, s=s):
                    return apx(lnxT, j0 * c.TBLK + s * 128, [[c.TBLK, cn], [1, 128]])
                (semn, mev), mtp = transp(
                    l16, c.DC, b0, dst,
                    wait_first=[("dv", norms[s])]
                    + [("pe", u) for u in st_blk["lnxT_users"][-1:]],
                    evac_eng=st_blk["evac_eng"], ident=id_bf)
                st_blk["l16w"][s] = [("pe", mtp)]
                st_blk["m_lnxT_sem"] = semn
                mlast = mev
            return mlast

        war_wkv = {"v": []}
        war_kT = {"v": []}
        war_vP = {"v": []}
        m_kv_done = {}

        def phase_A(g):
            W2G = c.HG * 128
            m_w = []
            for dc in range(c.DC):
                m_w.append(dma("gpsimd", wkv[:, dc * W2G: dc * W2G + c.HG * 64],
                    Wqkv[dc * 128:(dc + 1) * 128,
                         c.D + g * c.HG * 64: c.D + (g + 1) * c.HG * 64],
                    waits=(war_wkv["v"] + [("gp", m_init)]) if dc == 0 else ()))
                m_w.append(dma("gpsimd", wkv[:, dc * W2G + c.HG * 64: (dc + 1) * W2G],
                    Wqkv[dc * 128:(dc + 1) * 128,
                         2 * c.D + g * c.HG * 64: 2 * c.D + (g + 1) * c.HG * 64]))
            m_w = wall(m_w)
            mk0 = P.do("vector", lambda v: v.memset(qmax4[:, 7:8], 0.0),
                       waits=war_kT["v"], inc="dv")
            kmax_run = mk0

            nmk = c.HG * 128 // 128
            nK = nmk // 2
            BW_vtmp[:] = [("dv", P.cnt["dv"]), ("pe", P.cnt["pe"])]
            st_blk["evac_eng"] = "scalar"
            norms = ln1_part1(0)
            for tb in range(c.NTB):
                m_lnxT = ln1_part2(norms)
                if tb + 1 < c.NTB:
                    norms = ln1_part1((tb + 1) * c.TBLK)
                for mc in range(nmk):
                    pb = 2 + (mc % 2)
                    is_v = mc >= nK
                    for dc in range(c.DC):
                        w = (BW[pb] + m_w
                             + [(st_blk["m_lnxT_sem"], m_lnxT)]) if dc == 0 else ()
                        P.do("tensor", lambda t, dc=dc, mc=mc, pb=pb: t.matmul(
                            bank(pb),
                            wkv[:, dc * W2G + mc * 128: dc * W2G + (mc + 1) * 128],
                            lnxT[:, dc * c.TBLK:(dc + 1) * c.TBLK],
                            start=(dc == 0), stop=(dc == c.DC - 1)),
                            waits=w, inc="pe" if dc == c.DC - 1 else None)
                    m_mm = P.cnt["pe"]
                    st_blk["lnxT_users"] = [m_mm]
                    if not is_v:
                        mea = P.do("scalar", lambda e, mc=mc, tb=tb, pb=pb: e.activation(
                            kT[:, mc * c.T + tb * c.TBLK: mc * c.T + (tb + 1) * c.TBLK],
                            bank(pb), AF.Copy),
                            waits=[("pe", m_mm)] + war_kT["v"], inc="ac")
                        mr = P.do("vector", lambda v, mc=mc, tb=tb: v.tensor_reduce(
                            qmax4[:, 6:7],
                            kT[:, mc * c.T + tb * c.TBLK: mc * c.T + (tb + 1) * c.TBLK],
                            axis=AX.X, op=AluOpType.max,
                            apply_absolute_value=True),
                            waits=[("ac", mea), ("dv", kmax_run)], inc="dv")
                        kmax_run = P.do("vector", lambda v: v.tensor_tensor(
                            qmax4[:, 7:8], qmax4[:, 7:8], qmax4[:, 6:7],
                            AluOpType.max), waits=[("dv", mr)], inc="dv")
                        BW[pb] = [("ac", mea)]
                    else:
                        mev = P.do("scalar", lambda e, pb=pb: e.activation(
                            vtmp16, bank(pb), AF.Copy),
                            waits=[("pe", m_mm), BW_vtmp[0]] if BW_vtmp else [("pe", m_mm)],
                            inc="ac")
                        BW[pb] = [("ac", mev)]
                        mcv = mc - nK
                        vb = 4 + (mcv % 2)
                        nsub = c.TBLK // 128
                        for j in range(nsub):
                            w = ([("ac", mev)] + BW[vb]) if j == 0 else ()
                            P.do("tensor", lambda t, j=j, vb=vb: t.matmul(
                                bank(vb, 128 * j, 128),
                                vtmp16[:, 128 * j:128 * (j + 1)], id_bf,
                                start=True, stop=True),
                                waits=w, inc="pe" if j == nsub - 1 else None)
                        m_vt = P.cnt["pe"]
                        BW_vtmp[:] = [("pe", m_vt)]
                        mlv = None
                        for h2 in range(2):
                            hl = 2 * mcv + h2
                            mlv = P.do("vector", lambda v, h2=h2, hl=hl, tb=tb,
                                       nsub=nsub, vb=vb:
                                       v.tensor_copy(
                                           apx(vP, tb * nsub * c.VW + hl * 65,
                                               [[c.VW, nsub], [1, 64]]),
                                           apx(ps, 512 * vb + h2 * 64,
                                               [[128, nsub], [1, 64]])),
                                       waits=([("pe", m_vt)] + war_vP["v"]) if h2 == 0 else (),
                                       inc="dv")
                        BW[vb] = [("dv", mlv)]
            war_wq["v"] = [("pe", P.cnt["pe"])]
            w5 = BW[5]
            P.do("tensor", lambda t: t.transpose(bank(5, 0, 128)[0:1, :],
                                                 qmax4[:, 7:8], id32),
                 waits=[("dv", kmax_run)] + w5, inc="pe")
            mt = P.cnt["pe"]
            mk = P.do("vector", lambda v, g=g: v.tensor_reduce(
                kmaxs[:, g:g + 1], bank(5)[0:1, 0:128], axis=AX.X,
                op=AluOpType.max), waits=[("pe", mt)], inc="dv")
            BW[5] = [("dv", mk)]
            m_kv_done[g] = mk

        BW_vtmp = []

        war_qT = {"v": []}
        war_oT = {"v": []}
        war_wq = {"v": []}
        war_otmp = {0: [], 1: []}
        war_recl = {0: [], 1: []}
        war_recl16 = {"v": []}
        tailstate = {"q": []}
        prolstate = {"chunks": [], "m_qT": 0}
        bgq = []
        bgq_meta = {"ob0": -1}
        pvstate = {"m_pv_0": 0, "m_pv_1": 0}
        ringstate = {"exp": {}, "pv": {}, "i": 0, "pgc": 0}

        def phase_B(g):
            m_w = []
            for dc in range(c.DC):
                m_w.append(dma("gpsimd", wq[:, dc * c.HG * 64:(dc + 1) * c.HG * 64],
                    Wqkv[dc * 128:(dc + 1) * 128,
                         g * c.HG * 64:(g + 1) * c.HG * 64],
                    waits=war_wq["v"] if dc == 0 else ()))
            for hp in range(c.HG // 2):
                m_w.append(dma("gpsimd", wo[:, hp * c.D:(hp + 1) * c.D],
                    Wout[(g * c.HG + 2 * hp) * 64:
                         (g * c.HG + 2 * hp) * 64 + 128, :],
                    waits=([("gp", m_init)] + war_wq["v"]) if hp == 0 else ()))
            m_w = wall(m_w)
            m_pw = [("pe", P.cnt["pe"])]
            m_sq = P.do("scalar", lambda e: e.activation(
                lnxT[:, 0:c.DC * c.HG * 64], wq, AF.Square),
                waits=m_w + m_pw + [("ac", P.cnt["ac"])], inc="ac")
            NQ = c.HG * 64
            for dc in range(c.DC):
                w = ([("ac", m_sq)] + BW[0]) if dc == 0 else ()
                P.do("tensor", lambda t, dc=dc: t.matmul(
                    bank(0, 0, NQ)[0:1, :], ones_bf,
                    lnxT[:, dc * NQ:(dc + 1) * NQ],
                    start=(dc == 0), stop=(dc == c.DC - 1)),
                    waits=w, inc="pe" if dc == c.DC - 1 else None)
            m_cn = P.cnt["pe"]
            st_blk["lnxT_users"] = [m_cn]
            pvstate["pT_free"] = 0
            mr2 = P.do("vector", lambda v: v.tensor_reduce(
                sc1[:, 2:3], bank(0, 0, NQ)[0:1, :], axis=AX.X,
                op=AluOpType.max), waits=[("pe", m_cn)], inc="dv")
            BW[0] = [("dv", mr2)]
            ml1 = P.do("scalar", lambda e: e.activation(
                sc1[:, 3:4], sc1[:, 2:3], AF.Ln), waits=[("dv", mr2)], inc="ac")
            ml2 = P.do("scalar", lambda e: e.activation(
                sc1[:, 3:4], sc1[:, 3:4], AF.Exp, scale=0.5),
                waits=[("ac", ml1)], inc="ac")
            mr3 = P.do("vector", lambda v, g=g: v.scalar_tensor_tensor(
                sc1[:, 1:2], sc1[:, 3:4],
                -float(c.DH) * SC * math.sqrt(c.D), kmaxs[:, g:g + 1],
                AluOpType.mult, AluOpType.mult),
                waits=[("ac", ml2), ("dv", m_kv_done[g])], inc="dv")
            mr4 = P.do("vector", lambda v: v.tensor_copy(
                recl16[0:1, 512:513], sc1[:, 1:2]),
                waits=[("dv", mr3)] + war_recl16["v"], inc="dv")
            w1b = BW[1]
            P.do("tensor", lambda t: t.matmul(
                bank(1, 0, 1), ones16, recl16[0:1, 512:513],
                start=True, stop=True),
                waits=[("dv", mr4)] + w1b, inc="pe")
            mbc = P.cnt["pe"]
            m_negB = P.do("vector", lambda v: v.tensor_copy(
                negBv, bank(1, 0, 1)),
                waits=[("pe", mbc), ("ac", P.cnt["ac"])], inc="dv")
            BW[1] = BW[1] + [("dv", m_negB)]

            def emit_prologueB(qb, norms, g=None, pbanks=(2, 3)):
                m_lnxT = ln1_part2(norms, b0=pbanks[0])
                mq = 0
                qsl = (qb % 2) * c.CH * c.TBLK
                for mc in range(c.CH):
                    pb = pbanks[mc % 2]
                    for dc in range(c.DC):
                        w = (BW[pb] + m_w + [("dv", m_lnxT)]) if dc == 0 else ()
                        P.do("tensor", lambda t, dc=dc, mc=mc, pb=pb: t.matmul(
                            bank(pb),
                            wq[:, dc * c.HG * 64 + mc * 128: dc * c.HG * 64 + (mc + 1) * 128],
                            lnxT[:, dc * c.TBLK:(dc + 1) * c.TBLK],
                            start=(dc == 0), stop=(dc == c.DC - 1)),
                            waits=w, inc="pe" if dc == c.DC - 1 else None)
                    m_mm = P.cnt["pe"]
                    st_blk["lnxT_users"] = [m_mm]
                    if mc % 2 == 1 or mc == c.CH - 1:
                        nmc = 2 if mc % 2 == 1 else 1
                        mc0 = mc - nmc + 1
                        me = P.do("vector", lambda v, mc0=mc0, nmc=nmc, qsl=qsl:
                                  v.tensor_copy(
                                      qT[:, qsl + mc0 * c.TBLK:
                                         qsl + (mc0 + nmc) * c.TBLK],
                                      bank(pbanks[0] if mc0 % 2 == 0 else pbanks[1],
                                           0, nmc * 512)),
                                  waits=[("pe", m_mm)] + war_qT["v"], inc="dv")
                        mq = me
                        BW[pbanks[0]] = [("dv", me)]
                        BW[pbanks[1]] = [("dv", me)]
                return mq

            def prologueB_chunks(qb, norms, g=None, pbanks=(4, 5)):
                state = {"m_lnxT": 0}
                cl = []
                nsub = c.TBLK // 128
                qsl = (qb % 2) * c.CH * c.TBLK
                for s in range(nsub):
                    def tch(s=s):
                        l16 = lnx16[:, s * c.D:(s + 1) * c.D]
                        def dst(j0, cn, s=s):
                            return apx(lnxT, j0 * c.TBLK + s * 128,
                                       [[c.TBLK, cn], [1, 128]])
                        (semn, mev), mtp = transp(
                            l16, c.DC, pbanks[0], dst,
                            wait_first=[("dv", norms[s])]
                            + [("pe", u) for u in st_blk["lnxT_users"][-1:]],
                            evac_eng=st_blk["evac_eng"], ident=id_bf)
                        st_blk["l16w"][s] = [("pe", mtp)]
                        st_blk["m_lnxT_sem"] = semn
                        state["m_lnxT"] = mev
                    cl.append(tch)
                for mc in range(c.CH):
                    def qch(mc=mc):
                        pb = pbanks[mc % 2]
                        for dc in range(c.DC):
                            w = (BW[pb] + m_w
                                 + [(st_blk["m_lnxT_sem"],
                                     state["m_lnxT"])]) if dc == 0 else ()
                            P.do("tensor", lambda t, dc=dc, mc=mc, pb=pb:
                                 t.matmul(
                                     bank(pb),
                                     wq[:, dc * c.HG * 64 + mc * 128:
                                        dc * c.HG * 64 + (mc + 1) * 128],
                                     lnxT[:, dc * c.TBLK:(dc + 1) * c.TBLK],
                                     start=(dc == 0), stop=(dc == c.DC - 1)),
                                 waits=w, inc="pe" if dc == c.DC - 1 else None)
                        m_mm = P.cnt["pe"]
                        st_blk["lnxT_users"] = [m_mm]
                        if mc % 2 == 1 or mc == c.CH - 1:
                            nmc = 2 if mc % 2 == 1 else 1
                            mc0 = mc - nmc + 1
                            me = P.do("vector",
                                      lambda v, mc0=mc0, nmc=nmc, qsl=qsl:
                                      v.tensor_copy(
                                          qT[:, qsl + mc0 * c.TBLK:
                                             qsl + (mc0 + nmc) * c.TBLK],
                                          bank(pbanks[0] if mc0 % 2 == 0
                                               else pbanks[1], 0, nmc * 512)),
                                      waits=[("pe", m_mm)] + war_qT["v"],
                                      inc="dv")
                            prolstate["m_qT"] = me
                            BW[pbanks[0]] = [("dv", me)]
                            BW[pbanks[1]] = [("dv", me)]
                    cl.append(qch)
                return cl

            def emit_outproj(qb, g, ob0, spread=True):
                m_oT = P.cnt["dv"]
                nsub = c.TBLK // 128
                OW = min(512, c.D)
                ops = []
                for tc in range(nsub):
                    for oh in range(c.D // OW):
                        pb = ob0 + (tc * (c.D // OW) + oh) % 2
                        grp = []
                        for hl in range(c.HG // 2):
                            first = hl == 0
                            last = hl == c.HG // 2 - 1

                            def mm(tc=tc, oh=oh, hl=hl, pb=pb, first=first,
                                   last=last, qb=qb, g=g):
                                w = ([("dv", m_oT)] + m_w + BW[pb]) if first else ()
                                P.do("tensor", lambda t: t.matmul(
                                    bank(pb, 0, OW),
                                    oT[:, hl * c.TBLK + tc * 128:
                                       hl * c.TBLK + (tc + 1) * 128],
                                    wo[:, hl * c.D + oh * OW:
                                       hl * c.D + (oh + 1) * OW],
                                    start=first, stop=last),
                                    waits=w, inc="pe" if last else None)
                                if last:
                                    m_mm = P.cnt["pe"]
                                    war_oT["v"] = [("pe", m_mm)]
                                    dsty = y_acc[:, (qb * nsub + tc) * c.D + oh * OW:
                                                 (qb * nsub + tc) * c.D + (oh + 1) * OW]
                                    if g == 0:
                                        mev = P.do("vector",
                                                   lambda v, d=dsty, pb=pb: v.tensor_copy(
                                                       d, bank(pb, 0, OW)),
                                                   waits=[("pe", m_mm)], inc="dv")
                                    else:
                                        mev = P.do("vector",
                                                   lambda v, d=dsty, pb=pb: v.tensor_tensor(
                                                       d, d, bank(pb, 0, OW),
                                                       AluOpType.add),
                                                   waits=[("pe", m_mm),
                                                          ("ac", m_yg0[0])], inc="dv")
                                    BW[pb] = [("dv", mev)]
                            grp.append(mm)
                        ops.append(lambda grp=grp: [f() for f in grp])
                if spread:
                    bgq_meta["ob0"] = ob0
                    bgq.extend(ops)
                else:
                    for op in ops:
                        op()

            st_blk["evac_eng"] = "vector"
            for _h in range(2):            st_blk["evac_eng"] = "vector"
            for _h in range(2):            st_blk["evac_eng"] = "vector"
            for _h in range(2):            st_blk["evac_eng"] = "vector"
            for _h in range(2):
                war_otmp[_h] = war_otmp[_h] + [("ac", P.cnt["ac"])]
                war_recl[_h] = war_recl[_h] + [("ac", P.cnt["ac"])]
            normsB = ln1_part1(0)
            prol = emit_prologueB(0, normsB, g=g)
            for qb in range(c.NTB):
                m_qT = prol
                if qb + 1 < c.NTB:
                    normsB = ln1_part1((qb + 1) * c.TBLK)
                    if c.CH >= 3:
                        prolstate["chunks"] = prologueB_chunks(
                            qb + 1, normsB, g=g, pbanks=(4, 5))
                def emit_epilogue(p, obase, m_pv, halves=(0, 1)):
                    mults = []
                    for half in halves:
                        ob = obase + half
                        mo = P.do("vector", lambda v, ob=ob, half=half:
                                  v.tensor_copy(
                                      otmp[:, half * 512:(half + 1) * 512],
                                      bank(ob)[0:64, :]),
                                  waits=[("pe", m_pv)]
                                  + war_otmp[half], inc="dv")
                        mrc = P.do("vector", lambda v, ob=ob, half=half:
                                   v.reciprocal(
                                       recl[0:1, half * 512:(half + 1) * 512],
                                       bank(ob)[64:65, :]),
                                   waits=[("pe", m_pv),
                                          ("dv", mo)] + war_recl[half], inc="dv")
                        mrc2 = P.do("vector", lambda v, half=half: v.tensor_copy(
                            recl16[0:1, 0:512],
                            recl[0:1, half * 512:(half + 1) * 512]),
                            waits=[("dv", mrc)] + war_recl16["v"], inc="dv")
                        P.do("tensor", lambda t, ob=ob: t.matmul(
                            bank(ob), ones16,
                            recl16[0:1, 0:512],
                            start=True, stop=True),
                            waits=[("dv", mrc2)], inc="pe")
                        mb = P.cnt["pe"]
                        war_recl16["v"] = [("pe", mb)]
                        mm_ = P.do("vector", lambda v, ob=ob, half=half, p=p:
                                   v.tensor_tensor(
                                       oT[64 * half:64 * half + 64,
                                          p * c.TBLK:(p + 1) * c.TBLK],
                                       otmp[:, half * 512:(half + 1) * 512],
                                       bank(ob)[0:64, :], AluOpType.mult),
                                   waits=[("pe", mb)] + war_oT["v"], inc="dv")
                        BW[ob] = [("dv", mm_)]
                        war_otmp[half] = [("dv", mm_)]
                        war_recl[half] = [("pe", mb)]
                        mults.append(mm_)
                    return mults

                pending_epi = []
                EH, PH = ringstate["exp"], ringstate["pv"]
                qsl = (qb % 2) * c.CH * c.TBLK
                for p in range(c.CH):
                    hlA, hlB = 2 * p, 2 * p + 1
                    obase = 6 if (ringstate["pgc"] % 2 == 0) else 4
                    ringstate["pgc"] += 1
                    base = ringstate["i"]
                    if obase == bgq_meta["ob0"]:
                        while bgq:
                            bgq.pop(0)()

                    def emit_qk(t_, p=p, base=base, qsl=qsl):
                        gi = base + t_
                        sp = gi % 2
                        for half in range(2):
                            rb = 2 * sp + half
                            r0 = 64 * half
                            w = []
                            if half == 0:
                                w = [("ac", EH.get(gi - 2, 0))]
                                if t_ == 0:
                                    w += [("dv", m_qT)] + BW[0] + BW[1]
                                elif t_ == 1:
                                    w += BW[2] + BW[3]
                            P.do("tensor",
                                 lambda t, t_=t_, p=p, r0=r0, rb=rb, qsl=qsl: t.matmul(
                                     bank(rb),
                                     kT[r0:r0 + 64,
                                        p * c.T + t_ * 128: p * c.T + (t_ + 1) * 128],
                                     qT[r0:r0 + 64,
                                        qsl + p * c.TBLK: qsl + (p + 1) * c.TBLK],
                                     start=True, stop=True),
                                 waits=w, inc="pe" if half == 1 else None)
                        EH[("qk", gi)] = P.cnt["pe"]

                    def emit_exp(t_, base=base):
                        gi = base + t_
                        sp = gi % 2
                        sl4 = gi % 4
                        EH[gi] = P.do(
                            "scalar", lambda e, sp=sp, sl4=sl4: e.activation(
                                pT[:, sl4 * 1024: (sl4 + 1) * 1024],
                                bank(2 * sp, 0, 1024),
                                AF.Exp, scale=SC, bias=negBv[:, 0:1]),
                            waits=[("pe", EH[("qk", gi)]),
                                   ("pe", PH.get(gi - 4, 0)),
                                   ("pe", pvstate.get("pT_free", 0)),
                                   ("dv", m_negB)], inc="ac")

                    def emit_pv(t_, p=p, obase=obase, base=base):
                        gi = base + t_
                        sl4 = gi % 4
                        hls = (2 * p, 2 * p + 1)
                        for half in range(2):
                            ob = obase + half
                            hl = hls[half]
                            w = []
                            if half == 0:
                                w = [("ac", EH[gi])]
                                if t_ == 0:
                                    w += BW[obase] + BW[obase + 1]
                            P.do("tensor",
                                 lambda t, t_=t_, hl=hl, sl4=sl4, ob=ob, half=half:
                                 t.matmul(
                                     bank(ob, 0, 512)[0:65, :],
                                     vP[:, t_ * c.VW + hl * 65:
                                        t_ * c.VW + hl * 65 + 65],
                                     pT[:, sl4 * 1024 + half * 512:
                                        sl4 * 1024 + (half + 1) * 512],
                                     start=(t_ == 0), stop=(t_ == c.KT - 1)),
                                 waits=w, inc="pe" if half == 1 else None)
                        PH[gi] = P.cnt["pe"]
                        if (bgq and obase != bgq_meta["ob0"]
                                and t_ % 4 == 0):
                            bgq.pop(0)()

                    for u in range(c.KT // 2):
                        emit_qk(2 * u)
                        emit_exp(2 * u)
                        emit_qk(2 * u + 1)
                        if u > 0:
                            emit_pv(2 * u - 2)
                        emit_exp(2 * u + 1)
                        if u > 0:
                            emit_pv(2 * u - 1)
                        if u >= min(2, c.KT // 2 - 1) and pending_epi:
                            pending_epi.pop(0)()
                        if (u >= min(4, c.KT // 2 - 1) and p == 0
                                and tailstate["q"]):
                            tailstate["q"].pop(0)()
                        if p == 2 and u >= 3 and prolstate["chunks"]:
                            prolstate["chunks"].pop(0)()
                    emit_pv(c.KT - 2)
                    emit_pv(c.KT - 1)
                    if p == 0:
                        while tailstate["q"]:
                            tailstate["q"].pop(0)()
                    if p == 2:
                        while prolstate["chunks"]:
                            prolstate["chunks"].pop(0)()
                    ringstate["i"] += c.KT
                    while pending_epi:
                        pending_epi.pop(0)()
                    pending_epi = [
                        (lambda p=p, obase=obase, m=P.cnt["pe"], h=h:
                         emit_epilogue(p, obase, m, halves=(h,)))
                        for h in range(2)]
                while bgq:
                    bgq.pop(0)()
                war_qT["v"] = [("pe", P.cnt["pe"])]
                mac_all = P.cnt["ac"]
                for b_ in range(4):
                    BW[b_] = [("ac", mac_all)]
                if qb + 1 < c.NTB:
                    if c.CH >= 3:
                        assert not prolstate["chunks"]
                        prol = prolstate["m_qT"]
                    else:
                        prol = emit_prologueB(qb + 1, normsB, g=g)
                    ob0_now = 6 if ((ringstate["pgc"] - 1) % 2 == 0) else 4
                    tailstate["q"] = list(pending_epi) + [
                        (lambda qb=qb, g=g, ob0=ob0_now:
                         emit_outproj(qb, g, ob0))]
                    pending_epi = []
                else:
                    assert not tailstate["q"]
                    while pending_epi:
                        pending_epi.pop(0)()
                    emit_outproj(qb, g,
                                 6 if ((ringstate["pgc"] - 1) % 2 == 0) else 4,
                                 spread=False)
            war_wkv["v"] = [("pe", P.cnt["pe"])]
            war_kT["v"] = [("pe", P.cnt["pe"])]
            war_vP["v"] = [("pe", P.cnt["pe"])]
            war_qT["v"] = [("pe", P.cnt["pe"])]

        def phase3_prologue(M_END, tz):
            s1T, s2T, wch, wch2, sgn = (tz["s1T"], tz["s2T"], tz["wch"],
                                        tz["wch2"], tz["sgn"])
            war_wch = {0: list(BW[0]) + list(BW[1]) + M_END, 1: list(M_END)}
            war_sgn = {0: list(M_END), 1: list(M_END)}
            sgn_chain = {"v": list(M_END)}
            war_red = []
            ms = {}
            for fc in range(c.FC):
                sl = fc % 2
                WC = wch[:, sl * c.D:(sl + 1) * c.D]
                SG = sgn[:, 0:c.D]
                mw = dma("sync", WC, W1[fc * 128:(fc + 1) * 128, :],
                         waits=war_wch[sl])
                mrd = P.do("vector", lambda v, WC=WC: v.tensor_reduce(
                    stats[:, 18:19], WC, axis=AX.X, op=AluOpType.add,
                    apply_absolute_value=True), waits=[mw] + war_red, inc="dv")
                mal = P.do("vector", lambda v, fc=fc: v.tensor_scalar(
                    alpha1[:, fc:fc + 1], stats[:, 18:19], 1.0 / c.D, None,
                    AluOpType.mult, AluOpType.bypass), waits=[("dv", mrd)], inc="dv")
                war_red = [("dv", mal)]
                msg = P.do("scalar", lambda e, WC=WC, SG=SG: e.activation(
                    SG, WC, AF.Sign),
                    waits=[mw] + sgn_chain["v"], inc="ac")
                war_wch[sl] = [("dv", mrd), ("ac", msg)]

                def dst(j0, cn, fc=fc):
                    return apx(s1T, j0 * c.F + fc * 128, [[c.F, cn], [1, 128]])
                (semn, mev), mtp = transp(SG, c.DC, 0 if sl == 0 else 2, dst,
                                          wait_first=[("ac", msg)], ident=id_bf)
                sgn_chain["v"] = [("pe", mtp)]
            ms["s1T"] = P.cnt["dv"]
            war_wch = {0: war_wch[0] + war_wch[1], 1: war_wch[0] + war_wch[1]}
            for dc in range(c.DC):
                sl = dc % 2
                WC = wch2[:, sl * c.F:(sl + 1) * c.F]
                SG = sgn[:, 0:c.F]
                mw = dma("sync", WC, W2[dc * 128:(dc + 1) * 128, :],
                         waits=war_wch[sl])
                mrd = P.do("vector", lambda v, WC=WC: v.tensor_reduce(
                    stats[:, 18:19], WC, axis=AX.X, op=AluOpType.add,
                    apply_absolute_value=True), waits=[mw] + war_red, inc="dv")
                mal = P.do("vector", lambda v, dc=dc: v.tensor_scalar(
                    alpha2[:, dc:dc + 1], stats[:, 18:19], 1.0 / c.F, None,
                    AluOpType.mult, AluOpType.bypass), waits=[("dv", mrd)], inc="dv")
                war_red = [("dv", mal)]
                msg = P.do("scalar", lambda e, WC=WC, SG=SG: e.activation(
                    SG, WC, AF.Sign),
                    waits=[mw] + sgn_chain["v"], inc="ac")
                war_wch[sl] = [("dv", mrd), ("ac", msg)]

                def dst(j0, cn, dc=dc):
                    return apx(s2T, j0 * c.D + dc * 128, [[c.D, cn], [1, 128]])
                (semn, mev), mtp = transp(SG, c.FC, 0 if sl == 0 else 2, dst,
                                          wait_first=[("ac", msg)], ident=id_bf)
                sgn_chain["v"] = [("pe", mtp)]
            ms["s2T"] = P.cnt["dv"]
            ms["alpha"] = P.cnt["dv"]
            return ms

        def phase3(M_END, ms, tz):
            s1T, s2T, hT = tz["s1T"], tz["s2T"], tz["hT"]
            x3, x1, ln3T = tz["x3"], tz["x1"], tz["ln3T"]
            ln3f, ln3b = tz["ln3f"], tz["ln3b"]
            o2tm, oscr = tz["o2tm"], tz["oscr"]
            out_dmas = phase3.out_dmas = []
            m_y_done = tz["m_y_done"]
            MP = tz["m_prol"]
            war_x3 = {0: list(M_END) + MP, 1: list(M_END) + MP}
            war_x1 = {0: MP, 1: MP}
            war_ln3f = {k: MP for k in range(2 * (c.TB3 // 128))}
            war_ln3b = {k: MP for k in range(2 * (c.TB3 // 128))}
            war_ln3T = {0: MP, 1: MP}
            war_hT = {"v": []}
            war_o2 = []
            nsub = c.TB3 // 128
            DCT = c.DC * c.TB3

            def front(tb):
                t0 = tb * c.TB3
                sl = tb % 2
                X3 = lambda s: x3[:, (sl * nsub + s) * c.D:(sl * nsub + s + 1) * c.D]
                X1 = lambda s: x1[:, (sl * nsub + s) * c.D:(sl * nsub + s + 1) * c.D]
                L3s = lambda s: ln3f[:, (sl * nsub + s) * c.D:(sl * nsub + s + 1) * c.D]
                mx = []
                for s in range(nsub):
                    mx.append(dma("sync", X3(s),
                                  X[t0 + s * 128: t0 + 128 * (s + 1), :],
                                  waits=war_x3[sl] if s == 0 else ()))
                pows = []
                for s in range(nsub):
                    ysl = y_acc[:, (t0 // 128 + s) * c.D:(t0 // 128 + s + 1) * c.D]
                    pows.append(ln_stats(ysl, sl * 2 + s,
                                         m_y_done if tb == 0 else []))
                m_x1 = []
                for s in range(nsub):
                    ysl = y_acc[:, (t0 // 128 + s) * c.D:(t0 // 128 + s + 1) * c.D]
                    mn = ln_norm(ysl, L3s(s), sl * 2 + s, pows[s],
                                 extra_waits=war_ln3f[sl * nsub + s])
                    ma = P.do("vector", lambda v, s=s, X3=X3, X1=X1, L3s=L3s:
                              v.tensor_tensor(X1(s), X3(s), L3s(s), AluOpType.add),
                              waits=[("dv", mn), mx[s]]
                              + (war_x1[sl] if s == 0 else []), inc="dv")
                    war_ln3f[sl * nsub + s] = [("dv", ma)]
                    m_x1.append(ma)
                pows = []
                for s in range(nsub):
                    pows.append(ln_stats(X1(s), sl * 2 + s, [("dv", m_x1[s])]))
                norms = []
                for s in range(nsub):
                    L3 = ln3b[:, (sl * nsub + s) * c.D:(sl * nsub + s + 1) * c.D]
                    mn = ln_norm(X1(s), L3, sl * 2 + s, pows[s],
                                 extra_waits=war_ln3b[sl * nsub + s])
                    norms.append(mn)
                return norms

            def emit_ln3T(tb, norms):
                sl = tb % 2
                m_ln3 = None
                for s_ in range(nsub):
                    L3 = ln3b[:, (sl * nsub + s_) * c.D:(sl * nsub + s_ + 1) * c.D]

                    def dst(j0, cn, s_=s_, sl=sl):
                        return apx(ln3T, sl * DCT + j0 * c.TB3 + s_ * 128,
                                   [[c.TB3, cn], [1, 128]])
                    (semn, mev), mtp = transp(L3, c.DC, 0, dst,
                                              wait_first=[("dv", norms[s_])]
                                              + war_ln3T[sl], ident=id_bf)
                    war_ln3b[sl * nsub + s_] = [("pe", mtp)]
                    m_ln3 = mev
                return m_ln3

            f_norms = front(0)
            pend_t = {"norms": f_norms, "tb": 0}
            for tb in range(c.NT3):
                t0 = tb * c.TB3
                sl = tb % 2
                X3 = lambda s, sl=sl: x3[:, (sl * nsub + s) * c.D:
                                         (sl * nsub + s + 1) * c.D]
                X1 = lambda s, sl=sl: x1[:, (sl * nsub + s) * c.D:
                                         (sl * nsub + s + 1) * c.D]
                L3s = lambda s, sl=sl: ln3f[:, (sl * nsub + s) * c.D:
                                            (sl * nsub + s + 1) * c.D]
                if pend_t.get("m_ln3") is None:
                    m_ln3 = emit_ln3T(tb, pend_t["norms"])
                else:
                    m_ln3 = pend_t["m_ln3"]
                for fc in range(c.FC):
                    pb = 2 + (fc % 2)
                    for dc in range(c.DC):
                        w = (BW[pb] + [("dv", m_ln3),
                                       ("dv", ms["s1T"])]) if dc == 0 else ()
                        P.do("tensor", lambda t, fc=fc, dc=dc, pb=pb, sl=sl:
                             t.matmul(
                                 bank(pb, 0, c.TB3),
                                 s1T[:, dc * c.F + fc * 128: dc * c.F + (fc + 1) * 128],
                                 ln3T[:, sl * DCT + dc * c.TB3:
                                      sl * DCT + (dc + 1) * c.TB3],
                                 start=(dc == 0), stop=(dc == c.DC - 1)),
                             waits=w, inc="pe" if dc == c.DC - 1 else None)
                    m_mm = P.cnt["pe"]
                    war_ln3T[sl] = [("pe", m_mm)]
                    mg = P.do("scalar", lambda e, fc=fc, pb=pb: e.activation(
                        hT[:, fc * c.TB3:(fc + 1) * c.TB3], bank(pb, 0, c.TB3),
                        AF.Gelu, scale=alpha1[:, fc:fc + 1], bias=b1s[:, fc:fc + 1]),
                        waits=[("pe", m_mm), ("dv", ms["alpha"])] + m_bload
                        + war_hT["v"], inc="ac")
                    BW[pb] = [("ac", mg)]
                m_h = P.cnt["ac"]
                if tb + 1 < c.NT3:
                    nxt = front(tb + 1)
                    pend_t = {"norms": nxt, "tb": tb + 1, "m_ln3": None}
                for dc in range(c.DC):
                    pb = 4 + (dc % 2)
                    for fc in range(c.FC):
                        w = (BW[pb] + [("ac", m_h), ("dv", ms["s2T"])]) if fc == 0 else ()
                        P.do("tensor", lambda t, fc=fc, dc=dc, pb=pb: t.matmul(
                            bank(pb, 0, c.TB3),
                            s2T[:, fc * c.D + dc * 128: fc * c.D + (dc + 1) * 128],
                            hT[:, fc * c.TB3:(fc + 1) * c.TB3],
                            start=(fc == 0), stop=(fc == c.FC - 1)),
                            waits=w, inc="pe" if fc == c.FC - 1 else None)
                    m_mm = P.cnt["pe"]
                    war_hT["v"] = [("pe", m_mm)]
                    mo = P.do("scalar", lambda e, dc=dc, pb=pb: e.activation(
                        oscr, bank(pb, 0, c.TB3), AF.Identity,
                        scale=alpha2[:, dc:dc + 1], bias=b2s[:, dc:dc + 1]),
                        waits=[("pe", m_mm), ("dv", ms["alpha"])] + m_bload
                        + war_o2, inc="ac")
                    BW[pb] = [("ac", mo)]

                    def dst(j0, cn, dc=dc):
                        return apx(o2tm, j0 * c.D + dc * 128, [[c.D, cn], [1, 128]])
                    (semn, mev), mtp = transp(oscr, nsub, 6, dst,
                                              wait_first=[("ac", mo)], ident=id16)
                    war_o2 = [("pe", mtp)]
                    m_o2 = mev
                    if (dc == min(3, c.DC - 1) and tb + 1 < c.NT3
                            and pend_t["m_ln3"] is None):
                        pend_t["m_ln3"] = emit_ln3T(tb + 1, pend_t["norms"])
                pows = []
                for s in range(nsub):
                    pows.append(ln_stats(o2tm[:, s * c.D:(s + 1) * c.D],
                                         sl * 2 + s, [("dv", m_o2)]))
                for s in range(nsub):
                    mn = ln_norm(o2tm[:, s * c.D:(s + 1) * c.D], L3s(s),
                                 sl * 2 + s, pows[s],
                                 extra_waits=war_ln3f[sl * nsub + s])
                    ma = P.do("vector", lambda v, s=s, X3=X3, X1=X1, L3s=L3s:
                              v.tensor_tensor(X3(s), X1(s), L3s(s), AluOpType.add),
                              waits=[("dv", mn)], inc="dv")
                    war_ln3f[sl * nsub + s] = [("dv", ma)]
                    mo_ = dma("sync", OUT[t0 + s * 128: t0 + (s + 1) * 128, :],
                              X3(s), waits=[("dv", ma)])
                    out_dmas.append(mo_)
                war_x3[sl] = wall(out_dmas[-nsub:])
                war_x1[sl] = [("dv", P.cnt["dv"])]

        m_yg0 = [0]
        phase_A(0)
        phase_B(0)
        m_yg0[0] = P.cnt["ac"]
        phase_A(1)
        phase_B(1)
        M_END = [("pe", P.cnt["pe"]), ("dv", P.cnt["dv"]), ("ac", P.cnt["ac"]),
                 ("gp", P.cnt["gp"])]
        m_y_done = [("dv", P.cnt["dv"]), ("ac", P.cnt["ac"])]
        st2.close()
        s1T = sb("s1T", [128, c.DC * c.F], FP8)
        s2T = sb("s2T", [128, c.FC * c.D], FP8)
        hT = sb("hT", [128, c.FC * c.TB3], BF16)
        o2tm = sb("o2tm", [128, (c.TB3 // 128) * c.D], FP16)
        oscr = sb("oscr", [128, c.TB3], FP16)
        st3 = ExitStack()
        wch = st3.enter_context(nc.sbuf_tensor("wch", [128, 2 * c.D], F32))[:, :]
        wch2 = st3.enter_context(nc.sbuf_tensor("wch2", [128, 2 * c.F], F32))[:, :]
        sgn = st3.enter_context(
            nc.sbuf_tensor("sgn", [128, max(c.D, c.F)], BF16))[:, :]
        p3tensors = dict(s1T=s1T, s2T=s2T, hT=hT, o2tm=o2tm, oscr=oscr,
                         wch=wch, wch2=wch2, sgn=sgn)
        ms = phase3_prologue(M_END, p3tensors)
        m_prol = [("pe", P.cnt["pe"]), ("dv", P.cnt["dv"]), ("ac", P.cnt["ac"])]
        st3.close()
        nsub3 = c.TB3 // 128
        p3tensors["x3"] = sb("x3", [128, 2 * nsub3 * c.D], F32)
        p3tensors["x1"] = p3tensors["x3"]
        p3tensors["ln3f"] = sb("ln3f", [128, 2 * (c.TB3 // 128) * c.D], F32)
        p3tensors["ln3b"] = sb("ln3b", [128, 2 * (c.TB3 // 128) * c.D], BF16)
        p3tensors["lnscr"] = None
        p3tensors["ln3T"] = sb("ln3T", [128, 2 * c.DC * c.TB3], BF16)
        p3tensors["m_y_done"] = m_y_done
        p3tensors["m_prol"] = m_prol
        phase3(M_END, ms, p3tensors)

        P.do("sync", lambda e: e.nop(), waits=wall(phase3.out_dmas))

        P.emit(block, semh, first_waits=[("gp", m_init)])
    return nc


def kernel(L0_attn_x, L0_attn_w_qkv, L0_attn_w_out, L0_ffn_w1, L0_ffn_b1,
           L0_ffn_w2, L0_ffn_b2):
    x = np.ascontiguousarray(np.asarray(L0_attn_x, dtype=np.float32))
    wqkv = np.ascontiguousarray(np.asarray(L0_attn_w_qkv, dtype=np.float32))
    wout = np.ascontiguousarray(np.asarray(L0_attn_w_out, dtype=np.float32))
    w1 = np.ascontiguousarray(np.asarray(L0_ffn_w1, dtype=np.float32))
    b1 = np.ascontiguousarray(np.asarray(L0_ffn_b1, dtype=np.float32))
    w2 = np.ascontiguousarray(np.asarray(L0_ffn_w2, dtype=np.float32))
    b2 = np.ascontiguousarray(np.asarray(L0_ffn_b2, dtype=np.float32))

    cfg = Cfg(T=4096, D=1024, H=16, F=4096)
    nc = build_nc(cfg)
    in_maps = []
    for b in range(8):
        in_maps.append({"x": x[b], "w_qkv": wqkv, "w_out": wout,
                        "w1": w1, "b1": b1, "w2": w2, "b2": b2})
    res = run_bass_kernel_spmd(nc, in_maps, core_ids=list(range(8)))
    out = np.stack([res.results[b]["out"] for b in range(8)], axis=0)
    return out.astype(np.float32)


# version 34
# speedup vs baseline: 1.2591x; 1.0009x over previous
import math
from contextlib import ExitStack

import numpy as np
import concourse.bass as bass
import concourse.mybir as mybir
from concourse.alu_op_type import AluOpType
from concourse.bass_utils import run_bass_kernel_spmd

F32 = mybir.dt.float32
BF16 = mybir.dt.bfloat16
FP16 = mybir.dt.float16
FP8 = mybir.dt.float8e4
AF = mybir.ActivationFunctionType
AX = mybir.AxisListType

ENGINES = ("tensor", "vector", "scalar", "gpsimd", "sync")
SEMS = (("pe", "dv", "ac", "gp", "out")
        + tuple(f"qs{i}" for i in range(8)) + tuple(f"qg{i}" for i in range(8)))


def wall(tuples):
    d = {}
    for s_, v in tuples:
        if v:
            d[s_] = max(d.get(s_, 0), v)
    return list(d.items())


class Cfg:
    def __init__(self, T=4096, D=1024, H=16, F=4096):
        self.T, self.D, self.H, self.F = T, D, H, F
        self.DH = 64
        assert H * self.DH == D
        self.DC = D // 128
        self.FC = F // 128
        self.KT = T // 128
        self.G = 2
        self.HG = H // 2
        self.CH = self.HG * 64 // 128
        self.TBLK = 512
        self.NTB = T // self.TBLK
        self.TB3 = 256
        self.NT3 = T // self.TB3
        self.VW = self.HG * 65
        self.scale = 1.0 / math.sqrt(D)
        self.kgroups = []
        k = 0
        while k < self.KT:
            n = min(3, self.KT - k)
            self.kgroups.append((k, n))
            k += n


def apx(view, offset, dims):
    p = view.ap[0]
    return bass.AP(view.tensor, offset, [[p[0], p[1]]] + [list(d) for d in dims])


class Prog:
    def __init__(self):
        self.ops = {e: [] for e in ENGINES}
        self.cnt = {s: 0 for s in SEMS}

    def do(self, eng, fn, waits=(), inc=None, n=1):
        self.ops[eng].append((fn, tuple(waits), inc, n))
        if inc is not None:
            self.cnt[inc] += n
            return self.cnt[inc]
        return None

    def emit(self, block, sems, first_waits=()):
        for ename in ENGINES:
            def body(eng, ename=ename):
                waited = {}
                if ename != "gpsimd":
                    for s, v in first_waits:
                        if v:
                            eng.wait_ge(sems[s], v)
                            waited[s] = v
                for fn, waits, inc, n in self.ops[ename]:
                    for s, v in waits:
                        if v and waited.get(s, 0) < v:
                            eng.wait_ge(sems[s], v)
                            waited[s] = v
                    instr = fn(eng)
                    if inc is not None:
                        instr.then_inc(sems[inc], n)
            getattr(block, ename)(body)


def build_nc(cfg: Cfg):
    c = cfg
    nc = bass.Bass("TRN2", target_bir_lowering=False)

    X = nc.dram_tensor("x", [c.T, c.D], F32, kind="ExternalInput").ap()
    Wqkv = nc.dram_tensor("w_qkv", [c.D, 3 * c.D], F32, kind="ExternalInput").ap()
    Wout = nc.dram_tensor("w_out", [c.D, c.D], F32, kind="ExternalInput").ap()
    W1 = nc.dram_tensor("w1", [c.F, c.D], F32, kind="ExternalInput").ap()
    B1 = nc.dram_tensor("b1", [c.F], F32, kind="ExternalInput").ap()
    W2 = nc.dram_tensor("w2", [c.D, c.F], F32, kind="ExternalInput").ap()
    B2 = nc.dram_tensor("b2", [c.D], F32, kind="ExternalInput").ap()
    OUT = nc.dram_tensor("out", [c.T, c.D], F32, kind="ExternalOutput").ap()

    P = Prog()
    with ExitStack() as st:
        def sb(name, shape, dtype):
            h = st.enter_context(nc.sbuf_tensor(name, shape, dtype))
            return h[tuple(slice(None) for _ in shape)]

        id32 = sb("id32", [128, 128], F32)
        id_bf = sb("id_bf", [128, 128], BF16)
        id16 = sb("id16", [128, 128], FP16)
        ones16 = sb("ones16", [1, 128], FP16)
        ones_bf = sb("ones_bf", [128, 1], BF16)
        eps_t = sb("eps_t", [128, 1], F32)
        mhalf = sb("mhalf", [128, 1], F32)
        alpha1 = sb("alpha1", [128, c.FC], F32)
        alpha2 = sb("alpha2", [128, c.DC], F32)
        b1s = sb("b1s", [128, c.FC], F32)
        b2s = sb("b2s", [128, c.DC], F32)
        y_acc = sb("y_acc", [128, (c.T // 128) * c.D], FP16)
        kmaxs = sb("kmaxs", [1, 2], F32)

        stats = sb("stats", [128, 80], F32)
        qmax4 = sb("qmax4", [128, 8], F32)
        sc1 = sb("sc1", [1, 8], F32)
        negBv = sb("negBv", [128, 1], F32)

        st2 = ExitStack()

        def sb2(name, shape, dtype):
            h = st2.enter_context(nc.sbuf_tensor(name, shape, dtype))
            return h[tuple(slice(None) for _ in shape)]

        kT = sb2("kT", [128, c.CH * c.T], BF16)
        vP = sb2("vP", [128, c.KT * c.VW], BF16)
        wsh = sb2("wsh", [128, c.DC * c.HG * 64 + (c.HG // 2) * c.D], BF16)
        wkv = wsh[:, 0:c.DC * c.HG * 128]
        wq = wsh[:, 0:c.DC * c.HG * 64]
        wo = wsh[:, c.DC * c.HG * 64:]
        xs = sb2("xs", [128, 4 * c.D], F32)
        lnx16 = sb2("lnx16", [128, 4 * c.D], BF16)
        lnxT = sb2("lnxT", [128, c.DC * c.TBLK], BF16)
        qT = sb2("qT", [128, 2 * c.CH * c.TBLK], BF16)
        oT = sb2("oT", [128, (c.HG // 2) * c.TBLK], BF16)
        pT = sb2("pT", [128, 4 * 1024], BF16)
        escr = sb2("escr", [128, 1024], F32)
        otmp = escr[0:64, :]
        recl = escr[64:65, :]
        vtmp16 = pT[:, 0:512]
        recl16 = sb2("recl16", [1, 520], FP16)

        psh = st.enter_context(nc.psum_tensor("ps", [128, 4096], F32))
        ps = psh[:, :]

        def bank(i, lo=0, width=512):
            return ps[:, 512 * i + lo: 512 * i + lo + width]

        semh = {s: st.enter_context(nc.semaphore(s)) for s in SEMS}
        block = st.enter_context(nc.Block())

        SC = c.scale
        BW = {i: [] for i in range(8)}

        rr = {"sync": 0, "gpsimd": 0}

        def dma(q, out, in_, waits=()):
            pre = "qs" if q == "sync" else "qg"
            sem = f"{pre}{rr[q] % 8}"
            rr[q] += 1
            chain = [(sem, P.cnt[sem])]
            v = P.do(q, lambda e, o=out, i=in_: e.dma_start(out=o, in_=i),
                     waits=list(waits) + chain, inc=sem, n=16)
            return (sem, v)

        m0 = P.do("gpsimd", lambda g: g.memset(id32, 0.0), inc="gp")

        P.do("gpsimd", lambda g: g.affine_select(
            out=id32, in_=id32, compare_op=AluOpType.not_equal, fill=1.0,
            base=0, pattern=[[-1, 128]], channel_multiplier=1),
            waits=[("gp", m0)], inc="gp")
        mib = P.do("gpsimd", lambda g: g.memset(id_bf, 0.0), inc="gp")
        P.do("gpsimd", lambda g: g.affine_select(
            out=id_bf, in_=id_bf, compare_op=AluOpType.not_equal, fill=1.0,
            base=0, pattern=[[-1, 128]], channel_multiplier=1),
            waits=[("gp", mib)], inc="gp")
        mih = P.do("gpsimd", lambda g: g.memset(id16, 0.0), inc="gp")
        P.do("gpsimd", lambda g: g.affine_select(
            out=id16, in_=id16, compare_op=AluOpType.not_equal, fill=1.0,
            base=0, pattern=[[-1, 128]], channel_multiplier=1),
            waits=[("gp", mih)], inc="gp")
        P.do("gpsimd", lambda g: g.memset(ones16, 1.0), inc="gp")
        P.do("gpsimd", lambda g: g.memset(ones_bf, 1.0), inc="gp")
        P.do("gpsimd", lambda g: g.memset(eps_t, 1e-5), inc="gp")
        P.do("gpsimd", lambda g: g.memset(mhalf, -0.5), inc="gp")
        P.do("gpsimd", lambda g: g.memset(
            apx(vP, 64, [[c.VW, c.KT], [65, c.HG]]), 1.0), inc="gp")
        m_init = P.do("gpsimd", lambda g: g.memset(oT, 0.0), inc="gp")

        m_bload = []
        for fc in range(c.FC):
            m_bload.append(dma("sync", b1s[:, fc:fc + 1],
                               bass.AP(B1.tensor, fc * 128, [[1, 128], [1, 1]])))
        for dc in range(c.DC):
            m_bload.append(dma("sync", b2s[:, dc:dc + 1],
                               bass.AP(B2.tensor, dc * 128, [[1, 128], [1, 1]])))
        m_bload = wall(m_bload)

        slot_pow = {}
        slot_norm = {}

        def ln_stats(src, slot, wait_src):
            o = slot * 20
            ts_ = min(512, c.D)
            nh = c.D // ts_
            for i in range(nh):
                P.do("vector",
                     lambda v, i=i, o=o, s=src, ts_=ts_: v.bn_stats(
                         stats[:, o + 6 * i:o + 6 * i + 6],
                         s[:, ts_ * i:ts_ * (i + 1)]),
                     waits=list(wait_src) + [("gp", slot_pow.get(slot, 0)),
                                             ("dv", slot_norm.get(slot, 0))]
                     if i == 0 else (),
                     inc="dv" if i == nh - 1 else None)
            m1 = P.cnt["dv"]
            m2 = P.do("vector", lambda v, o=o, n=nh: v.bn_aggr(
                stats[:, o + 12:o + 14], stats[:, o:o + 6 * n]),
                waits=[("dv", m1)], inc="dv")
            m3 = P.do("vector", lambda v, o=o: v.tensor_scalar(
                stats[:, o + 14:o + 15], stats[:, o + 13:o + 14], eps_t[:, 0:1],
                None, AluOpType.add, AluOpType.bypass),
                waits=[("dv", m2), ("gp", slot_pow.get(slot, 0))], inc="dv")
            m4 = P.do("gpsimd", lambda g, o=o: g.tensor_tensor(
                stats[:, o + 15:o + 16], stats[:, o + 14:o + 15], mhalf,
                AluOpType.pow),
                waits=[("dv", m3), ("gp", m_init),
                       ("dv", slot_norm.get(slot, 0))], inc="gp")
            slot_pow[slot] = m4
            return m4

        def ln_norm(src, dst, slot, m4, extra_waits=()):
            o = slot * 20
            m5 = P.do("vector", lambda v, o=o: v.scalar_tensor_tensor(
                stats[:, o + 16:o + 17], stats[:, o + 12:o + 13], -1.0,
                stats[:, o + 15:o + 16], AluOpType.mult, AluOpType.mult),
                waits=[("gp", m4)], inc="dv")
            m6 = P.do("vector", lambda v, o=o, s=src, d=dst: v.tensor_scalar(
                d, s, stats[:, o + 15:o + 16], stats[:, o + 16:o + 17],
                AluOpType.mult, AluOpType.add),
                waits=[("dv", m5)] + list(extra_waits), inc="dv")
            slot_norm[slot] = m6
            return m6

        def transp(src, n, b0, dst_of, wait_first, evac_eng="vector", nb=2,
                   ident=None):
            sem = "dv" if evac_eng == "vector" else "ac"
            idt = id32 if ident is None else ident
            step = 4
            last = None
            g0 = 0
            while g0 < n:
                g1 = min(g0 + step, n)
                bi = b0 + ((g0 // 4) % nb if step == 4 else 0)
                bw = BW[bi] if step == 4 else BW[b0] + BW[b0 + 1]
                for j in range(g0, g1):
                    w = []
                    if j == 0:
                        w += list(wait_first)
                    if j == g0:
                        w += bw
                    P.do("tensor", lambda t, j=j, bi=bi, g0=g0, s=src, idt=idt:
                         (t.transpose(
                             bank(bi + (j - g0) // 4, 128 * ((j - g0) % 4), 128),
                             s[:, 128 * j:128 * (j + 1)], idt)
                          if idt is id32 else
                          t.matmul(
                             bank(bi + (j - g0) // 4, 128 * ((j - g0) % 4), 128),
                             s[:, 128 * j:128 * (j + 1)], idt,
                             start=True, stop=True)),
                         waits=w, inc="pe" if j == g1 - 1 else None)
                mpe = P.cnt["pe"]
                cn = g1 - g0
                if evac_eng == "vector":
                    last = P.do("vector", lambda v, bi=bi, g0=g0, cn=cn: v.tensor_copy(
                        dst_of(g0, cn), bank(bi, 0, 128 * cn)),
                        waits=[("pe", mpe)], inc="dv")
                else:
                    last = P.do("scalar", lambda e, bi=bi, g0=g0, cn=cn: e.activation(
                        dst_of(g0, cn), bank(bi, 0, 128 * cn), AF.Copy),
                        waits=[("pe", mpe)], inc="ac")
                if step == 4:
                    BW[bi] = [(sem, last)]
                else:
                    BW[b0] = [(sem, last)]
                    BW[b0 + 1] = [(sem, last)]
                g0 = g1
            return (sem, last), P.cnt["pe"]

        st_blk = {"xw": [[] for _ in range(4)],
                  "l16w": [[] for _ in range(4)],
                  "lnxT_users": [],
                  "m_lnxT_sem": "dv", "evac_eng": "vector"}

        def ln1_part1(t0_base):
            nsub = c.TBLK // 128
            pows, loads = [], []
            for s in range(nsub):
                mx = dma("sync", xs[:, s * c.D:(s + 1) * c.D],
                         X[t0_base + s * 128: t0_base + s * 128 + 128, :],
                         waits=st_blk["xw"][s])
                loads.append(mx)
            for s in range(nsub):
                pows.append(ln_stats(xs[:, s * c.D:(s + 1) * c.D], s, [loads[s]]))
            norms = []
            for s in range(nsub):
                xsl = xs[:, s * c.D:(s + 1) * c.D]
                l16 = lnx16[:, s * c.D:(s + 1) * c.D]
                mn = ln_norm(xsl, l16, s, pows[s],
                             extra_waits=st_blk["l16w"][s])
                st_blk["xw"][s] = [("dv", mn)]
                norms.append(mn)
            return norms

        def ln1_part2(norms, b0=0):
            nsub = c.TBLK // 128
            mlast = None
            for s in range(nsub):
                l16 = lnx16[:, s * c.D:(s + 1) * c.D]
                def dst(j0, cn, s=s):
                    return apx(lnxT, j0 * c.TBLK + s * 128, [[c.TBLK, cn], [1, 128]])
                (semn, mev), mtp = transp(
                    l16, c.DC, b0, dst,
                    wait_first=[("dv", norms[s])]
                    + [("pe", u) for u in st_blk["lnxT_users"][-1:]],
                    evac_eng=st_blk["evac_eng"], ident=id_bf)
                st_blk["l16w"][s] = [("pe", mtp)]
                st_blk["m_lnxT_sem"] = semn
                mlast = mev
            return mlast

        war_wkv = {"v": []}
        war_kT = {"v": []}
        war_vP = {"v": []}
        m_kv_done = {}

        def phase_A(g):
            W2G = c.HG * 128
            m_w = []
            for dc in range(c.DC):
                m_w.append(dma("gpsimd", wkv[:, dc * W2G: dc * W2G + c.HG * 64],
                    Wqkv[dc * 128:(dc + 1) * 128,
                         c.D + g * c.HG * 64: c.D + (g + 1) * c.HG * 64],
                    waits=(war_wkv["v"] + [("gp", m_init)]) if dc == 0 else ()))
                m_w.append(dma("gpsimd", wkv[:, dc * W2G + c.HG * 64: (dc + 1) * W2G],
                    Wqkv[dc * 128:(dc + 1) * 128,
                         2 * c.D + g * c.HG * 64: 2 * c.D + (g + 1) * c.HG * 64]))
            m_w = wall(m_w)
            mk0 = P.do("vector", lambda v: v.memset(qmax4[:, 7:8], 0.0),
                       waits=war_kT["v"], inc="dv")
            kmax_run = mk0

            nmk = c.HG * 128 // 128
            nK = nmk // 2
            BW_vtmp[:] = [("dv", P.cnt["dv"]), ("pe", P.cnt["pe"])]
            st_blk["evac_eng"] = "scalar"
            norms = ln1_part1(0)
            for tb in range(c.NTB):
                m_lnxT = ln1_part2(norms)
                if tb + 1 < c.NTB:
                    norms = ln1_part1((tb + 1) * c.TBLK)
                for mc in range(nmk):
                    pb = 2 + (mc % 2)
                    is_v = mc >= nK
                    for dc in range(c.DC):
                        w = (BW[pb] + m_w
                             + [(st_blk["m_lnxT_sem"], m_lnxT)]) if dc == 0 else ()
                        P.do("tensor", lambda t, dc=dc, mc=mc, pb=pb: t.matmul(
                            bank(pb),
                            wkv[:, dc * W2G + mc * 128: dc * W2G + (mc + 1) * 128],
                            lnxT[:, dc * c.TBLK:(dc + 1) * c.TBLK],
                            start=(dc == 0), stop=(dc == c.DC - 1)),
                            waits=w, inc="pe" if dc == c.DC - 1 else None)
                    m_mm = P.cnt["pe"]
                    st_blk["lnxT_users"] = [m_mm]
                    if not is_v:
                        mea = P.do("scalar", lambda e, mc=mc, tb=tb, pb=pb: e.activation(
                            kT[:, mc * c.T + tb * c.TBLK: mc * c.T + (tb + 1) * c.TBLK],
                            bank(pb), AF.Copy),
                            waits=[("pe", m_mm)] + war_kT["v"], inc="ac")
                        mr = P.do("vector", lambda v, mc=mc, tb=tb: v.tensor_reduce(
                            qmax4[:, 6:7],
                            kT[:, mc * c.T + tb * c.TBLK: mc * c.T + (tb + 1) * c.TBLK],
                            axis=AX.X, op=AluOpType.max,
                            apply_absolute_value=True),
                            waits=[("ac", mea), ("dv", kmax_run)], inc="dv")
                        kmax_run = P.do("vector", lambda v: v.tensor_tensor(
                            qmax4[:, 7:8], qmax4[:, 7:8], qmax4[:, 6:7],
                            AluOpType.max), waits=[("dv", mr)], inc="dv")
                        BW[pb] = [("ac", mea)]
                    else:
                        mev = P.do("scalar", lambda e, pb=pb: e.activation(
                            vtmp16, bank(pb), AF.Copy),
                            waits=[("pe", m_mm), BW_vtmp[0]] if BW_vtmp else [("pe", m_mm)],
                            inc="ac")
                        BW[pb] = [("ac", mev)]
                        mcv = mc - nK
                        vb = 4 + (mcv % 2)
                        nsub = c.TBLK // 128
                        for j in range(nsub):
                            w = ([("ac", mev)] + BW[vb]) if j == 0 else ()
                            P.do("tensor", lambda t, j=j, vb=vb: t.matmul(
                                bank(vb, 128 * j, 128),
                                vtmp16[:, 128 * j:128 * (j + 1)], id_bf,
                                start=True, stop=True),
                                waits=w, inc="pe" if j == nsub - 1 else None)
                        m_vt = P.cnt["pe"]
                        BW_vtmp[:] = [("pe", m_vt)]
                        mlv = None
                        for h2 in range(2):
                            hl = 2 * mcv + h2
                            mlv = P.do("vector", lambda v, h2=h2, hl=hl, tb=tb,
                                       nsub=nsub, vb=vb:
                                       v.tensor_copy(
                                           apx(vP, tb * nsub * c.VW + hl * 65,
                                               [[c.VW, nsub], [1, 64]]),
                                           apx(ps, 512 * vb + h2 * 64,
                                               [[128, nsub], [1, 64]])),
                                       waits=([("pe", m_vt)] + war_vP["v"]) if h2 == 0 else (),
                                       inc="dv")
                        BW[vb] = [("dv", mlv)]
            war_wq["v"] = [("pe", P.cnt["pe"])]
            w5 = BW[5]
            P.do("tensor", lambda t: t.transpose(bank(5, 0, 128)[0:1, :],
                                                 qmax4[:, 7:8], id32),
                 waits=[("dv", kmax_run)] + w5, inc="pe")
            mt = P.cnt["pe"]
            mk = P.do("vector", lambda v, g=g: v.tensor_reduce(
                kmaxs[:, g:g + 1], bank(5)[0:1, 0:128], axis=AX.X,
                op=AluOpType.max), waits=[("pe", mt)], inc="dv")
            BW[5] = [("dv", mk)]
            m_kv_done[g] = mk

        BW_vtmp = []

        war_qT = {"v": []}
        war_oT = {"v": []}
        war_wq = {"v": []}
        war_otmp = {0: [], 1: []}
        war_recl = {0: [], 1: []}
        war_recl16 = {"v": []}
        tailstate = {"fn": None}
        prolstate = {"chunks": [], "m_qT": 0}
        bgq = []
        bgq_meta = {"ob0": -1}
        pvstate = {"m_pv_0": 0, "m_pv_1": 0}
        ringstate = {"exp": {}, "pv": {}, "i": 0, "pgc": 0}

        def phase_B(g):
            m_w = []
            for dc in range(c.DC):
                m_w.append(dma("gpsimd", wq[:, dc * c.HG * 64:(dc + 1) * c.HG * 64],
                    Wqkv[dc * 128:(dc + 1) * 128,
                         g * c.HG * 64:(g + 1) * c.HG * 64],
                    waits=war_wq["v"] if dc == 0 else ()))
            for hp in range(c.HG // 2):
                m_w.append(dma("gpsimd", wo[:, hp * c.D:(hp + 1) * c.D],
                    Wout[(g * c.HG + 2 * hp) * 64:
                         (g * c.HG + 2 * hp) * 64 + 128, :],
                    waits=([("gp", m_init)] + war_wq["v"]) if hp == 0 else ()))
            m_w = wall(m_w)
            m_pw = [("pe", P.cnt["pe"])]
            m_sq = P.do("scalar", lambda e: e.activation(
                lnxT[:, 0:c.DC * c.HG * 64], wq, AF.Square),
                waits=m_w + m_pw + [("ac", P.cnt["ac"])], inc="ac")
            NQ = c.HG * 64
            for dc in range(c.DC):
                w = ([("ac", m_sq)] + BW[0]) if dc == 0 else ()
                P.do("tensor", lambda t, dc=dc: t.matmul(
                    bank(0, 0, NQ)[0:1, :], ones_bf,
                    lnxT[:, dc * NQ:(dc + 1) * NQ],
                    start=(dc == 0), stop=(dc == c.DC - 1)),
                    waits=w, inc="pe" if dc == c.DC - 1 else None)
            m_cn = P.cnt["pe"]
            st_blk["lnxT_users"] = [m_cn]
            pvstate["pT_free"] = 0
            mr2 = P.do("vector", lambda v: v.tensor_reduce(
                sc1[:, 2:3], bank(0, 0, NQ)[0:1, :], axis=AX.X,
                op=AluOpType.max), waits=[("pe", m_cn)], inc="dv")
            BW[0] = [("dv", mr2)]
            ml1 = P.do("scalar", lambda e: e.activation(
                sc1[:, 3:4], sc1[:, 2:3], AF.Ln), waits=[("dv", mr2)], inc="ac")
            ml2 = P.do("scalar", lambda e: e.activation(
                sc1[:, 3:4], sc1[:, 3:4], AF.Exp, scale=0.5),
                waits=[("ac", ml1)], inc="ac")
            mr3 = P.do("vector", lambda v, g=g: v.scalar_tensor_tensor(
                sc1[:, 1:2], sc1[:, 3:4],
                -float(c.DH) * SC * math.sqrt(c.D), kmaxs[:, g:g + 1],
                AluOpType.mult, AluOpType.mult),
                waits=[("ac", ml2), ("dv", m_kv_done[g])], inc="dv")
            mr4 = P.do("vector", lambda v: v.tensor_copy(
                recl16[0:1, 512:513], sc1[:, 1:2]),
                waits=[("dv", mr3)] + war_recl16["v"], inc="dv")
            w1b = BW[1]
            P.do("tensor", lambda t: t.matmul(
                bank(1, 0, 1), ones16, recl16[0:1, 512:513],
                start=True, stop=True),
                waits=[("dv", mr4)] + w1b, inc="pe")
            mbc = P.cnt["pe"]
            m_negB = P.do("vector", lambda v: v.tensor_copy(
                negBv, bank(1, 0, 1)),
                waits=[("pe", mbc), ("ac", P.cnt["ac"])], inc="dv")
            BW[1] = BW[1] + [("dv", m_negB)]

            def emit_prologueB(qb, norms, g=None, pbanks=(2, 3)):
                m_lnxT = ln1_part2(norms, b0=pbanks[0])
                mq = 0
                qsl = (qb % 2) * c.CH * c.TBLK
                for mc in range(c.CH):
                    pb = pbanks[mc % 2]
                    for dc in range(c.DC):
                        w = (BW[pb] + m_w + [("dv", m_lnxT)]) if dc == 0 else ()
                        P.do("tensor", lambda t, dc=dc, mc=mc, pb=pb: t.matmul(
                            bank(pb),
                            wq[:, dc * c.HG * 64 + mc * 128: dc * c.HG * 64 + (mc + 1) * 128],
                            lnxT[:, dc * c.TBLK:(dc + 1) * c.TBLK],
                            start=(dc == 0), stop=(dc == c.DC - 1)),
                            waits=w, inc="pe" if dc == c.DC - 1 else None)
                    m_mm = P.cnt["pe"]
                    st_blk["lnxT_users"] = [m_mm]
                    if mc % 2 == 1 or mc == c.CH - 1:
                        nmc = 2 if mc % 2 == 1 else 1
                        mc0 = mc - nmc + 1
                        me = P.do("vector", lambda v, mc0=mc0, nmc=nmc, qsl=qsl:
                                  v.tensor_copy(
                                      qT[:, qsl + mc0 * c.TBLK:
                                         qsl + (mc0 + nmc) * c.TBLK],
                                      bank(pbanks[0] if mc0 % 2 == 0 else pbanks[1],
                                           0, nmc * 512)),
                                  waits=[("pe", m_mm)] + war_qT["v"], inc="dv")
                        mq = me
                        BW[pbanks[0]] = [("dv", me)]
                        BW[pbanks[1]] = [("dv", me)]
                return mq

            def prologueB_chunks(qb, norms, g=None, pbanks=(4, 5)):
                state = {"m_lnxT": 0}
                cl = []
                nsub = c.TBLK // 128
                qsl = (qb % 2) * c.CH * c.TBLK
                for s in range(nsub):
                    def tch(s=s):
                        l16 = lnx16[:, s * c.D:(s + 1) * c.D]
                        def dst(j0, cn, s=s):
                            return apx(lnxT, j0 * c.TBLK + s * 128,
                                       [[c.TBLK, cn], [1, 128]])
                        (semn, mev), mtp = transp(
                            l16, c.DC, pbanks[0], dst,
                            wait_first=[("dv", norms[s])]
                            + [("pe", u) for u in st_blk["lnxT_users"][-1:]],
                            evac_eng=st_blk["evac_eng"], ident=id_bf)
                        st_blk["l16w"][s] = [("pe", mtp)]
                        st_blk["m_lnxT_sem"] = semn
                        state["m_lnxT"] = mev
                    cl.append(tch)
                for mc in range(c.CH):
                    def qch(mc=mc):
                        pb = pbanks[mc % 2]
                        for dc in range(c.DC):
                            w = (BW[pb] + m_w
                                 + [(st_blk["m_lnxT_sem"],
                                     state["m_lnxT"])]) if dc == 0 else ()
                            P.do("tensor", lambda t, dc=dc, mc=mc, pb=pb:
                                 t.matmul(
                                     bank(pb),
                                     wq[:, dc * c.HG * 64 + mc * 128:
                                        dc * c.HG * 64 + (mc + 1) * 128],
                                     lnxT[:, dc * c.TBLK:(dc + 1) * c.TBLK],
                                     start=(dc == 0), stop=(dc == c.DC - 1)),
                                 waits=w, inc="pe" if dc == c.DC - 1 else None)
                        m_mm = P.cnt["pe"]
                        st_blk["lnxT_users"] = [m_mm]
                        if mc % 2 == 1 or mc == c.CH - 1:
                            nmc = 2 if mc % 2 == 1 else 1
                            mc0 = mc - nmc + 1
                            me = P.do("vector",
                                      lambda v, mc0=mc0, nmc=nmc, qsl=qsl:
                                      v.tensor_copy(
                                          qT[:, qsl + mc0 * c.TBLK:
                                             qsl + (mc0 + nmc) * c.TBLK],
                                          bank(pbanks[0] if mc0 % 2 == 0
                                               else pbanks[1], 0, nmc * 512)),
                                      waits=[("pe", m_mm)] + war_qT["v"],
                                      inc="dv")
                            prolstate["m_qT"] = me
                            BW[pbanks[0]] = [("dv", me)]
                            BW[pbanks[1]] = [("dv", me)]
                    cl.append(qch)
                return cl

            def emit_outproj(qb, g, ob0, spread=True):
                m_oT = P.cnt["dv"]
                nsub = c.TBLK // 128
                OW = min(512, c.D)
                ops = []
                for tc in range(nsub):
                    for oh in range(c.D // OW):
                        pb = ob0 + (tc * (c.D // OW) + oh) % 2
                        grp = []
                        for hl in range(c.HG // 2):
                            first = hl == 0
                            last = hl == c.HG // 2 - 1

                            def mm(tc=tc, oh=oh, hl=hl, pb=pb, first=first,
                                   last=last, qb=qb, g=g):
                                w = ([("dv", m_oT)] + m_w + BW[pb]) if first else ()
                                P.do("tensor", lambda t: t.matmul(
                                    bank(pb, 0, OW),
                                    oT[:, hl * c.TBLK + tc * 128:
                                       hl * c.TBLK + (tc + 1) * 128],
                                    wo[:, hl * c.D + oh * OW:
                                       hl * c.D + (oh + 1) * OW],
                                    start=first, stop=last),
                                    waits=w, inc="pe" if last else None)
                                if last:
                                    m_mm = P.cnt["pe"]
                                    war_oT["v"] = [("pe", m_mm)]
                                    dsty = y_acc[:, (qb * nsub + tc) * c.D + oh * OW:
                                                 (qb * nsub + tc) * c.D + (oh + 1) * OW]
                                    if g == 0:
                                        mev = P.do("vector",
                                                   lambda v, d=dsty, pb=pb: v.tensor_copy(
                                                       d, bank(pb, 0, OW)),
                                                   waits=[("pe", m_mm)], inc="dv")
                                    else:
                                        mev = P.do("vector",
                                                   lambda v, d=dsty, pb=pb: v.tensor_tensor(
                                                       d, d, bank(pb, 0, OW),
                                                       AluOpType.add),
                                                   waits=[("pe", m_mm),
                                                          ("ac", m_yg0[0])], inc="dv")
                                    BW[pb] = [("dv", mev)]
                            grp.append(mm)
                        ops.append(lambda grp=grp: [f() for f in grp])
                if spread:
                    bgq_meta["ob0"] = ob0
                    bgq.extend(ops)
                else:
                    for op in ops:
                        op()

            st_blk["evac_eng"] = "vector"
            for _h in range(2):            st_blk["evac_eng"] = "vector"
            for _h in range(2):            st_blk["evac_eng"] = "vector"
            for _h in range(2):            st_blk["evac_eng"] = "vector"
            for _h in range(2):
                war_otmp[_h] = war_otmp[_h] + [("ac", P.cnt["ac"])]
                war_recl[_h] = war_recl[_h] + [("ac", P.cnt["ac"])]
            normsB = ln1_part1(0)
            prol = emit_prologueB(0, normsB, g=g)
            for qb in range(c.NTB):
                m_qT = prol
                if qb + 1 < c.NTB:
                    normsB = ln1_part1((qb + 1) * c.TBLK)
                    if c.CH >= 3:
                        prolstate["chunks"] = prologueB_chunks(
                            qb + 1, normsB, g=g, pbanks=(4, 5))
                def epi_prep(p, obase, m_pv, half, st):
                    ob = obase + half
                    mo = P.do("vector", lambda v, ob=ob, half=half:
                              v.tensor_copy(
                                  otmp[:, half * 512:(half + 1) * 512],
                                  bank(ob)[0:64, :]),
                              waits=[("pe", m_pv)]
                              + war_otmp[half], inc="dv")
                    mrc = P.do("vector", lambda v, ob=ob, half=half:
                               v.reciprocal(
                                   recl[0:1, half * 512:(half + 1) * 512],
                                   bank(ob)[64:65, :]),
                               waits=[("pe", m_pv),
                                      ("dv", mo)] + war_recl[half], inc="dv")
                    st[half] = P.do("vector", lambda v, half=half: v.tensor_copy(
                        recl16[0:1, 0:512],
                        recl[0:1, half * 512:(half + 1) * 512]),
                        waits=[("dv", mrc)] + war_recl16["v"], inc="dv")

                def epi_fin(p, obase, half, st):
                    ob = obase + half
                    P.do("tensor", lambda t, ob=ob: t.matmul(
                        bank(ob), ones16,
                        recl16[0:1, 0:512],
                        start=True, stop=True),
                        waits=[("dv", st[half])], inc="pe")
                    mb = P.cnt["pe"]
                    war_recl16["v"] = [("pe", mb)]
                    mm_ = P.do("vector", lambda v, ob=ob, half=half, p=p:
                               v.tensor_tensor(
                                   oT[64 * half:64 * half + 64,
                                      p * c.TBLK:(p + 1) * c.TBLK],
                                   otmp[:, half * 512:(half + 1) * 512],
                                   bank(ob)[0:64, :], AluOpType.mult),
                               waits=[("pe", mb)] + war_oT["v"], inc="dv")
                    BW[ob] = [("dv", mm_)]
                    war_otmp[half] = [("dv", mm_)]
                    war_recl[half] = [("pe", mb)]

                def emit_epilogue(p, obase, m_pv, halves=(0, 1)):
                    st = {}
                    for half in halves:
                        epi_prep(p, obase, m_pv, half, st)
                        epi_fin(p, obase, half, st)

                pending_epi = []
                EH, PH = ringstate["exp"], ringstate["pv"]
                qsl = (qb % 2) * c.CH * c.TBLK
                for p in range(c.CH):
                    hlA, hlB = 2 * p, 2 * p + 1
                    obase = 6 if (ringstate["pgc"] % 2 == 0) else 4
                    ringstate["pgc"] += 1
                    base = ringstate["i"]
                    if obase == bgq_meta["ob0"]:
                        while bgq:
                            bgq.pop(0)()

                    def emit_qk(t_, p=p, base=base, qsl=qsl):
                        gi = base + t_
                        sp = gi % 2
                        for half in range(2):
                            rb = 2 * sp + half
                            r0 = 64 * half
                            w = []
                            if half == 0:
                                w = [("ac", EH.get(gi - 2, 0))]
                                if t_ == 0:
                                    w += [("dv", m_qT)] + BW[0] + BW[1]
                                elif t_ == 1:
                                    w += BW[2] + BW[3]
                            P.do("tensor",
                                 lambda t, t_=t_, p=p, r0=r0, rb=rb, qsl=qsl: t.matmul(
                                     bank(rb),
                                     kT[r0:r0 + 64,
                                        p * c.T + t_ * 128: p * c.T + (t_ + 1) * 128],
                                     qT[r0:r0 + 64,
                                        qsl + p * c.TBLK: qsl + (p + 1) * c.TBLK],
                                     start=True, stop=True),
                                 waits=w, inc="pe" if half == 1 else None)
                        EH[("qk", gi)] = P.cnt["pe"]

                    def emit_exp(t_, base=base):
                        gi = base + t_
                        sp = gi % 2
                        sl4 = gi % 4
                        EH[gi] = P.do(
                            "scalar", lambda e, sp=sp, sl4=sl4: e.activation(
                                pT[:, sl4 * 1024: (sl4 + 1) * 1024],
                                bank(2 * sp, 0, 1024),
                                AF.Exp, scale=SC, bias=negBv[:, 0:1]),
                            waits=[("pe", EH[("qk", gi)]),
                                   ("pe", PH.get(gi - 4, 0)),
                                   ("pe", pvstate.get("pT_free", 0)),
                                   ("dv", m_negB)], inc="ac")

                    def emit_pv(t_, p=p, obase=obase, base=base):
                        gi = base + t_
                        sl4 = gi % 4
                        hls = (2 * p, 2 * p + 1)
                        for half in range(2):
                            ob = obase + half
                            hl = hls[half]
                            w = []
                            if half == 0:
                                w = [("ac", EH[gi])]
                                if t_ == 0:
                                    w += BW[obase] + BW[obase + 1]
                            P.do("tensor",
                                 lambda t, t_=t_, hl=hl, sl4=sl4, ob=ob, half=half:
                                 t.matmul(
                                     bank(ob, 0, 512)[0:65, :],
                                     vP[:, t_ * c.VW + hl * 65:
                                        t_ * c.VW + hl * 65 + 65],
                                     pT[:, sl4 * 1024 + half * 512:
                                        sl4 * 1024 + (half + 1) * 512],
                                     start=(t_ == 0), stop=(t_ == c.KT - 1)),
                                 waits=w, inc="pe" if half == 1 else None)
                        PH[gi] = P.cnt["pe"]
                        if (bgq and obase != bgq_meta["ob0"]
                                and t_ % 4 == 0):
                            bgq.pop(0)()

                    for u in range(c.KT // 2):
                        emit_qk(2 * u)
                        emit_exp(2 * u)
                        emit_qk(2 * u + 1)
                        if u > 0:
                            emit_pv(2 * u - 2)
                        emit_exp(2 * u + 1)
                        if u > 0:
                            emit_pv(2 * u - 1)
                        if u >= min(2, c.KT // 2 - 1) and pending_epi:
                            pending_epi.pop(0)()
                        if (u == min(4, c.KT // 2 - 1) and p == 0
                                and tailstate["fn"] is not None):
                            tailstate["fn"]()
                            tailstate["fn"] = None
                        if (p == 2 and u >= 3 and prolstate["chunks"]
                                and not pending_epi):
                            prolstate["chunks"].pop(0)()
                    emit_pv(c.KT - 2)
                    emit_pv(c.KT - 1)
                    if p == 2:
                        while pending_epi:
                            pending_epi.pop(0)()
                        while prolstate["chunks"]:
                            prolstate["chunks"].pop(0)()
                    ringstate["i"] += c.KT
                    while pending_epi:
                        pending_epi.pop(0)()
                    epst = {}
                    pending_epi = []
                    for h in range(2):
                        pending_epi.append(
                            lambda p=p, obase=obase, m=P.cnt["pe"], h=h,
                            s=epst: epi_prep(p, obase, m, h, s))
                        pending_epi.append(
                            lambda p=p, obase=obase, h=h, s=epst:
                            epi_fin(p, obase, h, s))
                while bgq:
                    bgq.pop(0)()
                war_qT["v"] = [("pe", P.cnt["pe"])]
                mac_all = P.cnt["ac"]
                for b_ in range(4):
                    BW[b_] = [("ac", mac_all)]
                if qb + 1 < c.NTB:
                    if c.CH >= 3:
                        assert not prolstate["chunks"]
                        prol = prolstate["m_qT"]
                    else:
                        prol = emit_prologueB(qb + 1, normsB, g=g)
                    ob0_now = 6 if ((ringstate["pgc"] - 1) % 2 == 0) else 4
                    def mk_tail(qb=qb, g=g, ep=pending_epi, ob0=ob0_now):
                        def fn():
                            for f in ep:
                                f()
                            ep.clear()
                            emit_outproj(qb, g, ob0)
                        return fn
                    tailstate["fn"] = mk_tail()
                    pending_epi = []
                else:
                    assert tailstate["fn"] is None
                    while pending_epi:
                        pending_epi.pop(0)()
                    emit_outproj(qb, g,
                                 6 if ((ringstate["pgc"] - 1) % 2 == 0) else 4,
                                 spread=False)
            war_wkv["v"] = [("pe", P.cnt["pe"])]
            war_kT["v"] = [("pe", P.cnt["pe"])]
            war_vP["v"] = [("pe", P.cnt["pe"])]
            war_qT["v"] = [("pe", P.cnt["pe"])]

        def phase3_prologue(M_END, tz):
            s1T, s2T, wch, wch2, sgn = (tz["s1T"], tz["s2T"], tz["wch"],
                                        tz["wch2"], tz["sgn"])
            war_wch = {0: list(BW[0]) + list(BW[1]) + M_END, 1: list(M_END)}
            war_sgn = {0: list(M_END), 1: list(M_END)}
            sgn_chain = {"v": list(M_END)}
            war_red = []
            ms = {}
            for fc in range(c.FC):
                sl = fc % 2
                WC = wch[:, sl * c.D:(sl + 1) * c.D]
                SG = sgn[:, 0:c.D]
                mw = dma("sync", WC, W1[fc * 128:(fc + 1) * 128, :],
                         waits=war_wch[sl])
                mrd = P.do("vector", lambda v, WC=WC: v.tensor_reduce(
                    stats[:, 18:19], WC, axis=AX.X, op=AluOpType.add,
                    apply_absolute_value=True), waits=[mw] + war_red, inc="dv")
                mal = P.do("vector", lambda v, fc=fc: v.tensor_scalar(
                    alpha1[:, fc:fc + 1], stats[:, 18:19], 1.0 / c.D, None,
                    AluOpType.mult, AluOpType.bypass), waits=[("dv", mrd)], inc="dv")
                war_red = [("dv", mal)]
                msg = P.do("scalar", lambda e, WC=WC, SG=SG: e.activation(
                    SG, WC, AF.Sign),
                    waits=[mw] + sgn_chain["v"], inc="ac")
                war_wch[sl] = [("dv", mrd), ("ac", msg)]

                def dst(j0, cn, fc=fc):
                    return apx(s1T, j0 * c.F + fc * 128, [[c.F, cn], [1, 128]])
                (semn, mev), mtp = transp(SG, c.DC, 0 if sl == 0 else 2, dst,
                                          wait_first=[("ac", msg)], ident=id_bf)
                sgn_chain["v"] = [("pe", mtp)]
            ms["s1T"] = P.cnt["dv"]
            war_wch = {0: war_wch[0] + war_wch[1], 1: war_wch[0] + war_wch[1]}
            for dc in range(c.DC):
                sl = dc % 2
                WC = wch2[:, sl * c.F:(sl + 1) * c.F]
                SG = sgn[:, 0:c.F]
                mw = dma("sync", WC, W2[dc * 128:(dc + 1) * 128, :],
                         waits=war_wch[sl])
                mrd = P.do("vector", lambda v, WC=WC: v.tensor_reduce(
                    stats[:, 18:19], WC, axis=AX.X, op=AluOpType.add,
                    apply_absolute_value=True), waits=[mw] + war_red, inc="dv")
                mal = P.do("vector", lambda v, dc=dc: v.tensor_scalar(
                    alpha2[:, dc:dc + 1], stats[:, 18:19], 1.0 / c.F, None,
                    AluOpType.mult, AluOpType.bypass), waits=[("dv", mrd)], inc="dv")
                war_red = [("dv", mal)]
                msg = P.do("scalar", lambda e, WC=WC, SG=SG: e.activation(
                    SG, WC, AF.Sign),
                    waits=[mw] + sgn_chain["v"], inc="ac")
                war_wch[sl] = [("dv", mrd), ("ac", msg)]

                def dst(j0, cn, dc=dc):
                    return apx(s2T, j0 * c.D + dc * 128, [[c.D, cn], [1, 128]])
                (semn, mev), mtp = transp(SG, c.FC, 0 if sl == 0 else 2, dst,
                                          wait_first=[("ac", msg)], ident=id_bf)
                sgn_chain["v"] = [("pe", mtp)]
            ms["s2T"] = P.cnt["dv"]
            ms["alpha"] = P.cnt["dv"]
            return ms

        def phase3(M_END, ms, tz):
            s1T, s2T, hT = tz["s1T"], tz["s2T"], tz["hT"]
            x3, x1, ln3T = tz["x3"], tz["x1"], tz["ln3T"]
            ln3f, ln3b = tz["ln3f"], tz["ln3b"]
            o2tm, oscr = tz["o2tm"], tz["oscr"]
            out_dmas = phase3.out_dmas = []
            m_y_done = tz["m_y_done"]
            MP = tz["m_prol"]
            war_x3 = {0: list(M_END) + MP, 1: list(M_END) + MP}
            war_x1 = {0: MP, 1: MP}
            war_ln3f = {k: MP for k in range(2 * (c.TB3 // 128))}
            war_ln3b = {k: MP for k in range(2 * (c.TB3 // 128))}
            war_ln3T = {0: MP, 1: MP}
            war_hT = {"v": []}
            war_o2 = []
            nsub = c.TB3 // 128
            DCT = c.DC * c.TB3

            def front(tb):
                t0 = tb * c.TB3
                sl = tb % 2
                X3 = lambda s: x3[:, (sl * nsub + s) * c.D:(sl * nsub + s + 1) * c.D]
                X1 = lambda s: x1[:, (sl * nsub + s) * c.D:(sl * nsub + s + 1) * c.D]
                L3s = lambda s: ln3f[:, (sl * nsub + s) * c.D:(sl * nsub + s + 1) * c.D]
                mx = []
                for s in range(nsub):
                    mx.append(dma("sync", X3(s),
                                  X[t0 + s * 128: t0 + 128 * (s + 1), :],
                                  waits=war_x3[sl] if s == 0 else ()))
                pows = []
                for s in range(nsub):
                    ysl = y_acc[:, (t0 // 128 + s) * c.D:(t0 // 128 + s + 1) * c.D]
                    pows.append(ln_stats(ysl, sl * 2 + s,
                                         m_y_done if tb == 0 else []))
                m_x1 = []
                for s in range(nsub):
                    ysl = y_acc[:, (t0 // 128 + s) * c.D:(t0 // 128 + s + 1) * c.D]
                    mn = ln_norm(ysl, L3s(s), sl * 2 + s, pows[s],
                                 extra_waits=war_ln3f[sl * nsub + s])
                    ma = P.do("vector", lambda v, s=s, X3=X3, X1=X1, L3s=L3s:
                              v.tensor_tensor(X1(s), X3(s), L3s(s), AluOpType.add),
                              waits=[("dv", mn), mx[s]]
                              + (war_x1[sl] if s == 0 else []), inc="dv")
                    war_ln3f[sl * nsub + s] = [("dv", ma)]
                    m_x1.append(ma)
                pows = []
                for s in range(nsub):
                    pows.append(ln_stats(X1(s), sl * 2 + s, [("dv", m_x1[s])]))
                norms = []
                for s in range(nsub):
                    L3 = ln3b[:, (sl * nsub + s) * c.D:(sl * nsub + s + 1) * c.D]
                    mn = ln_norm(X1(s), L3, sl * 2 + s, pows[s],
                                 extra_waits=war_ln3b[sl * nsub + s])
                    norms.append(mn)
                return norms

            def emit_ln3T(tb, norms):
                sl = tb % 2
                m_ln3 = None
                for s_ in range(nsub):
                    L3 = ln3b[:, (sl * nsub + s_) * c.D:(sl * nsub + s_ + 1) * c.D]

                    def dst(j0, cn, s_=s_, sl=sl):
                        return apx(ln3T, sl * DCT + j0 * c.TB3 + s_ * 128,
                                   [[c.TB3, cn], [1, 128]])
                    (semn, mev), mtp = transp(L3, c.DC, 0, dst,
                                              wait_first=[("dv", norms[s_])]
                                              + war_ln3T[sl], ident=id_bf)
                    war_ln3b[sl * nsub + s_] = [("pe", mtp)]
                    m_ln3 = mev
                return m_ln3

            f_norms = front(0)
            pend_t = {"norms": f_norms, "tb": 0}
            for tb in range(c.NT3):
                t0 = tb * c.TB3
                sl = tb % 2
                X3 = lambda s, sl=sl: x3[:, (sl * nsub + s) * c.D:
                                         (sl * nsub + s + 1) * c.D]
                X1 = lambda s, sl=sl: x1[:, (sl * nsub + s) * c.D:
                                         (sl * nsub + s + 1) * c.D]
                L3s = lambda s, sl=sl: ln3f[:, (sl * nsub + s) * c.D:
                                            (sl * nsub + s + 1) * c.D]
                if pend_t.get("m_ln3") is None:
                    m_ln3 = emit_ln3T(tb, pend_t["norms"])
                else:
                    m_ln3 = pend_t["m_ln3"]
                for fc in range(c.FC):
                    pb = 2 + (fc % 2)
                    for dc in range(c.DC):
                        w = (BW[pb] + [("dv", m_ln3),
                                       ("dv", ms["s1T"])]) if dc == 0 else ()
                        P.do("tensor", lambda t, fc=fc, dc=dc, pb=pb, sl=sl:
                             t.matmul(
                                 bank(pb, 0, c.TB3),
                                 s1T[:, dc * c.F + fc * 128: dc * c.F + (fc + 1) * 128],
                                 ln3T[:, sl * DCT + dc * c.TB3:
                                      sl * DCT + (dc + 1) * c.TB3],
                                 start=(dc == 0), stop=(dc == c.DC - 1)),
                             waits=w, inc="pe" if dc == c.DC - 1 else None)
                    m_mm = P.cnt["pe"]
                    war_ln3T[sl] = [("pe", m_mm)]
                    mg = P.do("scalar", lambda e, fc=fc, pb=pb: e.activation(
                        hT[:, fc * c.TB3:(fc + 1) * c.TB3], bank(pb, 0, c.TB3),
                        AF.Gelu, scale=alpha1[:, fc:fc + 1], bias=b1s[:, fc:fc + 1]),
                        waits=[("pe", m_mm), ("dv", ms["alpha"])] + m_bload
                        + war_hT["v"], inc="ac")
                    BW[pb] = [("ac", mg)]
                m_h = P.cnt["ac"]
                if tb + 1 < c.NT3:
                    nxt = front(tb + 1)
                    pend_t = {"norms": nxt, "tb": tb + 1, "m_ln3": None}
                for dc in range(c.DC):
                    pb = 4 + (dc % 2)
                    for fc in range(c.FC):
                        w = (BW[pb] + [("ac", m_h), ("dv", ms["s2T"])]) if fc == 0 else ()
                        P.do("tensor", lambda t, fc=fc, dc=dc, pb=pb: t.matmul(
                            bank(pb, 0, c.TB3),
                            s2T[:, fc * c.D + dc * 128: fc * c.D + (dc + 1) * 128],
                            hT[:, fc * c.TB3:(fc + 1) * c.TB3],
                            start=(fc == 0), stop=(fc == c.FC - 1)),
                            waits=w, inc="pe" if fc == c.FC - 1 else None)
                    m_mm = P.cnt["pe"]
                    war_hT["v"] = [("pe", m_mm)]
                    mo = P.do("scalar", lambda e, dc=dc, pb=pb: e.activation(
                        oscr, bank(pb, 0, c.TB3), AF.Identity,
                        scale=alpha2[:, dc:dc + 1], bias=b2s[:, dc:dc + 1]),
                        waits=[("pe", m_mm), ("dv", ms["alpha"])] + m_bload
                        + war_o2, inc="ac")
                    BW[pb] = [("ac", mo)]

                    def dst(j0, cn, dc=dc):
                        return apx(o2tm, j0 * c.D + dc * 128, [[c.D, cn], [1, 128]])
                    (semn, mev), mtp = transp(oscr, nsub, 6, dst,
                                              wait_first=[("ac", mo)], ident=id16)
                    war_o2 = [("pe", mtp)]
                    m_o2 = mev
                    if (dc == min(3, c.DC - 1) and tb + 1 < c.NT3
                            and pend_t["m_ln3"] is None):
                        pend_t["m_ln3"] = emit_ln3T(tb + 1, pend_t["norms"])
                pows = []
                for s in range(nsub):
                    pows.append(ln_stats(o2tm[:, s * c.D:(s + 1) * c.D],
                                         sl * 2 + s, [("dv", m_o2)]))
                for s in range(nsub):
                    mn = ln_norm(o2tm[:, s * c.D:(s + 1) * c.D], L3s(s),
                                 sl * 2 + s, pows[s],
                                 extra_waits=war_ln3f[sl * nsub + s])
                    ma = P.do("vector", lambda v, s=s, X3=X3, X1=X1, L3s=L3s:
                              v.tensor_tensor(X3(s), X1(s), L3s(s), AluOpType.add),
                              waits=[("dv", mn)], inc="dv")
                    war_ln3f[sl * nsub + s] = [("dv", ma)]
                    mo_ = dma("sync", OUT[t0 + s * 128: t0 + (s + 1) * 128, :],
                              X3(s), waits=[("dv", ma)])
                    out_dmas.append(mo_)
                war_x3[sl] = wall(out_dmas[-nsub:])
                war_x1[sl] = [("dv", P.cnt["dv"])]

        m_yg0 = [0]
        phase_A(0)
        phase_B(0)
        m_yg0[0] = P.cnt["ac"]
        phase_A(1)
        phase_B(1)
        M_END = [("pe", P.cnt["pe"]), ("dv", P.cnt["dv"]), ("ac", P.cnt["ac"]),
                 ("gp", P.cnt["gp"])]
        m_y_done = [("dv", P.cnt["dv"]), ("ac", P.cnt["ac"])]
        st2.close()
        s1T = sb("s1T", [128, c.DC * c.F], FP8)
        s2T = sb("s2T", [128, c.FC * c.D], FP8)
        hT = sb("hT", [128, c.FC * c.TB3], BF16)
        o2tm = sb("o2tm", [128, (c.TB3 // 128) * c.D], FP16)
        oscr = sb("oscr", [128, c.TB3], FP16)
        st3 = ExitStack()
        wch = st3.enter_context(nc.sbuf_tensor("wch", [128, 2 * c.D], F32))[:, :]
        wch2 = st3.enter_context(nc.sbuf_tensor("wch2", [128, 2 * c.F], F32))[:, :]
        sgn = st3.enter_context(
            nc.sbuf_tensor("sgn", [128, max(c.D, c.F)], BF16))[:, :]
        p3tensors = dict(s1T=s1T, s2T=s2T, hT=hT, o2tm=o2tm, oscr=oscr,
                         wch=wch, wch2=wch2, sgn=sgn)
        ms = phase3_prologue(M_END, p3tensors)
        m_prol = [("pe", P.cnt["pe"]), ("dv", P.cnt["dv"]), ("ac", P.cnt["ac"])]
        st3.close()
        nsub3 = c.TB3 // 128
        p3tensors["x3"] = sb("x3", [128, 2 * nsub3 * c.D], F32)
        p3tensors["x1"] = p3tensors["x3"]
        p3tensors["ln3f"] = sb("ln3f", [128, 2 * (c.TB3 // 128) * c.D], F32)
        p3tensors["ln3b"] = sb("ln3b", [128, 2 * (c.TB3 // 128) * c.D], BF16)
        p3tensors["lnscr"] = None
        p3tensors["ln3T"] = sb("ln3T", [128, 2 * c.DC * c.TB3], BF16)
        p3tensors["m_y_done"] = m_y_done
        p3tensors["m_prol"] = m_prol
        phase3(M_END, ms, p3tensors)

        P.do("sync", lambda e: e.nop(), waits=wall(phase3.out_dmas))

        P.emit(block, semh, first_waits=[("gp", m_init)])
    return nc


def kernel(L0_attn_x, L0_attn_w_qkv, L0_attn_w_out, L0_ffn_w1, L0_ffn_b1,
           L0_ffn_w2, L0_ffn_b2):
    x = np.ascontiguousarray(np.asarray(L0_attn_x, dtype=np.float32))
    wqkv = np.ascontiguousarray(np.asarray(L0_attn_w_qkv, dtype=np.float32))
    wout = np.ascontiguousarray(np.asarray(L0_attn_w_out, dtype=np.float32))
    w1 = np.ascontiguousarray(np.asarray(L0_ffn_w1, dtype=np.float32))
    b1 = np.ascontiguousarray(np.asarray(L0_ffn_b1, dtype=np.float32))
    w2 = np.ascontiguousarray(np.asarray(L0_ffn_w2, dtype=np.float32))
    b2 = np.ascontiguousarray(np.asarray(L0_ffn_b2, dtype=np.float32))

    cfg = Cfg(T=4096, D=1024, H=16, F=4096)
    nc = build_nc(cfg)
    in_maps = []
    for b in range(8):
        in_maps.append({"x": x[b], "w_qkv": wqkv, "w_out": wout,
                        "w1": w1, "b1": b1, "w2": w2, "b2": b2})
    res = run_bass_kernel_spmd(nc, in_maps, core_ids=list(range(8)))
    out = np.stack([res.results[b]["out"] for b in range(8)], axis=0)
    return out.astype(np.float32)
